# version 02
# baseline (speedup 1.0000x reference)
import numpy as np
import ml_dtypes
from contextlib import ExitStack
import concourse.bass as bass
import concourse.mybir as mybir
from concourse.bass_utils import run_bass_kernel_spmd

F32 = mybir.dt.float32
BF16 = mybir.dt.bfloat16
AF = mybir.ActivationFunctionType
ALU = mybir.AluOpType
AX = mybir.AxisListType

B_, S_, H = 4, 4096, 4096
NKV, D = 8, 128
TL, TF = 2048, 4096
NCH = H // 128
SCALE = float(1.0 / np.sqrt(128.0))
EPS = 1e-5

ENGS = ("sync", "tensor", "vector", "scalar", "gpsimd")
B_LIMIT = None
B_STAGE = 6

AB_CAP = 59392
ABO = dict(lnT=0, wsbA=32768, lnb=40960, kst=45056, vtm=45568, vst=46080,
           wsbB=32768, khb=36864, vhb=40960, qs0=45056, qs1=47104,
           pbuf=49152, ptb=53248, atst=57344,
           wosb=0, atb=32768,
           lcC=40960, lcTt=36864,
           lch=0, wTsb=32768, dstg=49152)
AF_CAP = 17408
AFO = dict(stA=0, xbuf=8192, sqb=12288,
           stB=0,
           stC=0, yev=16384,
           yfull=0, x0t=4096, x1t=8192, sq2=12288,
           stD=0, biasb=8192)


def build_launch1(dbg=(), phases="ABCD"):
    nc = bass.Bass("TRN2", target_bir_lowering=False, debug=False)

    def inp(name, shape, dt=F32):
        return nc.dram_tensor(name, shape, dt, kind="ExternalInput").ap()

    def outp(name, shape, dt=F32):
        return nc.dram_tensor(name, shape, dt, kind="ExternalOutput").ap()

    def scr(name, shape, dt):
        kind = "ExternalOutput" if name in dbg else "Internal"
        return nc.dram_tensor(name, shape, dt, kind=kind).ap()

    x0 = inp("x0", [TF, H])
    wq_t = inp("wq_t", [32, 128, NCH, 128])
    wkv_t = inp("wkv_t", [8, 128, NCH, 256])
    wo_t = inp("wo_t", [8, 128, NCH, 512])
    wT_t = inp("wT_t", [16, 128, NCH, 256])
    bias_i = inp("bias_i", [128, H])
    centT = inp("centT", [H, 64])
    tri1 = inp("tri1", [128, 128]); tri2 = inp("tri2", [128, 128])
    identb = inp("identb", [128, 128], BF16)
    identf = inp("identf", [128, 128], F32)

    kT = scr("kT", [NKV, 128, TF], BF16)
    vT = scr("vT", [NKV, 128, 32, 128], BF16)
    lnTs = scr("lnTs", [2, 128, NCH * 1024], BF16)
    at3 = scr("at3", [16, 128, NCH, 128], BF16)
    y_s = scr("y_s", [16, 128, H], F32)
    lcT4 = scr("lcT4", [16, 128, NCH, 128], BF16)

    x1o = outp("x1", [TL, H], F32)
    denseo = outp("dense", [TL, H], BF16)
    qselo = outp("qsel", [128, 16], F32)
    cmaxo = outp("cmax", [64, 1], F32)

    q = {e: [] for e in ENGS}
    pbuf_guard = [0]
    at_store = [0]
    yev_store = [0]
    ctx = ExitStack()
    with ctx:
        semh, semc = {}, {}
        for n in ["ld_x", "ld_w", "ld_k", "ld_v", "ld_l", "ld_a", "ld_y", "ld_m",
                  "st_kv", "st_ln", "st_at", "st_y", "st_x1", "st_lct", "st_d", "st_o",
                  "pe", "ve", "ac", "gp", "bar", "arn0", "arn1", "arn2", "arn3", "trc", "ptc"]:
            semh[n] = ctx.enter_context(nc.semaphore(n)); semc[n] = 0

        AB = ctx.enter_context(nc.sbuf_tensor("AB", [128, AB_CAP], BF16))
        AFm = ctx.enter_context(nc.sbuf_tensor("AFm", [128, AF_CAP], F32))
        tri1b = ctx.enter_context(nc.sbuf_tensor("tri1b", [128, 128], F32))
        tri2b = ctx.enter_context(nc.sbuf_tensor("tri2b", [128, 128], F32))
        idb = ctx.enter_context(nc.sbuf_tensor("idb", [128, 128], BF16))
        idf = ctx.enter_context(nc.sbuf_tensor("idf", [128, 128], F32))
        stat = ctx.enter_context(nc.sbuf_tensor("stat", [128, 16], F32))
        mxb = ctx.enter_context(nc.sbuf_tensor("mxb", [128, 8], F32))
        sab = ctx.enter_context(nc.sbuf_tensor("sab", [128, 8], F32))
        fcb = ctx.enter_context(nc.sbuf_tensor("fcb", [128, 8], F32))
        qsel_sb = ctx.enter_context(nc.sbuf_tensor("qsel_sb", [128, 16], F32))
        cmax_sb = ctx.enter_context(nc.sbuf_tensor("cmax_sb", [64, 4], F32))
        centb = ctx.enter_context(nc.sbuf_tensor("centb", [128, NCH * 64], BF16))
        dsm = ctx.enter_context(nc.sbuf_tensor("dsm", [128, 80], F32))
        epsb = ctx.enter_context(nc.sbuf_tensor("epsb", [128, 1], F32))
        lgb = ctx.enter_context(nc.sbuf_tensor("lgb", [128, 128], F32))

        arena = ctx.enter_context(nc.psum_tensor("arena", [128, 2048], F32))
        trB = ctx.enter_context(nc.psum_tensor("trBp", [128, 1024], BF16))
        avP = ctx.enter_context(nc.psum_tensor("avPp", [128, 512], F32))
        qpP = ctx.enter_context(nc.psum_tensor("qpPp", [128, 1024], F32))

        blk = ctx.enter_context(nc.Block())

        def ab(name, n):
            return AB[:, ABO[name]:ABO[name] + n]

        def af(name, n):
            return AFm[:, AFO[name]:AFO[name] + n]

        def dma(e, out, in_, s="ld", **kw):
            semc[s] += 16
            h, tgt = semh[s], semc[s]
            def f(eng, out=out, in_=in_, h=h, kw=kw):
                eng.dma_start(out=out, in_=in_, **kw).then_inc(h, 16)
            q[e].append(f)
            return tgt

        def wait(e, s, v=None):
            if v is None:
                v = semc[s]
            if v <= 0:
                return
            def f(eng, h=semh[s], v=v):
                eng.wait_ge(h, v)
            q[e].append(f)

        def wait_all_dma(e):
            for s2 in ("ld_x", "ld_w", "ld_k", "ld_v", "ld_l", "ld_a", "ld_y", "ld_m",
                       "st_kv", "st_ln", "st_at", "st_y", "st_x1", "st_lct", "st_d", "st_o"):
                wait(e, s2)

        OWN = {"vector": "ve", "scalar": "ac", "gpsimd": "gp"}

        def op(e, fn, s=None):
            if e in ("vector", "scalar"):
                own = OWN[e]
                wait(e, own, semc[own])
                if s is None:
                    s = own
            if s is None:
                def f0(eng, fn=fn):
                    fn(eng)
                q[e].append(f0)
                return None
            semc[s] += 1
            h, tgt = semh[s], semc[s]
            def f(eng, fn=fn, h=h):
                fn(eng).then_inc(h, 1)
            q[e].append(f)
            return tgt

        def mm(out, lhsT, rhs, first, last, s="pe"):
            if last:
                semc[s] += 1
                def f(eng, out=out, lhsT=lhsT, rhs=rhs, first=first, h=semh[s]):
                    eng.matmul(out=out, lhsT=lhsT, rhs=rhs, start=first,
                               stop=True).then_inc(h, 1)
            else:
                def f(eng, out=out, lhsT=lhsT, rhs=rhs, first=first):
                    eng.matmul(out=out, lhsT=lhsT, rhs=rhs, start=first, stop=False)
            q["tensor"].append(f)
            return semc[s]

        def tposes(srcs, dsts_in_trB, last_inc=True):
            n = len(srcs)
            for i, src in enumerate(srcs):
                last = (i == n - 1) and last_inc
                if last:
                    semc["pe"] += 1
                    def f(eng, src=src, i=dsts_in_trB[i], h=semh["pe"]):
                        eng.transpose(out=trB[:, i * 128:(i + 1) * 128], in_=src,
                                      identity=idb[:]).then_inc(h, 1)
                else:
                    def f(eng, src=src, i=dsts_in_trB[i]):
                        eng.transpose(out=trB[:, i * 128:(i + 1) * 128], in_=src,
                                      identity=idb[:])
                q["tensor"].append(f)
            return semc["pe"]

        def barrier():
            wait_all_dma("sync")
            for e in ENGS:
                semc["bar"] += 1
                def f(eng, h=semh["bar"]):
                    eng.nop().then_inc(h, 1)
                q[e].append(f)
            v = semc["bar"]
            for e in ENGS:
                wait(e, "bar", v)

        def ln_stats(src):
            sq = CUR_SQ[0]
            op("vector", lambda e, src=src: e.reduce_sum(out=stat[:, 0:1], in_=src, axis=AX.X))
            op("vector", lambda e, src=src, sq=sq: e.scalar_tensor_tensor(
                out=sq, in0=src, scalar=1.0, in1=src, op0=ALU.mult, op1=ALU.mult,
                accum_out=stat[:, 1:2]))
            op("vector", lambda e: e.tensor_scalar_mul(out=stat[:, 2:3], in0=stat[:, 0:1],
                                                       scalar1=1.0 / H))
            op("vector", lambda e: e.tensor_copy(out=stat[:, 12:13], in_=stat[:, 2:3]))
            op("vector", lambda e: e.tensor_scalar_mul(out=stat[:, 3:4], in0=stat[:, 2:3],
                                                       scalar1=stat[:, 12:13]))
            op("vector", lambda e: e.tensor_scalar_mul(out=stat[:, 13:14], in0=stat[:, 1:2],
                                                       scalar1=1.0 / H))
            tv = op("vector", lambda e: e.tensor_single_scalar(
                out=stat[:, 4:5], in_=stat[:, 13:14], scalar=stat[:, 3:4], op=ALU.subtract))
            wait("scalar", "ve", tv)
            ts = op("scalar", lambda e: e.activation(out=stat[:, 5:6], in_=stat[:, 4:5],
                                                     func=AF.Sqrt, bias=epsb[:]))
            wait("vector", "ac", ts)
            op("vector", lambda e: e.reciprocal(out=stat[:, 6:7], in_=stat[:, 5:6]))
            op("vector", lambda e: e.tensor_scalar_mul(out=stat[:, 14:15], in0=stat[:, 2:3],
                                                       scalar1=-1.0))
            tv = op("vector", lambda e: e.tensor_scalar_mul(out=stat[:, 7:8], in0=stat[:, 14:15],
                                                            scalar1=stat[:, 6:7]))
            wait("scalar", "ve", tv)

        def ln_apply(src, dst):
            return op("scalar", lambda e, src=src, dst=dst: e.activation(
                out=dst, in_=src, func=AF.Identity, scale=stat[:, 6:7],
                bias=stat[:, 7:8]))

        CUR_SQ = [af("sqb", 4096)]

        dma("sync", tri1b[:], tri1, s="ld_m")
        dma("sync", tri2b[:], tri2, s="ld_m")
        dma("sync", idb[:], identb, s="ld_m")
        t0 = dma("sync", idf[:], identf, s="ld_m")
        t_cent = dma("sync", af("stA", 2048).rearrange("p (a c) -> p a c", c=64),
                     centT.rearrange("(a p) c -> p a c", p=128), s="ld_m")
        wait("gpsimd", "ld_m")
        tg0 = op("gpsimd", lambda e: e.tensor_copy(out=centb[:], in_=af("stA", 2048)), "gp")
        op("vector", lambda e: e.memset(cmax_sb[:, 0:1], -1e30), "ve")
        op("vector", lambda e: e.memset(epsb[:], EPS), "ve")
        op("vector", lambda e: e.memset(qsel_sb[:], 0.0), "ve")
        wait("sync", "gp", tg0)

        lnT3 = ab("lnT", 32768).rearrange("p (c t) -> p c t", t=1024)
        trB3 = trB[:].rearrange("p (k q) -> p k q", q=128)

        if "A" in phases:
            CUR_SQ[0] = af("sqb", 4096)
            last_k_store = 0
            last_v_store = 0
            for sset in range(2):
                row0 = TL if sset == 0 else 0
                tok0 = TL if sset == 0 else 0
                for half in range(2):
                    wait("vector", "st_ln")
                    for tloc in range(8):
                        r0 = row0 + half * 1024 + tloc * 128
                        wait("sync", "ve", semc["ve"])
                        wait("sync", "ac", semc["ac"])
                        dma("sync", af("xbuf", 4096), x0[r0:r0 + 128, :], s="ld_x")
                        wait("vector", "ld_x")
                        ln_stats(af("xbuf", 4096))
                        wait("scalar", "pe", semc["pe"])
                        ta = ln_apply(af("xbuf", 4096), ab("lnb", 4096))
                        wait("tensor", "ac", ta)
                        for bch in range(4):
                            wait("tensor", "trc", semc["trc"])
                            tp = tposes([ab("lnb", 4096)[:, (bch * 8 + k) * 128:(bch * 8 + k + 1) * 128]
                                         for k in range(8)], list(range(8)))
                            wait("vector", "pe", tp)
                            dst = lnT3[:, bch * 8:(bch + 1) * 8, tloc * 128:(tloc + 1) * 128]
                            op("vector", lambda e, dst=dst: e.tensor_copy(out=dst, in_=trB3[:, 0:8, :]), "trc")
                    if sset == 1:
                        wait("sync", "trc", semc["trc"])
                        dma("sync", lnTs[half], ab("lnT", 32768), s="st_ln")
                    for sl in range(8):
                        wait("sync", "gp", semc["gp"])
                        dma("sync", af("stA", 8192), wkv_t[sl].rearrange("p a n -> p (a n)"), s="ld_w")
                        wait("gpsimd", "ld_w")
                        wait("gpsimd", "pe", semc["pe"])
                        tg = op("gpsimd", lambda e: e.tensor_copy(out=ab("wsbA", 8192),
                                                                  in_=af("stA", 8192)), "gp")
                        wsA = ab("wsbA", 8192).rearrange("p (a n) -> p a n", n=256)
                        wait("tensor", "gp", tg)
                        wait("tensor", "trc", semc["trc"])
                        for grp in range(2):
                            for dc in range(2):
                                pk = qpP[:, dc * 512:(dc + 1) * 512]
                                wait("tensor", "ve", semc["ve"])
                                for ch in range(NCH):
                                    tpk = mm(pk, wsA[:, ch, dc * 128:(dc + 1) * 128],
                                             lnT3[:, ch, grp * 512:(grp + 1) * 512],
                                             ch == 0, ch == NCH - 1)
                                wait("vector", "pe", tpk)
                                hh = (sl % 4) * 2 + dc
                                c0 = tok0 + half * 1024 + grp * 512
                                if sl < 4:
                                    wait("vector", "st_kv")
                                    tv = op("vector", lambda e, pk=pk: e.tensor_copy(
                                        out=ab("kst", 512), in_=pk), "ve")
                                    wait("sync", "ve", tv)
                                    dma("sync", kT[hh][:, c0:c0 + 512],
                                        ab("kst", 512), s="st_kv")
                                else:
                                    wait("vector", "st_kv")
                                    tv = op("vector", lambda e, pk=pk: e.tensor_copy(
                                        out=ab("vtm", 512), in_=pk), "ve")
                                    wait("tensor", "ve", tv)
                                    wait("tensor", "trc", semc["trc"])
                                    tp = tposes([ab("vtm", 512)[:, j * 128:(j + 1) * 128]
                                                 for j in range(4)], list(range(4)))
                                    wait("vector", "pe", tp)
                                    tv = op("vector", lambda e: e.tensor_copy(
                                        out=ab("vst", 512), in_=trB[:, 0:512]), "trc")
                                    wait("sync", "trc", tv)
                                    kt0 = c0 // 128
                                    dst = vT[hh, :, kt0:kt0 + 4, :].rearrange("p j c -> p (j c)")
                                    dma("sync", dst, ab("vst", 512), s="st_kv")
            barrier()

        if "B" in phases:
            LB = B_LIMIT or (2, 8, 4, 8)
            arn_cnt = {0: 0, 1: 0, 2: 0, 3: 0}
            qb_guard = [0, 0]
            for half in (LB[0] if isinstance(LB[0], (list, tuple)) else range(LB[0])):
                wait("sync", "pe", semc["pe"])
                dma("sync", ab("lnT", 32768), lnTs[half], s="ld_l")
                for h in range(LB[1]):
                    wait("sync", "pe", semc["pe"])
                    dma("sync", ab("khb", 4096), kT[h], s="ld_k")
                    dma("sync", ab("vhb", 4096), vT[h].rearrange("p k c -> p (k c)"), s="ld_v")
                    for g in range(LB[2]):
                        qh = h * 4 + g
                        qb = ab("qs0", 1024) if qh % 2 == 0 else ab("qs1", 1024)
                        wait("sync", "gp", semc["gp"])
                        dma("sync", af("stB", 4096), wq_t[qh].rearrange("p a n -> p (a n)"), s="ld_w")
                        wait("gpsimd", "ld_w")
                        wait("gpsimd", "pe", semc["pe"])
                        tg = op("gpsimd", lambda e: e.tensor_copy(out=ab("wsbB", 4096),
                                                                  in_=af("stB", 4096)), "gp")
                        wsB = ab("wsbB", 4096).rearrange("p (a n) -> p a n", n=128)
                        wait("tensor", "gp", tg)
                        wait("tensor", "ld_l")
                        wait("tensor", "pe", qb_guard[qh % 2])
                        wait("tensor", "ve", semc["ve"])
                        for tgrp in range(2):
                            pk = qpP[:, tgrp * 512:(tgrp + 1) * 512]
                            for ch in range(NCH):
                                tpk = mm(pk, wsB[:, ch, :],
                                         lnT3[:, ch, tgrp * 512:(tgrp + 1) * 512],
                                         ch == 0, ch == NCH - 1)
                            wait("vector", "pe", tpk)
                            op("vector", lambda e, pk=pk, qb=qb, tgrp=tgrp: e.tensor_scalar_mul(
                                out=qb[:, tgrp * 512:(tgrp + 1) * 512], in0=pk, scalar1=SCALE), "ve")
                        tqev = semc["ve"]
                        wait("tensor", "ve", tqev)
                        wait("tensor", "ld_k")
                        wait("tensor", "ld_v")
                        for tloc in range(LB[3]):
                            t = half * 8 + tloc
                            fks = [("L", j) for j in range(t + 1)] + [("P", j) for j in range(t + 1)]
                            nfk = len(fks)
                            nr = (nfk + 3) // 4
                            for r in range(nr):
                                sub = fks[r * 4:(r + 1) * 4]
                                hb = r % 4
                                abh = arena[:, hb * 512:hb * 512 + 512]
                                wait("tensor", "ac", arn_cnt[hb])
                                kk = 0
                                while kk < len(sub):
                                    side, j0 = sub[kk]
                                    run = 1
                                    while (kk + run < len(sub) and sub[kk + run][0] == side
                                           and sub[kk + run][1] == j0 + run):
                                        run += 1
                                    base = (0 if side == "L" else 2048) + j0 * 128
                                    if kk + run == len(sub):
                                        tpk = mm(abh[:, kk * 128:(kk + run) * 128],
                                                 qb[:, tloc * 128:(tloc + 1) * 128],
                                                 ab("khb", 4096)[:, base:base + run * 128],
                                                 True, True)
                                    else:
                                        def fseg(eng, kk=kk, run=run, base=base, abh=abh,
                                                 qb=qb, tloc=tloc):
                                            eng.matmul(out=abh[:, kk * 128:(kk + run) * 128],
                                                       lhsT=qb[:, tloc * 128:(tloc + 1) * 128],
                                                       rhs=ab("khb", 4096)[:, base:base + run * 128],
                                                       start=True, stop=True)
                                        q["tensor"].append(fseg)
                                    kk += run
                                if B_STAGE < 2:
                                    arn_cnt[hb] = arn_cnt[hb]
                                    continue
                                wait("vector", "pe", tpk)
                                for kk2, (side, j) in enumerate(sub):
                                    if j == t:
                                        trib = tri1b if side == "L" else tri2b
                                        op("vector", lambda e, kk2=kk2, abh=abh, trib=trib: e.tensor_add(
                                            out=abh[:, kk2 * 128:(kk2 + 1) * 128],
                                            in0=abh[:, kk2 * 128:(kk2 + 1) * 128], in1=trib[:]))
                                used = len(sub) * 128
                                tv = op("vector", lambda e, abh=abh, used=used, r=r: e.reduce_max(
                                    out=mxb[:, r:r + 1], in_=abh[:, 0:used], axis=AX.X,
                                    negate=True), "ve")
                                wait("scalar", "ve", tv)
                                if B_STAGE < 3:
                                    continue
                                if r == 0:
                                    wait("scalar", "pe", pbuf_guard[0])
                                wait("scalar", "ac", semc["ac"])
                                semc["ac"] += 1
                                arn_cnt[hb] = semc["ac"]
                                def fe(eng, abh=abh, used=used, r=r, h2=semh["ac"]):
                                    eng.activation(out=ab("pbuf", 4096)[:, r * 512:r * 512 + used],
                                                   in_=abh[:, 0:used], func=AF.Exp,
                                                   bias=mxb[:, r:r + 1],
                                                   accum_out=sab[:, r:r + 1]).then_inc(h2, 1)
                                q["scalar"].append(fe)
                            if B_STAGE < 3:
                                continue
                            if B_STAGE < 4:
                                continue
                            wait("vector", "ac", semc["ac"])
                            op("vector", lambda e, nr=nr: e.tensor_scalar_mul(
                                out=fcb[:, 0:nr], in0=mxb[:, 0:nr], scalar1=-1.0))
                            tv = op("vector", lambda e, nr=nr: e.reduce_max(
                                out=stat[:, 8:9], in_=fcb[:, 0:nr], axis=AX.X, negate=True), "ve")
                            wait("scalar", "ve", tv)
                            ts = op("scalar", lambda e, nr=nr: e.activation(
                                out=fcb[:, 0:nr], in_=fcb[:, 0:nr], func=AF.Exp,
                                bias=stat[:, 8:9]), "ac")
                            wait("vector", "ac", ts)
                            for r in range(nr):
                                op("vector", lambda e, r=r: e.tensor_scalar_mul(
                                    out=sab[:, r:r + 1], in0=sab[:, r:r + 1],
                                    scalar1=fcb[:, r:r + 1]))
                            op("vector", lambda e, nr=nr: e.reduce_sum(
                                out=stat[:, 9:10], in_=sab[:, 0:nr], axis=AX.X))
                            op("vector", lambda e: e.reciprocal(out=stat[:, 10:11], in_=stat[:, 9:10]))
                            op("vector", lambda e, nr=nr: e.tensor_scalar_mul(
                                out=fcb[:, 0:nr], in0=fcb[:, 0:nr], scalar1=stat[:, 10:11]))
                            tnorm = None
                            for r in range(nr):
                                used = min(4, nfk - r * 4) * 128
                                tnorm = op("vector", lambda e, r=r, used=used: e.tensor_scalar_mul(
                                    out=ab("pbuf", 4096)[:, r * 512:r * 512 + used],
                                    in0=ab("pbuf", 4096)[:, r * 512:r * 512 + used],
                                    scalar1=fcb[:, r:r + 1]), "ve")
                            if B_STAGE < 5:
                                continue
                            wait("tensor", "ve", tnorm)
                            wait("tensor", "ptc", semc["ptc"])
                            for bt in range((nfk + 7) // 8):
                                wait("tensor", "trc", semc["trc"])
                                cnt = min(8, nfk - bt * 8)
                                tp = tposes([ab("pbuf", 4096)[:, (bt * 8 + k) * 128:(bt * 8 + k + 1) * 128]
                                             for k in range(cnt)], list(range(cnt)))
                                wait("vector", "pe", tp)
                                op("vector", lambda e, bt=bt, cnt=cnt: e.tensor_copy(
                                    out=ab("ptb", 4096)[:, bt * 1024:bt * 1024 + cnt * 128],
                                    in_=trB[:, 0:cnt * 128]), "trc")
                            pbuf_guard[0] = semc["pe"]
                            if B_STAGE < 6:
                                continue
                            wait("tensor", "trc", semc["trc"])
                            for i_fk, (side, j) in enumerate(fks):
                                vcol = (j if side == "L" else 16 + j) * 128
                                tav = mm(avP[:, (tloc % 4) * 128:(tloc % 4) * 128 + 128],
                                         ab("vhb", 4096)[:, vcol:vcol + 128],
                                         ab("ptb", 4096)[:, i_fk * 128:(i_fk + 1) * 128],
                                         i_fk == 0, i_fk == nfk - 1)
                            semc["ptc"] += 1
                            def fpt(eng, h2=semh["ptc"]):
                                eng.nop().then_inc(h2, 1)
                            q["tensor"].append(fpt)
                            if tloc % 4 == 3:
                                wait("vector", "pe", tav)
                                wait("vector", "st_at")
                                tv = op("vector", lambda e: e.tensor_copy(out=ab("atst", 512),
                                                                          in_=avP[:]), "ve")
                                wait("sync", "ve", tv)
                                Gt = t // 4
                                dst = at3[4 * Gt:4 * Gt + 4, :, qh, :].rearrange("j p q -> p j q")
                                src = ab("atst", 512).rearrange("p (j q) -> p j q", q=128)
                                dma("sync", dst, src, s="st_at")
                                wait("tensor", "ve", tv)
                        qb_guard[qh % 2] = semc["pe"]
            barrier()

        if "C" in phases:
            wo3 = ab("wosb", 32768).rearrange("p (a n) -> p a n", n=1024)
            for qq in range(4):
                for piece in range(2):
                    wait("sync", "gp", semc["gp"])
                    dma("sync", af("stC", 16384), wo_t[qq * 2 + piece].rearrange("p a n -> p (a n)"), s="ld_w")
                    wait("gpsimd", "ld_w")
                    wait("gpsimd", "pe", semc["pe"])
                    src = af("stC", 16384).rearrange("p (a n) -> p a n", n=512)
                    dst = wo3[:, :, piece * 512:(piece + 1) * 512]
                    tg = op("gpsimd", lambda e, src=src, dst=dst: e.tensor_copy(out=dst, in_=src), "gp")
                for tt in range(16):
                    wait("sync", "pe", semc["pe"])
                    dma("sync", ab("atb", 4096), at3[tt].rearrange("p c q -> p (c q)"), s="ld_a")
                    wait("tensor", "ld_a")
                    wait("tensor", "gp", semc["gp"])
                    wait("tensor", "ve", semc["ve"])
                    for dq in range(NCH):
                        for obi in range(2):
                            tpk = mm(arena[:, obi * 512:(obi + 1) * 512],
                                     ab("atb", 4096)[:, dq * 128:(dq + 1) * 128],
                                     wo3[:, dq, obi * 512:(obi + 1) * 512],
                                     dq == 0, dq == NCH - 1)
                    wait("vector", "pe", tpk)
                    wait("vector", "st_y")
                    op("vector", lambda e: e.tensor_copy(out=af("yev", 1024)[:, 0:512],
                                                         in_=arena[:, 0:512]))
                    tv = op("vector", lambda e: e.tensor_copy(out=af("yev", 1024)[:, 512:1024],
                                                              in_=arena[:, 512:1024]), "ve")
                    wait("sync", "ve", tv)
                    dma("sync", y_s[tt][:, qq * 1024:(qq + 1) * 1024],
                        af("yev", 1024), s="st_y")
            barrier()

        if "C" in phases:
            cb3 = centb[:].rearrange("p (a c) -> p a c", c=64)
            x1_store = 0
            lct_store = 0
            for tt in range(16):
                wait("sync", "ve", semc["ve"])
                wait("sync", "ac", semc["ac"])
                dma("sync", af("yfull", 4096), y_s[tt], s="ld_y")
                dma("sync", af("x0t", 4096), x0[tt * 128:(tt + 1) * 128, :], s="ld_x")
                wait("vector", "ld_y")
                CUR_SQ[0] = af("sq2", 4096)
                ln_stats(af("yfull", 4096))
                ta = ln_apply(af("yfull", 4096), af("sq2", 4096))
                wait("vector", "ac", ta)
                wait("vector", "ld_x")
                wait("vector", "st_x1")
                wait("vector", "ac", semc["ac"])
                tv = op("vector", lambda e: e.tensor_add(out=af("x1t", 4096), in0=af("x0t", 4096),
                                                         in1=af("sq2", 4096)), "ve")
                wait("sync", "ve", tv)
                dma("sync", x1o[tt * 128:(tt + 1) * 128, :], af("x1t", 4096), s="st_x1")
                ln_stats(af("x1t", 4096))
                wait("scalar", "pe", semc["pe"])
                ta = ln_apply(af("x1t", 4096), ab("lcC", 4096))
                wait("tensor", "ac", ta)
                wait("tensor", "st_lct")
                for bch in range(4):
                    wait("tensor", "trc", semc["trc"])
                    tp = tposes([ab("lcC", 4096)[:, (bch * 8 + k) * 128:(bch * 8 + k + 1) * 128]
                                 for k in range(8)], list(range(8)))
                    wait("vector", "pe", tp)
                    dst = ab("lcTt", 4096).rearrange("p (c t) -> p c t", t=128)[:, bch * 8:(bch + 1) * 8, :]
                    op("vector", lambda e, dst=dst: e.tensor_copy(out=dst, in_=trB3[:, 0:8, :]), "trc")
                wait("sync", "trc", semc["trc"])
                dma("sync", lcT4[tt].rearrange("p c q -> p (c q)"), ab("lcTt", 4096), s="st_lct")
                wait("tensor", "trc", semc["trc"])
                for ch in range(NCH):
                    tpk = mm(arena[0:64, 1024:1152], cb3[:, ch, :],
                             ab("lcTt", 4096)[:, ch * 128:(ch + 1) * 128],
                             ch == 0, ch == NCH - 1)
                wait("vector", "pe", tpk)
                tv = op("vector", lambda e: e.tensor_copy(out=lgb[0:64, :], in_=arena[0:64, 1024:1152]), "ve")
                wait("tensor", "ve", tv)
                semc["pe"] += 1
                def ftr1(eng, h=semh["pe"]):
                    eng.transpose(out=arena[:, 512:640], in_=lgb[:], identity=idf[:]).then_inc(h, 1)
                q["tensor"].append(ftr1)
                wait("vector", "pe", semc["pe"])
                tv = op("vector", lambda e: e.reduce_max(out=dsm[:, 64:65], in_=arena[:, 512:576],
                                                         axis=AX.X, negate=True), "ve")
                tv = op("vector", lambda e: e.tensor_scalar_mul(out=dsm[:, 65:66], in0=dsm[:, 64:65],
                                                                scalar1=10.0), "ve")
                wait("scalar", "ve", tv)
                ts = op("scalar", lambda e: e.activation(out=dsm[:, 0:64], in_=arena[:, 512:576],
                                                         func=AF.Exp, scale=10.0, bias=dsm[:, 65:66],
                                                         accum_out=dsm[:, 66:67]), "ac")
                wait("vector", "ac", ts)
                op("vector", lambda e: e.reciprocal(out=dsm[:, 67:68], in_=dsm[:, 66:67]))
                op("vector", lambda e: e.tensor_scalar_mul(out=dsm[:, 0:64], in0=dsm[:, 0:64],
                                                       scalar1=dsm[:, 67:68]))
                op("vector", lambda e: e.reduce_max(out=dsm[:, 68:69], in_=dsm[:, 0:64], axis=AX.X))
                tv = op("vector", lambda e, tt=tt: e.tensor_single_scalar(
                    out=qsel_sb[:, tt:tt + 1], in_=dsm[:, 68:69], scalar=0.01, op=ALU.is_gt), "ve")
                wait("tensor", "ve", tv)
                semc["pe"] += 1
                def ftr2(eng, h=semh["pe"]):
                    eng.transpose(out=arena[0:64, 1536:1664], in_=dsm[:, 0:64],
                                  identity=idf[:]).then_inc(h, 1)
                q["tensor"].append(ftr2)
                wait("vector", "pe", semc["pe"])
                op("vector", lambda e: e.reduce_max(out=cmax_sb[:, 1:2],
                                                    in_=arena[0:64, 1536:1664], axis=AX.X))
                op("vector", lambda e: e.tensor_copy(out=cmax_sb[:, 2:3], in_=cmax_sb[:, 0:1]))
                op("vector", lambda e: e.tensor_scalar_max(out=cmax_sb[:, 0:1],
                                                           in0=cmax_sb[:, 1:2],
                                                           scalar1=cmax_sb[:, 2:3]))
            tv = semc["ve"]
            wait("sync", "ve", tv)
            dma("sync", qselo, qsel_sb[:], s="st_o")
            dma("sync", cmaxo, cmax_sb[:, 0:1], s="st_o")
            barrier()

        if "D" in phases:
            dma("sync", af("biasb", 4096), bias_i, s="ld_m")
            dstg_store = 0
            for half in range(2):
                wait("sync", "pe", semc["pe"])
                for tt8 in range(8):
                    dma("sync", ab("lch", 32768)[:, tt8 * 4096:(tt8 + 1) * 4096],
                        lcT4[half * 8 + tt8].rearrange("p c q -> p (c q)"), s="ld_l")
                for ob in range(8):
                    for piece in range(2):
                        wait("sync", "gp", semc["gp"])
                        dma("sync", af("stD", 8192), wT_t[ob * 2 + piece].rearrange("p a n -> p (a n)"), s="ld_w")
                        wait("gpsimd", "ld_w")
                        wait("gpsimd", "pe", semc["pe"])
                        src = af("stD", 8192).rearrange("p (a n) -> p a n", n=256)
                        dst = ab("wTsb", 16384).rearrange("p (a n) -> p a n", n=512)[:, :, piece * 256:(piece + 1) * 256]
                        tg = op("gpsimd", lambda e, src=src, dst=dst: e.tensor_copy(out=dst, in_=src), "gp")
                    wT3 = ab("wTsb", 16384).rearrange("p (a n) -> p a n", n=512)
                    wait("tensor", "gp", semc["gp"])
                    wait("tensor", "ld_l")
                    wait("tensor", "ld_m")
                    for tt8 in range(8):
                        wait("tensor", "ve", semc["ve"])
                        pk = arena[:, (tt8 % 2) * 512:(tt8 % 2) * 512 + 512]
                        for ch in range(NCH):
                            tpk = mm(pk, ab("lch", 32768)[:, (tt8 * 32 + ch) * 128:(tt8 * 32 + ch + 1) * 128],
                                     wT3[:, ch, :], ch == 0, ch == NCH - 1)
                        wait("vector", "pe", tpk)
                        wait("vector", "st_d")
                        tv = op("vector", lambda e, pk=pk, ob=ob: e.tensor_add(
                            out=ab("dstg", 512), in0=pk,
                            in1=af("biasb", 4096)[:, ob * 512:(ob + 1) * 512]), "ve")
                        wait("sync", "ve", tv)
                        r0 = (half * 8 + tt8) * 128
                        dma("sync", denseo[r0:r0 + 128, ob * 512:(ob + 1) * 512],
                            ab("dstg", 512), s="st_d")
            barrier()

        wait_all_dma("sync")

        for e in ENGS:
            getattr(blk, e)(lambda eng, qq=q[e]: [f(eng) for f in qq])

    return nc


def build_launch2():
    nc = bass.Bass("TRN2", target_bir_lowering=False, debug=False)
    x1 = nc.dram_tensor("x1", [TL, H], F32, kind="ExternalInput").ap()
    dense = nc.dram_tensor("dense", [TL, H], BF16, kind="ExternalInput").ap()
    qsel = nc.dram_tensor("qsel", [128, 16], F32, kind="ExternalInput").ap()
    bits = nc.dram_tensor("bits", [64, 8], F32, kind="ExternalInput").ap()
    AT = nc.dram_tensor("AT", [64, H], BF16, kind="ExternalInput").ap()
    ones64 = nc.dram_tensor("ones64", [64, 128], BF16, kind="ExternalInput").ap()
    out = nc.dram_tensor("out", [TL, H], F32, kind="ExternalOutput").ap()

    q = {e: [] for e in ENGS}
    ctx = ExitStack()
    with ctx:
        semh, semc = {}, {}
        for n in ["ld_m", "ld_d", "ld_x", "st_o", "pe", "ve", "ac"]:
            semh[n] = ctx.enter_context(nc.semaphore(n + "2")); semc[n] = 0
        SBt = lambda n, sh, dt=F32: ctx.enter_context(nc.sbuf_tensor(n, sh, dt))
        bsb = SBt("bsb", [64, 8]); sel = SBt("sel", [64, 2])
        ATs = SBt("ATs", [64, H], BF16); ones_b = SBt("ones_b", [64, 128], BF16)
        rsel = SBt("rsel", [128, H]); dsb = SBt("dsb", [128, H], BF16)
        x1b = SBt("x1b", [128, H]); vb = SBt("vb", [128, H])
        sq = SBt("sq", [128, H]); qsb = SBt("qsb", [128, 16]); st2 = SBt("st2", [128, 16])
        ob_out = SBt("ob_out", [128, H])
        epsb2 = SBt("epsb2", [128, 1])
        pr = ctx.enter_context(nc.psum_tensor("pr2", [128, 1024], F32))
        blk = ctx.enter_context(nc.Block())

        def dma(e, o, i, s="ld"):
            semc[s] += 16
            def f(eng, o=o, i=i, h=semh[s]):
                eng.dma_start(out=o, in_=i).then_inc(h, 16)
            q[e].append(f)
            return semc[s]

        def wait(e, s, v=None):
            if v is None:
                v = semc[s]
            if v <= 0:
                return
            def f(eng, h=semh[s], v=v):
                eng.wait_ge(h, v)
            q[e].append(f)

        OWN2 = {"vector": "ve", "scalar": "ac"}

        def op(e, fn, s=None):
            if e in OWN2:
                own = OWN2[e]
                wait(e, own, semc[own])
                if s is None:
                    s = own
            if s is None:
                def f0(eng, fn=fn):
                    fn(eng)
                q[e].append(f0)
                return None
            semc[s] += 1
            def f(eng, fn=fn, h=semh[s]):
                fn(eng).then_inc(h, 1)
            q[e].append(f)
            return semc[s]

        op("vector", lambda e: e.memset(epsb2[:], EPS))
        dma("sync", bsb[:], bits, s="ld_m")
        dma("sync", ATs[:], AT, s="ld_m")
        dma("sync", ones_b[:], ones64, s="ld_m")
        dma("sync", qsb[:], qsel, s="ld_m")
        wait("vector", "ld_m")
        op("vector", lambda e: e.reduce_max(out=sel[:, 0:1], in_=bsb[:], axis=AX.X))
        op("vector", lambda e: e.tensor_single_scalar(out=sel[:, 1:2], in_=sel[:, 0:1],
                                                      scalar=0.01, op=ALU.is_gt))
        tv = op("vector", lambda e: e.tensor_scalar_mul(out=ATs[:], in0=ATs[:], scalar1=sel[:, 1:2]), "ve")
        wait("tensor", "ve", tv)
        copy_t = [0, 0]
        for ob in range(8):
            wait("tensor", "ve", copy_t[ob % 2])
            tp = op("tensor", lambda e, ob=ob: e.matmul(
                out=pr[:, (ob % 2) * 512:(ob % 2) * 512 + 512], lhsT=ones_b[:],
                rhs=ATs[:, ob * 512:(ob + 1) * 512], start=True, stop=True), "pe")
            wait("vector", "pe", tp)
            copy_t[ob % 2] = op("vector", lambda e, ob=ob: e.tensor_copy(
                out=rsel[:, ob * 512:(ob + 1) * 512],
                in_=pr[:, (ob % 2) * 512:(ob % 2) * 512 + 512]), "ve")

        for tt in range(16):
            wait("sync", "ve", semc["ve"])
            wait("sync", "ac", semc["ac"])
            dma("sync", dsb[:], dense[tt * 128:(tt + 1) * 128, :], s="ld_d")
            dma("sync", x1b[:], x1[tt * 128:(tt + 1) * 128, :], s="ld_x")
            wait("vector", "ld_d")
            op("vector", lambda e, tt=tt: e.tensor_scalar_mul(out=vb[:], in0=dsb[:],
                                                          scalar1=qsb[:, tt:tt + 1]))
            op("vector", lambda e: e.tensor_mul(out=vb[:], in0=vb[:], in1=rsel[:]))
            op("vector", lambda e: e.reduce_sum(out=st2[:, 0:1], in_=vb[:], axis=AX.X))
            op("vector", lambda e: e.scalar_tensor_tensor(out=sq[:], in0=vb[:], scalar=1.0,
                                                          in1=vb[:], op0=ALU.mult, op1=ALU.mult,
                                                          accum_out=st2[:, 1:2]))
            op("vector", lambda e: e.tensor_scalar_mul(out=st2[:, 2:3], in0=st2[:, 0:1],
                                                       scalar1=1.0 / H))
            op("vector", lambda e: e.tensor_copy(out=st2[:, 9:10], in_=st2[:, 2:3]))
            op("vector", lambda e: e.tensor_scalar_mul(out=st2[:, 3:4], in0=st2[:, 2:3],
                                                       scalar1=st2[:, 9:10]))
            op("vector", lambda e: e.tensor_scalar_mul(out=st2[:, 8:9], in0=st2[:, 1:2],
                                                       scalar1=1.0 / H))
            tv = op("vector", lambda e: e.tensor_single_scalar(
                out=st2[:, 4:5], in_=st2[:, 8:9], scalar=st2[:, 3:4], op=ALU.subtract), "ve")
            wait("scalar", "ve", tv)
            ts = op("scalar", lambda e: e.activation(out=st2[:, 5:6], in_=st2[:, 4:5],
                                                     func=AF.Sqrt, bias=epsb2[:]), "ac")
            wait("vector", "ac", ts)
            op("vector", lambda e: e.reciprocal(out=st2[:, 6:7], in_=st2[:, 5:6]))
            op("vector", lambda e: e.tensor_scalar_mul(out=st2[:, 10:11], in0=st2[:, 2:3],
                                                       scalar1=-1.0))
            tv = op("vector", lambda e: e.tensor_scalar_mul(out=st2[:, 7:8], in0=st2[:, 10:11],
                                                            scalar1=st2[:, 6:7]), "ve")
            wait("scalar", "ve", tv)
            ts = op("scalar", lambda e: e.activation(out=sq[:], in_=vb[:], func=AF.Identity,
                                                     scale=st2[:, 6:7], bias=st2[:, 7:8]), "ac")
            wait("vector", "ac", ts)
            wait("vector", "ld_x")
            wait("vector", "st_o")
            tv = op("vector", lambda e: e.tensor_add(out=ob_out[:], in0=x1b[:], in1=sq[:]), "ve")
            wait("sync", "ve", tv)
            dma("sync", out[tt * 128:(tt + 1) * 128, :], ob_out[:], s="st_o")
        wait("sync", "st_o")

        for e in ENGS:
            getattr(blk, e)(lambda eng, qq=q[e]: [f(eng) for f in qq])
    return nc


def prep_shared(inputs):
    f4 = np.float32
    wq = np.ascontiguousarray(inputs["L0_attn_w_q"], f4)
    wk = np.ascontiguousarray(inputs["L0_attn_w_k"], f4)
    wv = np.ascontiguousarray(inputs["L0_attn_w_v"], f4)
    wo = np.ascontiguousarray(inputs["L0_attn_w_o"], f4)
    W = np.ascontiguousarray(inputs["L0_ffn_weight"], f4)
    bias = np.asarray(inputs["L0_ffn_bias"], f4)
    cent = np.ascontiguousarray(inputs["L0_ffn_centroids"], f4)
    assign = np.asarray(inputs["L0_ffn_assignments"])

    d = {}
    d["wq_t"] = np.ascontiguousarray(wq.reshape(32, 128, 32, 128).transpose(2, 1, 0, 3))
    wk_t = wk.reshape(32, 128, 4, 256).transpose(2, 1, 0, 3)
    wv_t = wv.reshape(32, 128, 4, 256).transpose(2, 1, 0, 3)
    d["wkv_t"] = np.ascontiguousarray(np.concatenate([wk_t, wv_t], axis=0))
    d["wo_t"] = np.ascontiguousarray(wo.reshape(32, 128, 8, 512).transpose(2, 1, 0, 3))
    wT = np.ascontiguousarray(W.T)
    d["wT_t"] = np.ascontiguousarray(wT.reshape(32, 128, 16, 256).transpose(2, 1, 0, 3))
    d["bias_i"] = np.ascontiguousarray(np.broadcast_to(bias[None, :], (128, H)))
    d["centT"] = np.ascontiguousarray(cent.T)
    d["identb"] = np.eye(128, dtype=ml_dtypes.bfloat16)
    d["identf"] = np.eye(128, dtype=np.float32)
    jj = np.arange(128)[None, :]
    pp = np.arange(128)[:, None]
    d["tri1"] = np.where(jj <= pp, 0.0, -1e9).astype(f4)
    tri2_p0 = np.where(jj <= pp - 1, 0.0, -1e9).astype(f4)
    d["tri2_p0"] = tri2_p0
    d["tri2_p1"] = d["tri1"]
    AT = np.zeros((64, H), np.float32)
    AT[np.asarray(assign, np.int64), np.arange(H)] = 1.0
    d["AT"] = AT.astype(ml_dtypes.bfloat16)
    d["ones64"] = np.ones((64, 128), dtype=ml_dtypes.bfloat16)
    return d


def core_inputs1(inputs, shared, c):
    b, p = c // 2, c % 2
    x = np.asarray(inputs["L0_attn_hidden_states"], np.float32)
    x0p = np.ascontiguousarray(np.concatenate([x[b, p::2], x[b, 1 - p::2]], axis=0))
    return {
        "x0": x0p, "wq_t": shared["wq_t"], "wkv_t": shared["wkv_t"],
        "wo_t": shared["wo_t"], "wT_t": shared["wT_t"], "bias_i": shared["bias_i"],
        "centT": shared["centT"], "tri1": shared["tri1"],
        "tri2": shared["tri2_p0"] if p == 0 else shared["tri2_p1"],
        "identb": shared["identb"], "identf": shared["identf"],
    }


def kernel(**inputs):
    shared = prep_shared(inputs)
    nc1 = build_launch1()
    in_maps1 = [core_inputs1(inputs, shared, c) for c in range(8)]
    res1 = run_bass_kernel_spmd(nc1, in_maps1, core_ids=list(range(8)))
    bits = np.stack([np.asarray(res1.results[c]["cmax"][:, 0], np.float32)
                     for c in range(8)], axis=1)
    bits = np.ascontiguousarray(bits)
    nc2 = build_launch2()
    in_maps2 = []
    for c in range(8):
        r = res1.results[c]
        in_maps2.append({
            "x1": np.asarray(r["x1"], np.float32),
            "dense": np.asarray(r["dense"]).view(ml_dtypes.bfloat16)
                     if r["dense"].dtype != ml_dtypes.bfloat16 else r["dense"],
            "qsel": np.asarray(r["qsel"], np.float32),
            "bits": bits, "AT": shared["AT"], "ones64": shared["ones64"],
        })
    res2 = run_bass_kernel_spmd(nc2, in_maps2, core_ids=list(range(8)))
    out = np.empty((B_, S_, H), np.float32)
    for c in range(8):
        b, p = c // 2, c % 2
        x2 = np.asarray(res2.results[c]["out"], np.float32)
        out[b, p::2, :] = x2
    return out


# version 03
# speedup vs baseline: 1.0670x; 1.0670x over previous
import numpy as np
import ml_dtypes
from contextlib import ExitStack
import concourse.bass as bass
import concourse.mybir as mybir
from concourse.bass_utils import run_bass_kernel_spmd

F32 = mybir.dt.float32
BF16 = mybir.dt.bfloat16
AF = mybir.ActivationFunctionType
ALU = mybir.AluOpType
AX = mybir.AxisListType

B_, S_, H = 4, 4096, 4096
NKV, D = 8, 128
TL, TF = 2048, 4096
NCH = H // 128
SCALE = float(1.0 / np.sqrt(128.0))
EPS = 1e-5

ENGS = ("sync", "tensor", "vector", "scalar", "gpsimd")
B_LIMIT = None
B_STAGE = 6

AB_CAP = 66048
ABO = dict(lnT=0, wsbA=32768, lnb=40960, kst=45056, vtm=45568, vst=46080,
           wsbB=32768, khb=36864, vhb=40960, qs0=45056, qs1=47104,
           pbuf0=49152, pbuf1=53248, ptb0=57344, ptb1=61440, atst=65536,
           wosb=0, atb=32768, atb1=36864,
           lcC=40960, lcTt=36864,
           lch=0, wTsb=32768, dstg=49152)
AF_CAP = 17408
AFO = dict(stA=0, xbuf0=8192, xbuf1=12288,
           stB=0,
           stC=0, yev=16384,
           yfull=0, x0t=4096, x1t=8192, sq2=12288,
           stD=0, biasb=8192)


def build_launch1(dbg=(), phases="ABCD"):
    nc = bass.Bass("TRN2", target_bir_lowering=False, debug=False)

    def inp(name, shape, dt=F32):
        return nc.dram_tensor(name, shape, dt, kind="ExternalInput").ap()

    def outp(name, shape, dt=F32):
        return nc.dram_tensor(name, shape, dt, kind="ExternalOutput").ap()

    def scr(name, shape, dt):
        kind = "ExternalOutput" if name in dbg else "Internal"
        return nc.dram_tensor(name, shape, dt, kind=kind).ap()

    x0 = inp("x0", [TF, H])
    wq_t = inp("wq_t", [32, 128, NCH, 128])
    wkv_t = inp("wkv_t", [8, 128, NCH, 256])
    wo_t = inp("wo_t", [8, 128, NCH, 512])
    wT_t = inp("wT_t", [16, 128, NCH, 256])
    bias_i = inp("bias_i", [128, H])
    centT = inp("centT", [H, 64])
    tri1 = inp("tri1", [128, 128]); tri2 = inp("tri2", [128, 128])
    identb = inp("identb", [128, 128], BF16)
    identf = inp("identf", [128, 128], F32)

    kT = scr("kT", [NKV, 128, TF], BF16)
    vT = scr("vT", [NKV, 128, 32, 128], BF16)
    lnTs = scr("lnTs", [2, 128, NCH * 1024], BF16)
    at3 = scr("at3", [16, 128, NCH, 128], BF16)
    y_s = scr("y_s", [16, 128, H], F32)
    lcT4 = scr("lcT4", [16, 128, NCH, 128], BF16)

    x1o = outp("x1", [TL, H], F32)
    denseo = outp("dense", [TL, H], BF16)
    qselo = outp("qsel", [128, 16], F32)
    cmaxo = outp("cmax", [64, 1], F32)

    q = {e: [] for e in ENGS}
    pbuf_guard = [0]
    at_store = [0]
    yev_store = [0]
    ctx = ExitStack()
    with ctx:
        semh, semc = {}, {}
        for n in ["ld_x", "ld_w", "ld_k", "ld_v", "ld_l", "ld_a", "ld_y", "ld_m",
                  "st_kv", "st_ln", "st_at", "st_y", "st_x1", "st_lct", "st_d", "st_o",
                  "pe", "ve", "ac", "gp", "bar", "arn0", "arn1", "arn2", "arn3", "trc", "ptc"]:
            semh[n] = ctx.enter_context(nc.semaphore(n)); semc[n] = 0

        AB = ctx.enter_context(nc.sbuf_tensor("AB", [128, AB_CAP], BF16))
        AFm = ctx.enter_context(nc.sbuf_tensor("AFm", [128, AF_CAP], F32))
        tri1b = ctx.enter_context(nc.sbuf_tensor("tri1b", [128, 128], F32))
        tri2b = ctx.enter_context(nc.sbuf_tensor("tri2b", [128, 128], F32))
        idb = ctx.enter_context(nc.sbuf_tensor("idb", [128, 128], BF16))
        idf = ctx.enter_context(nc.sbuf_tensor("idf", [128, 128], F32))
        stat = ctx.enter_context(nc.sbuf_tensor("stat", [128, 16], F32))
        mxb = ctx.enter_context(nc.sbuf_tensor("mxb", [128, 16], F32))
        sab = ctx.enter_context(nc.sbuf_tensor("sab", [128, 16], F32))
        fcb = ctx.enter_context(nc.sbuf_tensor("fcb", [128, 16], F32))
        qsel_sb = ctx.enter_context(nc.sbuf_tensor("qsel_sb", [128, 16], F32))
        cmax_sb = ctx.enter_context(nc.sbuf_tensor("cmax_sb", [64, 4], F32))
        centb = ctx.enter_context(nc.sbuf_tensor("centb", [128, NCH * 64], BF16))
        dsm = ctx.enter_context(nc.sbuf_tensor("dsm", [128, 80], F32))
        epsb = ctx.enter_context(nc.sbuf_tensor("epsb", [128, 1], F32))
        lgb = ctx.enter_context(nc.sbuf_tensor("lgb", [128, 128], F32))

        arena = ctx.enter_context(nc.psum_tensor("arena", [128, 2048], F32))
        trB = ctx.enter_context(nc.psum_tensor("trBp", [128, 1024], BF16))
        avP = ctx.enter_context(nc.psum_tensor("avPp", [128, 512], F32))
        qpP = ctx.enter_context(nc.psum_tensor("qpPp", [128, 1024], F32))

        blk = ctx.enter_context(nc.Block())

        def ab(name, n):
            return AB[:, ABO[name]:ABO[name] + n]

        def af(name, n):
            return AFm[:, AFO[name]:AFO[name] + n]

        def dma(e, out, in_, s="ld", **kw):
            semc[s] += 16
            h, tgt = semh[s], semc[s]
            def f(eng, out=out, in_=in_, h=h, kw=kw):
                eng.dma_start(out=out, in_=in_, **kw).then_inc(h, 16)
            q[e].append(f)
            return tgt

        def wait(e, s, v=None):
            if v is None:
                v = semc[s]
            if v <= 0:
                return
            def f(eng, h=semh[s], v=v):
                eng.wait_ge(h, v)
            q[e].append(f)

        def wait_all_dma(e):
            for s2 in ("ld_x", "ld_w", "ld_k", "ld_v", "ld_l", "ld_a", "ld_y", "ld_m",
                       "st_kv", "st_ln", "st_at", "st_y", "st_x1", "st_lct", "st_d", "st_o"):
                wait(e, s2)

        OWN = {"vector": "ve", "scalar": "ac", "gpsimd": "gp"}

        def op(e, fn, s=None):
            if e in ("vector", "scalar"):
                own = OWN[e]
                wait(e, own, semc[own])
                if s is None:
                    s = own
            if s is None:
                def f0(eng, fn=fn):
                    fn(eng)
                q[e].append(f0)
                return None
            semc[s] += 1
            h, tgt = semh[s], semc[s]
            def f(eng, fn=fn, h=h):
                fn(eng).then_inc(h, 1)
            q[e].append(f)
            return tgt

        def mm(out, lhsT, rhs, first, last, s="pe"):
            if last:
                semc[s] += 1
                def f(eng, out=out, lhsT=lhsT, rhs=rhs, first=first, h=semh[s]):
                    eng.matmul(out=out, lhsT=lhsT, rhs=rhs, start=first,
                               stop=True).then_inc(h, 1)
            else:
                def f(eng, out=out, lhsT=lhsT, rhs=rhs, first=first):
                    eng.matmul(out=out, lhsT=lhsT, rhs=rhs, start=first, stop=False)
            q["tensor"].append(f)
            return semc[s]

        def tposes(srcs, dsts_in_trB, last_inc=True):
            n = len(srcs)
            for i, src in enumerate(srcs):
                last = (i == n - 1) and last_inc
                if last:
                    semc["pe"] += 1
                    def f(eng, src=src, i=dsts_in_trB[i], h=semh["pe"]):
                        eng.transpose(out=trB[:, i * 128:(i + 1) * 128], in_=src,
                                      identity=idb[:]).then_inc(h, 1)
                else:
                    def f(eng, src=src, i=dsts_in_trB[i]):
                        eng.transpose(out=trB[:, i * 128:(i + 1) * 128], in_=src,
                                      identity=idb[:])
                q["tensor"].append(f)
            return semc["pe"]

        def barrier():
            wait_all_dma("sync")
            for e in ENGS:
                semc["bar"] += 1
                def f(eng, h=semh["bar"]):
                    eng.nop().then_inc(h, 1)
                q[e].append(f)
            v = semc["bar"]
            for e in ENGS:
                wait(e, "bar", v)

        def ln_stats(src):
            sq = CUR_SQ[0]
            op("vector", lambda e, src=src: e.reduce_sum(out=stat[:, 0:1], in_=src, axis=AX.X))
            op("vector", lambda e, src=src, sq=sq: e.scalar_tensor_tensor(
                out=sq, in0=src, scalar=1.0, in1=src, op0=ALU.mult, op1=ALU.mult,
                accum_out=stat[:, 1:2]))
            op("vector", lambda e: e.tensor_scalar_mul(out=stat[:, 2:3], in0=stat[:, 0:1],
                                                       scalar1=1.0 / H))
            op("vector", lambda e: e.tensor_copy(out=stat[:, 12:13], in_=stat[:, 2:3]))
            op("vector", lambda e: e.tensor_scalar_mul(out=stat[:, 3:4], in0=stat[:, 2:3],
                                                       scalar1=stat[:, 12:13]))
            op("vector", lambda e: e.tensor_scalar_mul(out=stat[:, 13:14], in0=stat[:, 1:2],
                                                       scalar1=1.0 / H))
            tv = op("vector", lambda e: e.tensor_single_scalar(
                out=stat[:, 4:5], in_=stat[:, 13:14], scalar=stat[:, 3:4], op=ALU.subtract))
            wait("scalar", "ve", tv)
            ts = op("scalar", lambda e: e.activation(out=stat[:, 5:6], in_=stat[:, 4:5],
                                                     func=AF.Sqrt, bias=epsb[:]))
            wait("vector", "ac", ts)
            op("vector", lambda e: e.reciprocal(out=stat[:, 6:7], in_=stat[:, 5:6]))
            op("vector", lambda e: e.tensor_scalar_mul(out=stat[:, 14:15], in0=stat[:, 2:3],
                                                       scalar1=-1.0))
            tv = op("vector", lambda e: e.tensor_scalar_mul(out=stat[:, 7:8], in0=stat[:, 14:15],
                                                            scalar1=stat[:, 6:7]))
            wait("scalar", "ve", tv)

        def ln_apply(src, dst):
            return op("scalar", lambda e, src=src, dst=dst: e.activation(
                out=dst, in_=src, func=AF.Identity, scale=stat[:, 6:7],
                bias=stat[:, 7:8]))

        CUR_SQ = [ab("lnb", 4096)]

        dma("sync", tri1b[:], tri1, s="ld_m")
        dma("sync", tri2b[:], tri2, s="ld_m")
        dma("sync", idb[:], identb, s="ld_m")
        t0 = dma("sync", idf[:], identf, s="ld_m")
        t_cent = dma("sync", af("stA", 2048).rearrange("p (a c) -> p a c", c=64),
                     centT.rearrange("(a p) c -> p a c", p=128), s="ld_m")
        wait("gpsimd", "ld_m")
        tg0 = op("gpsimd", lambda e: e.tensor_copy(out=centb[:], in_=af("stA", 2048)), "gp")
        op("vector", lambda e: e.memset(cmax_sb[:, 0:1], -1e30), "ve")
        op("vector", lambda e: e.memset(epsb[:], EPS), "ve")
        op("vector", lambda e: e.memset(qsel_sb[:], 0.0), "ve")
        wait("sync", "gp", tg0)

        lnT3 = ab("lnT", 32768).rearrange("p (c t) -> p c t", t=1024)
        trB3 = trB[:].rearrange("p (k q) -> p k q", q=128)

        if "A" in phases:
            CUR_SQ[0] = ab("lnb", 4096)
            xguard = [(0, 0), (0, 0)]
            xld = ["ld_x", "ld_y"]
            for sset in range(2):
                row0 = TL if sset == 0 else 0
                tok0 = TL if sset == 0 else 0
                for half in range(2):
                    wait("vector", "st_ln")
                    for tloc in range(8):
                        r0 = row0 + half * 1024 + tloc * 128
                        bix = tloc % 2
                        xb = af("xbuf%d" % bix, 4096)
                        wait("sync", "ve", xguard[bix][0])
                        wait("sync", "ac", xguard[bix][1])
                        dma("sync", xb, x0[r0:r0 + 128, :], s=xld[bix])
                        wait("vector", xld[bix])
                        wait("vector", "pe", semc["pe"])
                        ln_stats(xb)
                        wait("scalar", "pe", semc["pe"])
                        ta = ln_apply(xb, ab("lnb", 4096))
                        xguard[bix] = (semc["ve"], semc["ac"])
                        wait("tensor", "ac", ta)
                        for bch in range(4):
                            wait("tensor", "trc", semc["trc"])
                            tp = tposes([ab("lnb", 4096)[:, (bch * 8 + k) * 128:(bch * 8 + k + 1) * 128]
                                         for k in range(8)], list(range(8)))
                            wait("vector", "pe", tp)
                            dst = lnT3[:, bch * 8:(bch + 1) * 8, tloc * 128:(tloc + 1) * 128]
                            op("vector", lambda e, dst=dst: e.tensor_copy(out=dst, in_=trB3[:, 0:8, :]), "trc")
                    if sset == 1:
                        wait("sync", "trc", semc["trc"])
                        dma("sync", lnTs[half], ab("lnT", 32768), s="st_ln")
                    for sl in range(8):
                        wait("sync", "gp", semc["gp"])
                        dma("sync", af("stA", 8192), wkv_t[sl].rearrange("p a n -> p (a n)"), s="ld_w")
                        wait("gpsimd", "ld_w")
                        wait("gpsimd", "pe", semc["pe"])
                        tg = op("gpsimd", lambda e: e.tensor_copy(out=ab("wsbA", 8192),
                                                                  in_=af("stA", 8192)), "gp")
                        wsA = ab("wsbA", 8192).rearrange("p (a n) -> p a n", n=256)
                        wait("tensor", "gp", tg)
                        wait("tensor", "trc", semc["trc"])
                        for grp in range(2):
                            for dc in range(2):
                                pk = qpP[:, dc * 512:(dc + 1) * 512]
                                wait("tensor", "ve", semc["ve"])
                                for ch in range(NCH):
                                    tpk = mm(pk, wsA[:, ch, dc * 128:(dc + 1) * 128],
                                             lnT3[:, ch, grp * 512:(grp + 1) * 512],
                                             ch == 0, ch == NCH - 1)
                                wait("vector", "pe", tpk)
                                hh = (sl % 4) * 2 + dc
                                c0 = tok0 + half * 1024 + grp * 512
                                if sl < 4:
                                    wait("vector", "st_kv")
                                    tv = op("vector", lambda e, pk=pk: e.tensor_copy(
                                        out=ab("kst", 512), in_=pk), "ve")
                                    wait("sync", "ve", tv)
                                    dma("sync", kT[hh][:, c0:c0 + 512],
                                        ab("kst", 512), s="st_kv")
                                else:
                                    wait("vector", "st_kv")
                                    tv = op("vector", lambda e, pk=pk: e.tensor_copy(
                                        out=ab("vtm", 512), in_=pk), "ve")
                                    wait("tensor", "ve", tv)
                                    wait("tensor", "trc", semc["trc"])
                                    tp = tposes([ab("vtm", 512)[:, j * 128:(j + 1) * 128]
                                                 for j in range(4)], list(range(4)))
                                    wait("vector", "pe", tp)
                                    tv = op("vector", lambda e: e.tensor_copy(
                                        out=ab("vst", 512), in_=trB[:, 0:512]), "trc")
                                    wait("sync", "trc", tv)
                                    kt0 = c0 // 128
                                    dst = vT[hh, :, kt0:kt0 + 4, :].rearrange("p j c -> p (j c)")
                                    dma("sync", dst, ab("vst", 512), s="st_kv")
            barrier()

        if "B" in phases:
            LB = B_LIMIT or (2, 8, 4, 8)
            arn_cnt = {0: 0, 1: 0, 2: 0, 3: 0}
            pbg = {0: 0, 1: 0}
            ptcg = {0: 0, 1: 0}
            rrctr = [0]
            qb_guard = [0, 0]
            for half in (LB[0] if isinstance(LB[0], (list, tuple)) else range(LB[0])):
                wait("sync", "pe", semc["pe"])
                dma("sync", ab("lnT", 32768), lnTs[half], s="ld_l")
                for h in range(LB[1]):
                    wait("sync", "pe", semc["pe"])
                    dma("sync", ab("khb", 4096), kT[h], s="ld_k")
                    dma("sync", ab("vhb", 4096), vT[h].rearrange("p k c -> p (k c)"), s="ld_v")
                    for g in range(LB[2]):
                        qh = h * 4 + g
                        qb = ab("qs0", 1024) if qh % 2 == 0 else ab("qs1", 1024)
                        wait("sync", "gp", semc["gp"])
                        dma("sync", af("stB", 4096), wq_t[qh].rearrange("p a n -> p (a n)"), s="ld_w")
                        wait("gpsimd", "ld_w")
                        wait("gpsimd", "pe", semc["pe"])
                        tg = op("gpsimd", lambda e: e.tensor_copy(out=ab("wsbB", 4096),
                                                                  in_=af("stB", 4096)), "gp")
                        wsB = ab("wsbB", 4096).rearrange("p (a n) -> p a n", n=128)
                        wait("tensor", "gp", tg)
                        wait("tensor", "ld_l")
                        wait("tensor", "pe", qb_guard[qh % 2])
                        wait("tensor", "ve", semc["ve"])
                        for tgrp in range(2):
                            pk = qpP[:, tgrp * 512:(tgrp + 1) * 512]
                            for ch in range(NCH):
                                tpk = mm(pk, wsB[:, ch, :],
                                         lnT3[:, ch, tgrp * 512:(tgrp + 1) * 512],
                                         ch == 0, ch == NCH - 1)
                            wait("vector", "pe", tpk)
                            op("vector", lambda e, pk=pk, qb=qb, tgrp=tgrp: e.tensor_scalar_mul(
                                out=qb[:, tgrp * 512:(tgrp + 1) * 512], in0=pk, scalar1=SCALE), "ve")
                        tqev = semc["ve"]
                        wait("tensor", "ve", tqev)
                        wait("tensor", "ld_k")
                        wait("tensor", "ld_v")
                        def softmax_stage(tloc, t, par):
                            fks = [("L", j) for j in range(t + 1)] + [("P", j) for j in range(t + 1)]
                            nfk = len(fks)
                            nr = (nfk + 3) // 4
                            pb = ab("pbuf%d" % par, 4096)
                            mb = par * 8
                            for r in range(nr):
                                sub = fks[r * 4:(r + 1) * 4]
                                hb = rrctr[0] % 4
                                rrctr[0] += 1
                                abh = arena[:, hb * 512:hb * 512 + 512]
                                wait("tensor", "ac", arn_cnt[hb])
                                kk = 0
                                while kk < len(sub):
                                    side, j0 = sub[kk]
                                    run = 1
                                    while (kk + run < len(sub) and sub[kk + run][0] == side
                                           and sub[kk + run][1] == j0 + run):
                                        run += 1
                                    base = (0 if side == "L" else 2048) + j0 * 128
                                    tpk = mm(abh[:, kk * 128:(kk + run) * 128],
                                             qb[:, tloc * 128:(tloc + 1) * 128],
                                             ab("khb", 4096)[:, base:base + run * 128],
                                             True, True)
                                    kk += run
                                wait("vector", "pe", tpk)
                                for kk2, (side, j) in enumerate(sub):
                                    if j == t:
                                        trib = tri1b if side == "L" else tri2b
                                        op("vector", lambda e, kk2=kk2, abh=abh, trib=trib: e.tensor_add(
                                            out=abh[:, kk2 * 128:(kk2 + 1) * 128],
                                            in0=abh[:, kk2 * 128:(kk2 + 1) * 128], in1=trib[:]))
                                used = len(sub) * 128
                                tv = op("vector", lambda e, abh=abh, used=used, r=r, mb=mb: e.reduce_max(
                                    out=mxb[:, mb + r:mb + r + 1], in_=abh[:, 0:used], axis=AX.X,
                                    negate=True))
                                wait("scalar", "ve", tv)
                                if r == 0:
                                    wait("scalar", "pe", pbg[par])
                                wait("scalar", "ac", semc["ac"])
                                semc["ac"] += 1
                                arn_cnt[hb] = semc["ac"]
                                def fe(eng, abh=abh, used=used, r=r, mb=mb, pb=pb, h2=semh["ac"]):
                                    eng.activation(out=pb[:, r * 512:r * 512 + used],
                                                   in_=abh[:, 0:used], func=AF.Exp,
                                                   bias=mxb[:, mb + r:mb + r + 1],
                                                   accum_out=sab[:, mb + r:mb + r + 1]).then_inc(h2, 1)
                                q["scalar"].append(fe)
                            wait("vector", "ac", semc["ac"])
                            op("vector", lambda e, nr=nr, mb=mb: e.tensor_scalar_mul(
                                out=fcb[:, mb:mb + nr], in0=mxb[:, mb:mb + nr], scalar1=-1.0))
                            tv = op("vector", lambda e, nr=nr, mb=mb: e.reduce_max(
                                out=stat[:, 8:9], in_=fcb[:, mb:mb + nr], axis=AX.X, negate=True))
                            wait("scalar", "ve", tv)
                            ts = op("scalar", lambda e, nr=nr, mb=mb: e.activation(
                                out=fcb[:, mb:mb + nr], in_=fcb[:, mb:mb + nr], func=AF.Exp,
                                bias=stat[:, 8:9]))
                            wait("vector", "ac", ts)
                            for r in range(nr):
                                op("vector", lambda e, r=r, mb=mb: e.tensor_scalar_mul(
                                    out=sab[:, mb + r:mb + r + 1], in0=sab[:, mb + r:mb + r + 1],
                                    scalar1=fcb[:, mb + r:mb + r + 1]))
                            op("vector", lambda e, nr=nr, mb=mb: e.reduce_sum(
                                out=stat[:, 9:10], in_=sab[:, mb:mb + nr], axis=AX.X))
                            op("vector", lambda e: e.reciprocal(out=stat[:, 10:11], in_=stat[:, 9:10]))
                            op("vector", lambda e, nr=nr, mb=mb: e.tensor_scalar_mul(
                                out=fcb[:, mb:mb + nr], in0=fcb[:, mb:mb + nr], scalar1=stat[:, 10:11]))
                            tnorm = None
                            for r in range(nr):
                                used = min(4, nfk - r * 4) * 128
                                tnorm = op("vector", lambda e, r=r, used=used, pb=pb, mb=mb: e.tensor_scalar_mul(
                                    out=pb[:, r * 512:r * 512 + used],
                                    in0=pb[:, r * 512:r * 512 + used],
                                    scalar1=fcb[:, mb + r:mb + r + 1]), "ve")
                            return (nfk, tnorm)

                        def txav_stage(t, tloc, par, info):
                            nfk, tnorm = info
                            pb = ab("pbuf%d" % par, 4096)
                            ptb = ab("ptb%d" % par, 4096)
                            wait("tensor", "ve", tnorm)
                            wait("tensor", "ptc", ptcg[par])
                            for bt in range((nfk + 7) // 8):
                                wait("tensor", "trc", semc["trc"])
                                cnt = min(8, nfk - bt * 8)
                                tp = tposes([pb[:, (bt * 8 + k) * 128:(bt * 8 + k + 1) * 128]
                                             for k in range(cnt)], list(range(cnt)))
                                wait("vector", "pe", tp)
                                op("vector", lambda e, bt=bt, cnt=cnt, ptb=ptb: e.tensor_copy(
                                    out=ptb[:, bt * 1024:bt * 1024 + cnt * 128],
                                    in_=trB[:, 0:cnt * 128]), "trc")
                            pbg[par] = semc["pe"]
                            wait("tensor", "trc", semc["trc"])
                            fks = [("L", j) for j in range(t + 1)] + [("P", j) for j in range(t + 1)]
                            for i_fk, (side, j) in enumerate(fks):
                                vcol = (j if side == "L" else 16 + j) * 128
                                tav = mm(avP[:, (tloc % 4) * 128:(tloc % 4) * 128 + 128],
                                         ab("vhb", 4096)[:, vcol:vcol + 128],
                                         ptb[:, i_fk * 128:(i_fk + 1) * 128],
                                         i_fk == 0, i_fk == nfk - 1)
                            semc["ptc"] += 1
                            def fpt(eng, h2=semh["ptc"]):
                                eng.nop().then_inc(h2, 1)
                            q["tensor"].append(fpt)
                            ptcg[par] = semc["ptc"]
                            if tloc % 4 == 3:
                                wait("vector", "pe", tav)
                                wait("vector", "st_at")
                                tv = op("vector", lambda e: e.tensor_copy(out=ab("atst", 512),
                                                                          in_=avP[:]), "ve")
                                wait("sync", "ve", tv)
                                Gt = t // 4
                                dst = at3[4 * Gt:4 * Gt + 4, :, qh, :].rearrange("j p q -> p j q")
                                src = ab("atst", 512).rearrange("p (j q) -> p j q", q=128)
                                dma("sync", dst, src, s="st_at")
                                wait("tensor", "ve", tv)

                        pend = None
                        for tloc in range(LB[3]):
                            t = half * 8 + tloc
                            par = tloc % 2
                            info = softmax_stage(tloc, t, par)
                            if pend is not None:
                                txav_stage(*pend)
                            pend = (t, tloc, par, info)
                        if pend is not None:
                            txav_stage(*pend)
                        qb_guard[qh % 2] = semc["pe"]
            barrier()

        if "C" in phases:
            aguard = [0, 0]
            wo3 = ab("wosb", 32768).rearrange("p (a n) -> p a n", n=1024)
            for qq in range(4):
                for piece in range(2):
                    wait("sync", "gp", semc["gp"])
                    dma("sync", af("stC", 16384), wo_t[qq * 2 + piece].rearrange("p a n -> p (a n)"), s="ld_w")
                    wait("gpsimd", "ld_w")
                    wait("gpsimd", "pe", semc["pe"])
                    src = af("stC", 16384).rearrange("p (a n) -> p a n", n=512)
                    dst = wo3[:, :, piece * 512:(piece + 1) * 512]
                    tg = op("gpsimd", lambda e, src=src, dst=dst: e.tensor_copy(out=dst, in_=src), "gp")
                for tt in range(16):
                    bix = tt % 2
                    anames = ("atb", "atb1")
                    astr = ("ld_a", "ld_k")
                    if qq == 0 and tt == 0:
                        dma("sync", ab(anames[0], 4096), at3[0].rearrange("p c q -> p (c q)"), s=astr[0])
                    if tt < 15 or qq < 3:
                        ntt = (tt + 1) % 16
                        nbix = (tt + 1) % 2
                        wait("sync", "pe", aguard[nbix])
                        dma("sync", ab(anames[nbix], 4096),
                            at3[ntt].rearrange("p c q -> p (c q)"), s=astr[nbix])
                    wait("tensor", astr[bix])
                    wait("tensor", "gp", semc["gp"])
                    wait("tensor", "ve", semc["ve"])
                    for dq in range(NCH):
                        for obi in range(2):
                            tpk = mm(arena[:, obi * 512:(obi + 1) * 512],
                                     ab(anames[bix], 4096)[:, dq * 128:(dq + 1) * 128],
                                     wo3[:, dq, obi * 512:(obi + 1) * 512],
                                     dq == 0, dq == NCH - 1)
                    aguard[bix] = semc["pe"]
                    wait("vector", "pe", tpk)
                    wait("vector", "st_y")
                    op("vector", lambda e: e.tensor_copy(out=af("yev", 1024)[:, 0:512],
                                                         in_=arena[:, 0:512]))
                    tv = op("vector", lambda e: e.tensor_copy(out=af("yev", 1024)[:, 512:1024],
                                                              in_=arena[:, 512:1024]), "ve")
                    wait("sync", "ve", tv)
                    dma("sync", y_s[tt][:, qq * 1024:(qq + 1) * 1024],
                        af("yev", 1024), s="st_y")
            barrier()

        if "C" in phases:
            cb3 = centb[:].rearrange("p (a c) -> p a c", c=64)
            x1_store = 0
            lct_store = 0
            for tt in range(16):
                wait("sync", "ve", semc["ve"])
                wait("sync", "ac", semc["ac"])
                dma("sync", af("yfull", 4096), y_s[tt], s="ld_y")
                dma("sync", af("x0t", 4096), x0[tt * 128:(tt + 1) * 128, :], s="ld_x")
                wait("vector", "ld_y")
                CUR_SQ[0] = af("sq2", 4096)
                ln_stats(af("yfull", 4096))
                ta = ln_apply(af("yfull", 4096), af("sq2", 4096))
                wait("vector", "ac", ta)
                wait("vector", "ld_x")
                wait("vector", "st_x1")
                wait("vector", "ac", semc["ac"])
                tv = op("vector", lambda e: e.tensor_add(out=af("x1t", 4096), in0=af("x0t", 4096),
                                                         in1=af("sq2", 4096)), "ve")
                wait("sync", "ve", tv)
                dma("sync", x1o[tt * 128:(tt + 1) * 128, :], af("x1t", 4096), s="st_x1")
                ln_stats(af("x1t", 4096))
                wait("scalar", "pe", semc["pe"])
                ta = ln_apply(af("x1t", 4096), ab("lcC", 4096))
                wait("tensor", "ac", ta)
                wait("tensor", "st_lct")
                for bch in range(4):
                    wait("tensor", "trc", semc["trc"])
                    tp = tposes([ab("lcC", 4096)[:, (bch * 8 + k) * 128:(bch * 8 + k + 1) * 128]
                                 for k in range(8)], list(range(8)))
                    wait("vector", "pe", tp)
                    dst = ab("lcTt", 4096).rearrange("p (c t) -> p c t", t=128)[:, bch * 8:(bch + 1) * 8, :]
                    op("vector", lambda e, dst=dst: e.tensor_copy(out=dst, in_=trB3[:, 0:8, :]), "trc")
                wait("sync", "trc", semc["trc"])
                dma("sync", lcT4[tt].rearrange("p c q -> p (c q)"), ab("lcTt", 4096), s="st_lct")
                wait("tensor", "trc", semc["trc"])
                for ch in range(NCH):
                    tpk = mm(arena[0:64, 1024:1152], cb3[:, ch, :],
                             ab("lcTt", 4096)[:, ch * 128:(ch + 1) * 128],
                             ch == 0, ch == NCH - 1)
                wait("vector", "pe", tpk)
                tv = op("vector", lambda e: e.tensor_copy(out=lgb[0:64, :], in_=arena[0:64, 1024:1152]), "ve")
                wait("tensor", "ve", tv)
                semc["pe"] += 1
                def ftr1(eng, h=semh["pe"]):
                    eng.transpose(out=arena[:, 512:640], in_=lgb[:], identity=idf[:]).then_inc(h, 1)
                q["tensor"].append(ftr1)
                wait("vector", "pe", semc["pe"])
                tv = op("vector", lambda e: e.reduce_max(out=dsm[:, 64:65], in_=arena[:, 512:576],
                                                         axis=AX.X, negate=True), "ve")
                tv = op("vector", lambda e: e.tensor_scalar_mul(out=dsm[:, 65:66], in0=dsm[:, 64:65],
                                                                scalar1=10.0), "ve")
                wait("scalar", "ve", tv)
                ts = op("scalar", lambda e: e.activation(out=dsm[:, 0:64], in_=arena[:, 512:576],
                                                         func=AF.Exp, scale=10.0, bias=dsm[:, 65:66],
                                                         accum_out=dsm[:, 66:67]), "ac")
                wait("vector", "ac", ts)
                op("vector", lambda e: e.reciprocal(out=dsm[:, 67:68], in_=dsm[:, 66:67]))
                op("vector", lambda e: e.tensor_scalar_mul(out=dsm[:, 0:64], in0=dsm[:, 0:64],
                                                       scalar1=dsm[:, 67:68]))
                op("vector", lambda e: e.reduce_max(out=dsm[:, 68:69], in_=dsm[:, 0:64], axis=AX.X))
                tv = op("vector", lambda e, tt=tt: e.tensor_single_scalar(
                    out=qsel_sb[:, tt:tt + 1], in_=dsm[:, 68:69], scalar=0.01, op=ALU.is_gt), "ve")
                wait("tensor", "ve", tv)
                semc["pe"] += 1
                def ftr2(eng, h=semh["pe"]):
                    eng.transpose(out=arena[0:64, 1536:1664], in_=dsm[:, 0:64],
                                  identity=idf[:]).then_inc(h, 1)
                q["tensor"].append(ftr2)
                wait("vector", "pe", semc["pe"])
                op("vector", lambda e: e.reduce_max(out=cmax_sb[:, 1:2],
                                                    in_=arena[0:64, 1536:1664], axis=AX.X))
                op("vector", lambda e: e.tensor_copy(out=cmax_sb[:, 2:3], in_=cmax_sb[:, 0:1]))
                op("vector", lambda e: e.tensor_scalar_max(out=cmax_sb[:, 0:1],
                                                           in0=cmax_sb[:, 1:2],
                                                           scalar1=cmax_sb[:, 2:3]))
            tv = semc["ve"]
            wait("sync", "ve", tv)
            dma("sync", qselo, qsel_sb[:], s="st_o")
            dma("sync", cmaxo, cmax_sb[:, 0:1], s="st_o")
            barrier()

        if "D" in phases:
            dma("sync", af("biasb", 4096), bias_i, s="ld_m")
            dstg_store = 0
            for half in range(2):
                wait("sync", "pe", semc["pe"])
                for tt8 in range(8):
                    dma("sync", ab("lch", 32768)[:, tt8 * 4096:(tt8 + 1) * 4096],
                        lcT4[half * 8 + tt8].rearrange("p c q -> p (c q)"), s="ld_l")
                for ob in range(8):
                    for piece in range(2):
                        wait("sync", "gp", semc["gp"])
                        dma("sync", af("stD", 8192), wT_t[ob * 2 + piece].rearrange("p a n -> p (a n)"), s="ld_w")
                        wait("gpsimd", "ld_w")
                        wait("gpsimd", "pe", semc["pe"])
                        src = af("stD", 8192).rearrange("p (a n) -> p a n", n=256)
                        dst = ab("wTsb", 16384).rearrange("p (a n) -> p a n", n=512)[:, :, piece * 256:(piece + 1) * 256]
                        tg = op("gpsimd", lambda e, src=src, dst=dst: e.tensor_copy(out=dst, in_=src), "gp")
                    wT3 = ab("wTsb", 16384).rearrange("p (a n) -> p a n", n=512)
                    wait("tensor", "gp", semc["gp"])
                    wait("tensor", "ld_l")
                    wait("tensor", "ld_m")
                    for tt8 in range(8):
                        wait("tensor", "ve", semc["ve"])
                        pk = arena[:, (tt8 % 2) * 512:(tt8 % 2) * 512 + 512]
                        for ch in range(NCH):
                            tpk = mm(pk, ab("lch", 32768)[:, (tt8 * 32 + ch) * 128:(tt8 * 32 + ch + 1) * 128],
                                     wT3[:, ch, :], ch == 0, ch == NCH - 1)
                        wait("vector", "pe", tpk)
                        wait("vector", "st_d")
                        tv = op("vector", lambda e, pk=pk, ob=ob: e.tensor_add(
                            out=ab("dstg", 512), in0=pk,
                            in1=af("biasb", 4096)[:, ob * 512:(ob + 1) * 512]), "ve")
                        wait("sync", "ve", tv)
                        r0 = (half * 8 + tt8) * 128
                        dma("sync", denseo[r0:r0 + 128, ob * 512:(ob + 1) * 512],
                            ab("dstg", 512), s="st_d")
            barrier()

        wait_all_dma("sync")

        for e in ENGS:
            getattr(blk, e)(lambda eng, qq=q[e]: [f(eng) for f in qq])

    return nc


def build_launch2():
    nc = bass.Bass("TRN2", target_bir_lowering=False, debug=False)
    x1 = nc.dram_tensor("x1", [TL, H], F32, kind="ExternalInput").ap()
    dense = nc.dram_tensor("dense", [TL, H], BF16, kind="ExternalInput").ap()
    qsel = nc.dram_tensor("qsel", [128, 16], F32, kind="ExternalInput").ap()
    bits = nc.dram_tensor("bits", [64, 8], F32, kind="ExternalInput").ap()
    AT = nc.dram_tensor("AT", [64, H], BF16, kind="ExternalInput").ap()
    ones64 = nc.dram_tensor("ones64", [64, 128], BF16, kind="ExternalInput").ap()
    out = nc.dram_tensor("out", [TL, H], F32, kind="ExternalOutput").ap()

    q = {e: [] for e in ENGS}
    ctx = ExitStack()
    with ctx:
        semh, semc = {}, {}
        for n in ["ld_m", "ld_d0", "ld_d1", "ld_x0", "ld_x1", "st_o", "pe", "ve", "ac"]:
            semh[n] = ctx.enter_context(nc.semaphore(n + "2")); semc[n] = 0
        SBt = lambda n, sh, dt=F32: ctx.enter_context(nc.sbuf_tensor(n, sh, dt))
        bsb = SBt("bsb", [64, 8]); sel = SBt("sel", [64, 2])
        ATs = SBt("ATs", [64, H], BF16); ones_b = SBt("ones_b", [64, 128], BF16)
        rsel = SBt("rsel", [128, H])
        dsb_ = [SBt("dsb0", [128, H], BF16), SBt("dsb1", [128, H], BF16)]
        x1b_ = [SBt("x1b0", [128, H]), SBt("x1b1", [128, H])]
        vb = SBt("vb", [128, H])
        sq = SBt("sq", [128, H]); qsb = SBt("qsb", [128, 16]); st2 = SBt("st2", [128, 16])
        ob_out = SBt("ob_out", [128, H])
        epsb2 = SBt("epsb2", [128, 1])
        pr = ctx.enter_context(nc.psum_tensor("pr2", [128, 1024], F32))
        blk = ctx.enter_context(nc.Block())

        def dma(e, o, i, s="ld"):
            semc[s] += 16
            def f(eng, o=o, i=i, h=semh[s]):
                eng.dma_start(out=o, in_=i).then_inc(h, 16)
            q[e].append(f)
            return semc[s]

        def wait(e, s, v=None):
            if v is None:
                v = semc[s]
            if v <= 0:
                return
            def f(eng, h=semh[s], v=v):
                eng.wait_ge(h, v)
            q[e].append(f)

        OWN2 = {"vector": "ve", "scalar": "ac"}

        def op(e, fn, s=None):
            if e in OWN2:
                own = OWN2[e]
                wait(e, own, semc[own])
                if s is None:
                    s = own
            if s is None:
                def f0(eng, fn=fn):
                    fn(eng)
                q[e].append(f0)
                return None
            semc[s] += 1
            def f(eng, fn=fn, h=semh[s]):
                fn(eng).then_inc(h, 1)
            q[e].append(f)
            return semc[s]

        op("vector", lambda e: e.memset(epsb2[:], EPS))
        dma("sync", bsb[:], bits, s="ld_m")
        dma("sync", ATs[:], AT, s="ld_m")
        dma("sync", ones_b[:], ones64, s="ld_m")
        dma("sync", qsb[:], qsel, s="ld_m")
        wait("vector", "ld_m")
        op("vector", lambda e: e.reduce_max(out=sel[:, 0:1], in_=bsb[:], axis=AX.X))
        op("vector", lambda e: e.tensor_single_scalar(out=sel[:, 1:2], in_=sel[:, 0:1],
                                                      scalar=0.01, op=ALU.is_gt))
        tv = op("vector", lambda e: e.tensor_scalar_mul(out=ATs[:], in0=ATs[:], scalar1=sel[:, 1:2]), "ve")
        wait("tensor", "ve", tv)
        copy_t = [0, 0]
        for ob in range(8):
            wait("tensor", "ve", copy_t[ob % 2])
            tp = op("tensor", lambda e, ob=ob: e.matmul(
                out=pr[:, (ob % 2) * 512:(ob % 2) * 512 + 512], lhsT=ones_b[:],
                rhs=ATs[:, ob * 512:(ob + 1) * 512], start=True, stop=True), "pe")
            wait("vector", "pe", tp)
            copy_t[ob % 2] = op("vector", lambda e, ob=ob: e.tensor_copy(
                out=rsel[:, ob * 512:(ob + 1) * 512],
                in_=pr[:, (ob % 2) * 512:(ob % 2) * 512 + 512]), "ve")

        lguard = [(0, 0), (0, 0)]
        dma("sync", dsb_[0][:], dense[0:128, :], s="ld_d0")
        dma("sync", x1b_[0][:], x1[0:128, :], s="ld_x0")
        for tt in range(16):
            bix = tt % 2
            dsb = dsb_[bix]; x1b = x1b_[bix]
            if tt < 15:
                nbix = (tt + 1) % 2
                wait("sync", "ve", lguard[nbix][0])
                wait("sync", "ac", lguard[nbix][1])
                dma("sync", dsb_[nbix][:], dense[(tt + 1) * 128:(tt + 2) * 128, :], s="ld_d%d" % nbix)
                dma("sync", x1b_[nbix][:], x1[(tt + 1) * 128:(tt + 2) * 128, :], s="ld_x%d" % nbix)
            wait("vector", "ld_d%d" % bix)
            op("vector", lambda e, tt=tt, dsb=dsb: e.tensor_scalar_mul(out=vb[:], in0=dsb[:],
                                                          scalar1=qsb[:, tt:tt + 1]))
            op("vector", lambda e: e.tensor_mul(out=vb[:], in0=vb[:], in1=rsel[:]))
            op("vector", lambda e: e.reduce_sum(out=st2[:, 0:1], in_=vb[:], axis=AX.X))
            op("vector", lambda e: e.scalar_tensor_tensor(out=sq[:], in0=vb[:], scalar=1.0,
                                                          in1=vb[:], op0=ALU.mult, op1=ALU.mult,
                                                          accum_out=st2[:, 1:2]))
            op("vector", lambda e: e.tensor_scalar_mul(out=st2[:, 2:3], in0=st2[:, 0:1],
                                                       scalar1=1.0 / H))
            op("vector", lambda e: e.tensor_copy(out=st2[:, 9:10], in_=st2[:, 2:3]))
            op("vector", lambda e: e.tensor_scalar_mul(out=st2[:, 3:4], in0=st2[:, 2:3],
                                                       scalar1=st2[:, 9:10]))
            op("vector", lambda e: e.tensor_scalar_mul(out=st2[:, 8:9], in0=st2[:, 1:2],
                                                       scalar1=1.0 / H))
            tv = op("vector", lambda e: e.tensor_single_scalar(
                out=st2[:, 4:5], in_=st2[:, 8:9], scalar=st2[:, 3:4], op=ALU.subtract), "ve")
            wait("scalar", "ve", tv)
            ts = op("scalar", lambda e: e.activation(out=st2[:, 5:6], in_=st2[:, 4:5],
                                                     func=AF.Sqrt, bias=epsb2[:]), "ac")
            wait("vector", "ac", ts)
            op("vector", lambda e: e.reciprocal(out=st2[:, 6:7], in_=st2[:, 5:6]))
            op("vector", lambda e: e.tensor_scalar_mul(out=st2[:, 10:11], in0=st2[:, 2:3],
                                                       scalar1=-1.0))
            tv = op("vector", lambda e: e.tensor_scalar_mul(out=st2[:, 7:8], in0=st2[:, 10:11],
                                                            scalar1=st2[:, 6:7]), "ve")
            wait("scalar", "ve", tv)
            ts = op("scalar", lambda e: e.activation(out=sq[:], in_=vb[:], func=AF.Identity,
                                                     scale=st2[:, 6:7], bias=st2[:, 7:8]), "ac")
            wait("vector", "ac", ts)
            wait("vector", "ld_x%d" % bix)
            wait("vector", "st_o")
            tv = op("vector", lambda e, x1b=x1b: e.tensor_add(out=ob_out[:], in0=x1b[:], in1=sq[:]), "ve")
            lguard[bix] = (semc["ve"], semc["ac"])
            wait("sync", "ve", tv)
            dma("sync", out[tt * 128:(tt + 1) * 128, :], ob_out[:], s="st_o")
        wait("sync", "st_o")

        for e in ENGS:
            getattr(blk, e)(lambda eng, qq=q[e]: [f(eng) for f in qq])
    return nc


def prep_shared(inputs):
    f4 = np.float32
    wq = np.ascontiguousarray(inputs["L0_attn_w_q"], f4)
    wk = np.ascontiguousarray(inputs["L0_attn_w_k"], f4)
    wv = np.ascontiguousarray(inputs["L0_attn_w_v"], f4)
    wo = np.ascontiguousarray(inputs["L0_attn_w_o"], f4)
    W = np.ascontiguousarray(inputs["L0_ffn_weight"], f4)
    bias = np.asarray(inputs["L0_ffn_bias"], f4)
    cent = np.ascontiguousarray(inputs["L0_ffn_centroids"], f4)
    assign = np.asarray(inputs["L0_ffn_assignments"])

    d = {}
    d["wq_t"] = np.ascontiguousarray(wq.reshape(32, 128, 32, 128).transpose(2, 1, 0, 3))
    wk_t = wk.reshape(32, 128, 4, 256).transpose(2, 1, 0, 3)
    wv_t = wv.reshape(32, 128, 4, 256).transpose(2, 1, 0, 3)
    d["wkv_t"] = np.ascontiguousarray(np.concatenate([wk_t, wv_t], axis=0))
    d["wo_t"] = np.ascontiguousarray(wo.reshape(32, 128, 8, 512).transpose(2, 1, 0, 3))
    wT = np.ascontiguousarray(W.T)
    d["wT_t"] = np.ascontiguousarray(wT.reshape(32, 128, 16, 256).transpose(2, 1, 0, 3))
    d["bias_i"] = np.ascontiguousarray(np.broadcast_to(bias[None, :], (128, H)))
    d["centT"] = np.ascontiguousarray(cent.T)
    d["identb"] = np.eye(128, dtype=ml_dtypes.bfloat16)
    d["identf"] = np.eye(128, dtype=np.float32)
    jj = np.arange(128)[None, :]
    pp = np.arange(128)[:, None]
    d["tri1"] = np.where(jj <= pp, 0.0, -1e9).astype(f4)
    tri2_p0 = np.where(jj <= pp - 1, 0.0, -1e9).astype(f4)
    d["tri2_p0"] = tri2_p0
    d["tri2_p1"] = d["tri1"]
    AT = np.zeros((64, H), np.float32)
    AT[np.asarray(assign, np.int64), np.arange(H)] = 1.0
    d["AT"] = AT.astype(ml_dtypes.bfloat16)
    d["ones64"] = np.ones((64, 128), dtype=ml_dtypes.bfloat16)
    return d


def core_inputs1(inputs, shared, c):
    b, p = c // 2, c % 2
    x = np.asarray(inputs["L0_attn_hidden_states"], np.float32)
    x0p = np.ascontiguousarray(np.concatenate([x[b, p::2], x[b, 1 - p::2]], axis=0))
    return {
        "x0": x0p, "wq_t": shared["wq_t"], "wkv_t": shared["wkv_t"],
        "wo_t": shared["wo_t"], "wT_t": shared["wT_t"], "bias_i": shared["bias_i"],
        "centT": shared["centT"], "tri1": shared["tri1"],
        "tri2": shared["tri2_p0"] if p == 0 else shared["tri2_p1"],
        "identb": shared["identb"], "identf": shared["identf"],
    }


def kernel(**inputs):
    shared = prep_shared(inputs)
    nc1 = build_launch1()
    in_maps1 = [core_inputs1(inputs, shared, c) for c in range(8)]
    res1 = run_bass_kernel_spmd(nc1, in_maps1, core_ids=list(range(8)))
    bits = np.stack([np.asarray(res1.results[c]["cmax"][:, 0], np.float32)
                     for c in range(8)], axis=1)
    bits = np.ascontiguousarray(bits)
    nc2 = build_launch2()
    in_maps2 = []
    for c in range(8):
        r = res1.results[c]
        in_maps2.append({
            "x1": np.asarray(r["x1"], np.float32),
            "dense": np.asarray(r["dense"]).view(ml_dtypes.bfloat16)
                     if r["dense"].dtype != ml_dtypes.bfloat16 else r["dense"],
            "qsel": np.asarray(r["qsel"], np.float32),
            "bits": bits, "AT": shared["AT"], "ones64": shared["ones64"],
        })
    res2 = run_bass_kernel_spmd(nc2, in_maps2, core_ids=list(range(8)))
    out = np.empty((B_, S_, H), np.float32)
    for c in range(8):
        b, p = c // 2, c % 2
        x2 = np.asarray(res2.results[c]["out"], np.float32)
        out[b, p::2, :] = x2
    return out


# version 05
# speedup vs baseline: 1.2005x; 1.1252x over previous
import numpy as np
import ml_dtypes
from contextlib import ExitStack
import concourse.bass as bass
import concourse.mybir as mybir
from concourse.bass_utils import run_bass_kernel_spmd

F32 = mybir.dt.float32
BF16 = mybir.dt.bfloat16
AF = mybir.ActivationFunctionType
ALU = mybir.AluOpType
AX = mybir.AxisListType

B_, S_, H = 4, 4096, 4096
NKV, D = 8, 128
TL, TF = 2048, 4096
NCH = H // 128
SCALE = float(1.0 / np.sqrt(128.0))
EPS = 1e-5

ENGS = ("sync", "tensor", "vector", "scalar", "gpsimd")
B_LIMIT = None
PHASE_MARKS = []
B_STAGE = 6

AB_CAP = 70144
ABO = dict(lnT=0, wsbA=32768, lnb=40960, kst=45056, vtm=45568, vst=46080,
           wsbB=32768, wsbB1=66048, khb=36864, vhb=40960, qs0=45056, qs1=47104,
           pbuf0=49152, pbuf1=53248, ptb0=57344, ptb1=61440, atst=65536,
           wosb=0, atb=32768, atb1=36864,
           lcC=40960, lcTt=36864,
           lch=0, wTsb=32768, wTsb1=49664, dstg=49152, biasb2=66048)
AF_CAP = 17408
AFO = dict(stA=0, xbuf0=8192, xbuf1=12288,
           stB=0, stB1=4096,
           stC=0, yev=16384,
           yfull=0, x0t=4096, x1t=8192, sq2=12288,
           stD0=0, stD1=8192, biasb=8192)


def build_launch1(dbg=(), phases="ABCD"):
    nc = bass.Bass("TRN2", target_bir_lowering=False, debug=False)

    def inp(name, shape, dt=F32):
        return nc.dram_tensor(name, shape, dt, kind="ExternalInput").ap()

    def outp(name, shape, dt=F32):
        return nc.dram_tensor(name, shape, dt, kind="ExternalOutput").ap()

    def scr(name, shape, dt):
        kind = "ExternalOutput" if name in dbg else "Internal"
        return nc.dram_tensor(name, shape, dt, kind=kind).ap()

    x0 = inp("x0", [TF, H])
    wq_t = inp("wq_t", [32, 128, NCH, 128])
    wkv_t = inp("wkv_t", [8, 128, NCH, 256])
    wo_t = inp("wo_t", [8, 128, NCH, 512])
    wT_t = inp("wT_t", [16, 128, NCH, 256])
    bias_i = inp("bias_i", [128, H])
    centT = inp("centT", [H, 64])
    tri1 = inp("tri1", [128, 128], BF16); tri2 = inp("tri2", [128, 128], BF16)
    identb = inp("identb", [128, 128], BF16)
    identf = inp("identf", [128, 128], F32)

    kT = scr("kT", [NKV, 128, TF], BF16)
    vT = scr("vT", [NKV, 128, 32, 128], BF16)
    lnTs = scr("lnTs", [2, 128, NCH * 1024], BF16)
    at3 = scr("at3", [16, 128, NCH, 128], BF16)
    y_s = scr("y_s", [16, 128, H], F32)
    lcT4 = scr("lcT4", [16, 128, NCH, 128], BF16)

    x1o = outp("x1", [TL, H], F32)
    denseo = outp("dense", [TL, H], BF16)
    qselo = outp("qsel", [128, 16], F32)
    cmaxo = outp("cmax", [64, 1], F32)

    q = {e: [] for e in ENGS}
    pbuf_guard = [0]
    at_store = [0]
    yev_store = [0]
    ctx = ExitStack()
    with ctx:
        semh, semc = {}, {}
        for n in ["ld_x", "ld_w", "ld_w2", "ld_k", "ld_v", "ld_l", "ld_a", "ld_y", "ld_m",
                  "st_kv", "st_ln", "st_at", "st_y", "st_x1", "st_lct", "st_d", "st_o",
                  "pe", "ve", "ac", "gp", "bar", "arn0", "arn1", "arn2", "arn3", "trc", "ptc"]:
            semh[n] = ctx.enter_context(nc.semaphore(n)); semc[n] = 0

        AB = ctx.enter_context(nc.sbuf_tensor("AB", [128, AB_CAP], BF16))
        AFm = ctx.enter_context(nc.sbuf_tensor("AFm", [128, AF_CAP], F32))
        tri1b = ctx.enter_context(nc.sbuf_tensor("tri1b", [128, 128], BF16))
        tri2b = ctx.enter_context(nc.sbuf_tensor("tri2b", [128, 128], BF16))
        idb = ctx.enter_context(nc.sbuf_tensor("idb", [128, 128], BF16))
        idf = ctx.enter_context(nc.sbuf_tensor("idf", [128, 128], F32))
        stat = ctx.enter_context(nc.sbuf_tensor("stat", [128, 16], F32))
        mxb = ctx.enter_context(nc.sbuf_tensor("mxb", [128, 16], F32))
        sab = ctx.enter_context(nc.sbuf_tensor("sab", [128, 16], F32))
        fcb = ctx.enter_context(nc.sbuf_tensor("fcb", [128, 16], F32))
        qsel_sb = ctx.enter_context(nc.sbuf_tensor("qsel_sb", [128, 16], F32))
        cmax_sb = ctx.enter_context(nc.sbuf_tensor("cmax_sb", [64, 4], F32))
        dsm = ctx.enter_context(nc.sbuf_tensor("dsm", [128, 80], F32))
        epsb = ctx.enter_context(nc.sbuf_tensor("epsb", [128, 1], F32))
        lgb = ctx.enter_context(nc.sbuf_tensor("lgb", [128, 128], F32))

        arena = ctx.enter_context(nc.psum_tensor("arena", [128, 2048], F32))
        trB = ctx.enter_context(nc.psum_tensor("trBp", [128, 1024], BF16))
        avP = ctx.enter_context(nc.psum_tensor("avPp", [128, 512], F32))
        qpP = ctx.enter_context(nc.psum_tensor("qpPp", [128, 1024], F32))

        blk = ctx.enter_context(nc.Block())

        def ab(name, n):
            return AB[:, ABO[name]:ABO[name] + n]

        def af(name, n):
            return AFm[:, AFO[name]:AFO[name] + n]

        def dma(e, out, in_, s="ld", **kw):
            semc[s] += 16
            h, tgt = semh[s], semc[s]
            def f(eng, out=out, in_=in_, h=h, kw=kw):
                eng.dma_start(out=out, in_=in_, **kw).then_inc(h, 16)
            q[e].append(f)
            return tgt

        def wait(e, s, v=None):
            if v is None:
                v = semc[s]
            if v <= 0:
                return
            def f(eng, h=semh[s], v=v):
                eng.wait_ge(h, v)
            q[e].append(f)

        def wait_all_dma(e):
            for s2 in ("ld_x", "ld_w", "ld_w2", "ld_k", "ld_v", "ld_l", "ld_a", "ld_y", "ld_m",
                       "st_kv", "st_ln", "st_at", "st_y", "st_x1", "st_lct", "st_d", "st_o"):
                wait(e, s2)

        OWN = {"vector": "ve", "scalar": "ac", "gpsimd": "gp"}

        def op(e, fn, s=None, serial=True):
            if e in ("vector", "scalar"):
                own = OWN[e]
                if serial:
                    wait(e, own, semc[own])
                if s is None:
                    s = own
            if s is None:
                def f0(eng, fn=fn):
                    fn(eng)
                q[e].append(f0)
                return None
            semc[s] += 1
            h, tgt = semh[s], semc[s]
            def f(eng, fn=fn, h=h):
                fn(eng).then_inc(h, 1)
            q[e].append(f)
            return tgt

        def mm(out, lhsT, rhs, first, last, s="pe"):
            if last:
                semc[s] += 1
                def f(eng, out=out, lhsT=lhsT, rhs=rhs, first=first, h=semh[s]):
                    eng.matmul(out=out, lhsT=lhsT, rhs=rhs, start=first,
                               stop=True).then_inc(h, 1)
            else:
                def f(eng, out=out, lhsT=lhsT, rhs=rhs, first=first):
                    eng.matmul(out=out, lhsT=lhsT, rhs=rhs, start=first, stop=False)
            q["tensor"].append(f)
            return semc[s]

        def tposes(srcs, dsts_in_trB, last_inc=True):
            n = len(srcs)
            for i, src in enumerate(srcs):
                last = (i == n - 1) and last_inc
                if last:
                    semc["pe"] += 1
                    def f(eng, src=src, i=dsts_in_trB[i], h=semh["pe"]):
                        eng.transpose(out=trB[:, i * 128:(i + 1) * 128], in_=src,
                                      identity=idb[:]).then_inc(h, 1)
                else:
                    def f(eng, src=src, i=dsts_in_trB[i]):
                        eng.transpose(out=trB[:, i * 128:(i + 1) * 128], in_=src,
                                      identity=idb[:])
                q["tensor"].append(f)
            return semc["pe"]

        def barrier():
            wait_all_dma("sync")
            for e in ENGS:
                semc["bar"] += 1
                def f(eng, h=semh["bar"]):
                    eng.nop().then_inc(h, 1)
                q[e].append(f)
            v = semc["bar"]
            for e in ENGS:
                wait(e, "bar", v)

        def ln_stats(src):
            sq = CUR_SQ[0]
            op("vector", lambda e, src=src: e.reduce_sum(out=stat[:, 0:1], in_=src, axis=AX.X))
            op("vector", lambda e, src=src, sq=sq: e.scalar_tensor_tensor(
                out=sq, in0=src, scalar=1.0, in1=src, op0=ALU.mult, op1=ALU.mult,
                accum_out=stat[:, 1:2]))
            op("vector", lambda e: e.tensor_scalar_mul(out=stat[:, 2:3], in0=stat[:, 0:1],
                                                       scalar1=1.0 / H))
            op("vector", lambda e: e.tensor_copy(out=stat[:, 12:13], in_=stat[:, 2:3]))
            op("vector", lambda e: e.tensor_scalar_mul(out=stat[:, 3:4], in0=stat[:, 2:3],
                                                       scalar1=stat[:, 12:13]))
            op("vector", lambda e: e.tensor_scalar_mul(out=stat[:, 13:14], in0=stat[:, 1:2],
                                                       scalar1=1.0 / H))
            tv = op("vector", lambda e: e.tensor_single_scalar(
                out=stat[:, 4:5], in_=stat[:, 13:14], scalar=stat[:, 3:4], op=ALU.subtract))
            wait("scalar", "ve", tv)
            ts = op("scalar", lambda e: e.activation(out=stat[:, 5:6], in_=stat[:, 4:5],
                                                     func=AF.Sqrt, bias=epsb[:]))
            wait("vector", "ac", ts)
            op("vector", lambda e: e.reciprocal(out=stat[:, 6:7], in_=stat[:, 5:6]))
            op("vector", lambda e: e.tensor_scalar_mul(out=stat[:, 14:15], in0=stat[:, 2:3],
                                                       scalar1=-1.0))
            tv = op("vector", lambda e: e.tensor_scalar_mul(out=stat[:, 7:8], in0=stat[:, 14:15],
                                                            scalar1=stat[:, 6:7]))
            wait("scalar", "ve", tv)

        def ln_apply(src, dst):
            return op("scalar", lambda e, src=src, dst=dst: e.activation(
                out=dst, in_=src, func=AF.Identity, scale=stat[:, 6:7],
                bias=stat[:, 7:8]))

        CUR_SQ = [ab("lnb", 4096)]

        dma("sync", tri1b[:], tri1, s="ld_m")
        dma("sync", tri2b[:], tri2, s="ld_m")
        dma("sync", idb[:], identb, s="ld_m")
        t0 = dma("sync", idf[:], identf, s="ld_m")
        wait("tensor", "ld_m")
        wait("vector", "ld_m")
        wait("scalar", "ld_m")
        op("vector", lambda e: e.memset(cmax_sb[:, 0:1], -1e30), "ve")
        op("vector", lambda e: e.memset(epsb[:], EPS), "ve")
        op("vector", lambda e: e.memset(qsel_sb[:], 0.0), "ve")

        lnT3 = ab("lnT", 32768).rearrange("p (c t) -> p c t", t=1024)
        trB3 = trB[:].rearrange("p (k q) -> p k q", q=128)

        PHASE_MARKS.append(("preamble_end", {e: len(q[e]) for e in ENGS}))
        if "A" in phases:
            CUR_SQ[0] = ab("lnb", 4096)
            xguard = [(0, 0), (0, 0)]
            xld = ["ld_x", "ld_y"]
            for sset in range(2):
                row0 = TL if sset == 0 else 0
                tok0 = TL if sset == 0 else 0
                for half in range(2):
                    wait("vector", "st_ln")
                    for tloc in range(8):
                        r0 = row0 + half * 1024 + tloc * 128
                        bix = tloc % 2
                        xb = af("xbuf%d" % bix, 4096)
                        wait("sync", "ve", xguard[bix][0])
                        wait("sync", "ac", xguard[bix][1])
                        dma("sync", xb, x0[r0:r0 + 128, :], s=xld[bix])
                        wait("vector", xld[bix])
                        wait("vector", "pe", semc["pe"])
                        ln_stats(xb)
                        wait("scalar", "pe", semc["pe"])
                        ta = ln_apply(xb, ab("lnb", 4096))
                        xguard[bix] = (semc["ve"], semc["ac"])
                        wait("tensor", "ac", ta)
                        for bch in range(4):
                            wait("tensor", "trc", semc["trc"])
                            tp = tposes([ab("lnb", 4096)[:, (bch * 8 + k) * 128:(bch * 8 + k + 1) * 128]
                                         for k in range(8)], list(range(8)))
                            wait("vector", "pe", tp)
                            dst = lnT3[:, bch * 8:(bch + 1) * 8, tloc * 128:(tloc + 1) * 128]
                            op("vector", lambda e, dst=dst: e.tensor_copy(out=dst, in_=trB3[:, 0:8, :]), "trc",
                               serial=False)
                    if sset == 1:
                        wait("sync", "trc", semc["trc"])
                        dma("sync", lnTs[half], ab("lnT", 32768), s="st_ln")
                    for sl in range(8):
                        wait("sync", "gp", semc["gp"])
                        dma("sync", af("stA", 8192), wkv_t[sl].rearrange("p a n -> p (a n)"), s="ld_w")
                        wait("gpsimd", "ld_w")
                        wait("gpsimd", "pe", semc["pe"])
                        tg = op("gpsimd", lambda e: e.tensor_copy(out=ab("wsbA", 8192),
                                                                  in_=af("stA", 8192)), "gp")
                        wsA = ab("wsbA", 8192).rearrange("p (a n) -> p a n", n=256)
                        wait("tensor", "gp", tg)
                        wait("tensor", "trc", semc["trc"])
                        for grp in range(2):
                            for dc in range(2):
                                pk = qpP[:, dc * 512:(dc + 1) * 512]
                                wait("tensor", "ve", semc["ve"])
                                for ch in range(NCH):
                                    tpk = mm(pk, wsA[:, ch, dc * 128:(dc + 1) * 128],
                                             lnT3[:, ch, grp * 512:(grp + 1) * 512],
                                             ch == 0, ch == NCH - 1)
                                wait("vector", "pe", tpk)
                                hh = (sl % 4) * 2 + dc
                                c0 = tok0 + half * 1024 + grp * 512
                                if sl < 4:
                                    wait("vector", "st_kv")
                                    tv = op("vector", lambda e, pk=pk: e.tensor_copy(
                                        out=ab("kst", 512), in_=pk), "ve", serial=False)
                                    wait("sync", "ve", tv)
                                    dma("sync", kT[hh][:, c0:c0 + 512],
                                        ab("kst", 512), s="st_kv")
                                else:
                                    wait("vector", "st_kv")
                                    tv = op("vector", lambda e, pk=pk: e.tensor_copy(
                                        out=ab("vtm", 512), in_=pk), "ve", serial=False)
                                    wait("tensor", "ve", tv)
                                    wait("tensor", "trc", semc["trc"])
                                    tp = tposes([ab("vtm", 512)[:, j * 128:(j + 1) * 128]
                                                 for j in range(4)], list(range(4)))
                                    wait("vector", "pe", tp)
                                    tv = op("vector", lambda e: e.tensor_copy(
                                        out=ab("vst", 512), in_=trB[:, 0:512]), "trc", serial=False)
                                    wait("sync", "trc", tv)
                                    kt0 = c0 // 128
                                    dst = vT[hh, :, kt0:kt0 + 4, :].rearrange("p j c -> p (j c)")
                                    dma("sync", dst, ab("vst", 512), s="st_kv")
            barrier()

        PHASE_MARKS.append(("before_B", {e: len(q[e]) for e in ENGS}))
        if "B" in phases:
            LB = B_LIMIT or (2, 8, 4, 8)
            arn_cnt = {0: 0, 1: 0, 2: 0, 3: 0}
            pbg = {0: 0, 1: 0}
            ptcg = {0: 0, 1: 0}
            rrctr = [0]
            wqgp = {0: 0, 1: 0}
            wqpe = {0: 0, 1: 0}
            qb_guard = [0, 0]
            for half in (LB[0] if isinstance(LB[0], (list, tuple)) else range(LB[0])):
                wait("sync", "pe", semc["pe"])
                dma("sync", ab("lnT", 32768), lnTs[half], s="ld_l")
                for h in range(LB[1]):
                    wait("sync", "pe", semc["pe"])
                    dma("sync", ab("khb", 4096), kT[h], s="ld_k")
                    dma("sync", ab("vhb", 4096), vT[h].rearrange("p k c -> p (k c)"), s="ld_v")
                    for g in range(LB[2]):
                        qh = h * 4 + g
                        qpar = qh % 2
                        qb = ab("qs0", 1024) if qpar == 0 else ab("qs1", 1024)
                        stBn = ("stB", "stB1")[qpar]
                        wsn = ("wsbB", "wsbB1")[qpar]
                        wstr = ("ld_w", "ld_w2")[qpar]
                        if qh == h * 4 and half == 0 and h == 0:
                            dma("sync", af(stBn, 4096), wq_t[qh].rearrange("p a n -> p (a n)"), s=wstr)
                        nqh = qh + 1 if g < LB[2] - 1 else (h + 1) * 4
                        if nqh < 32 or half == 0:
                            nqh2 = nqh % 32
                            npar = nqh2 % 2
                            wait("sync", "gp", wqgp[npar])
                            dma("sync", af(("stB", "stB1")[npar], 4096),
                                wq_t[nqh2].rearrange("p a n -> p (a n)"), s=("ld_w", "ld_w2")[npar])
                        wait("gpsimd", wstr)
                        wait("gpsimd", "pe", wqpe[qpar])
                        tg = op("gpsimd", lambda e, wsn=wsn, stBn=stBn: e.tensor_copy(
                            out=ab(wsn, 4096), in_=af(stBn, 4096)), "gp")
                        wqgp[qpar] = semc["gp"]
                        wsB = ab(wsn, 4096).rearrange("p (a n) -> p a n", n=128)
                        wait("tensor", "gp", tg)
                        wait("tensor", "ld_l")
                        wait("tensor", "pe", qb_guard[qh % 2])
                        wait("tensor", "ve", semc["ve"])
                        for tgrp in range(2):
                            pk = qpP[:, tgrp * 512:(tgrp + 1) * 512]
                            for ch in range(NCH):
                                tpk = mm(pk, wsB[:, ch, :],
                                         lnT3[:, ch, tgrp * 512:(tgrp + 1) * 512],
                                         ch == 0, ch == NCH - 1)
                            wait("vector", "pe", tpk)
                            op("vector", lambda e, pk=pk, qb=qb, tgrp=tgrp: e.tensor_scalar_mul(
                                out=qb[:, tgrp * 512:(tgrp + 1) * 512], in0=pk, scalar1=SCALE), "ve",
                               serial=False)
                        tqev = semc["ve"]
                        wqpe[qpar] = semc["pe"]
                        wait("tensor", "ve", tqev)
                        wait("tensor", "ld_k")
                        wait("tensor", "ld_v")
                        def softmax_stage(tloc, t, par):
                            fks = [("L", j) for j in range(t + 1)] + [("P", j) for j in range(t + 1)]
                            nfk = len(fks)
                            nr = (nfk + 3) // 4
                            pb = ab("pbuf%d" % par, 4096)
                            mb = par * 8
                            for r in range(nr):
                                sub = fks[r * 4:(r + 1) * 4]
                                hb = rrctr[0] % 4
                                rrctr[0] += 1
                                abh = arena[:, hb * 512:hb * 512 + 512]
                                wait("tensor", "ac", arn_cnt[hb])
                                kk = 0
                                while kk < len(sub):
                                    side, j0 = sub[kk]
                                    run = 1
                                    while (kk + run < len(sub) and sub[kk + run][0] == side
                                           and sub[kk + run][1] == j0 + run):
                                        run += 1
                                    base = (0 if side == "L" else 2048) + j0 * 128
                                    tpk = mm(abh[:, kk * 128:(kk + run) * 128],
                                             qb[:, tloc * 128:(tloc + 1) * 128],
                                             ab("khb", 4096)[:, base:base + run * 128],
                                             True, True)
                                    kk += run
                                wait("vector", "pe", tpk)
                                for kk2, (side, j) in enumerate(sub):
                                    if j == t:
                                        trib = tri1b if side == "L" else tri2b
                                        op("vector", lambda e, kk2=kk2, abh=abh, trib=trib: e.tensor_add(
                                            out=abh[:, kk2 * 128:(kk2 + 1) * 128],
                                            in0=abh[:, kk2 * 128:(kk2 + 1) * 128], in1=trib[:]),
                                           serial=False)
                                used = len(sub) * 128
                                tv = op("vector", lambda e, abh=abh, used=used, r=r, mb=mb: e.reduce_max(
                                    out=mxb[:, mb + r:mb + r + 1], in_=abh[:, 0:used], axis=AX.X,
                                    negate=True))
                                wait("scalar", "ve", tv)
                                if r == 0:
                                    wait("scalar", "pe", pbg[par])
                                wait("scalar", "ac", semc["ac"])
                                semc["ac"] += 1
                                arn_cnt[hb] = semc["ac"]
                                def fe(eng, abh=abh, used=used, r=r, mb=mb, pb=pb, h2=semh["ac"]):
                                    eng.activation(out=pb[:, r * 512:r * 512 + used],
                                                   in_=abh[:, 0:used], func=AF.Exp,
                                                   bias=mxb[:, mb + r:mb + r + 1],
                                                   accum_out=sab[:, mb + r:mb + r + 1]).then_inc(h2, 1)
                                q["scalar"].append(fe)
                            wait("vector", "ac", semc["ac"])
                            op("vector", lambda e, nr=nr, mb=mb: e.tensor_scalar_mul(
                                out=fcb[:, mb:mb + nr], in0=mxb[:, mb:mb + nr], scalar1=-1.0))
                            tv = op("vector", lambda e, nr=nr, mb=mb: e.reduce_max(
                                out=stat[:, 8:9], in_=fcb[:, mb:mb + nr], axis=AX.X, negate=True))
                            wait("scalar", "ve", tv)
                            ts = op("scalar", lambda e, nr=nr, mb=mb: e.activation(
                                out=fcb[:, mb:mb + nr], in_=fcb[:, mb:mb + nr], func=AF.Exp,
                                bias=stat[:, 8:9]))
                            wait("vector", "ac", ts)
                            for r in range(nr):
                                op("vector", lambda e, r=r, mb=mb: e.tensor_scalar_mul(
                                    out=sab[:, mb + r:mb + r + 1], in0=sab[:, mb + r:mb + r + 1],
                                    scalar1=fcb[:, mb + r:mb + r + 1]))
                            op("vector", lambda e, nr=nr, mb=mb: e.reduce_sum(
                                out=stat[:, 9:10], in_=sab[:, mb:mb + nr], axis=AX.X))
                            op("vector", lambda e: e.reciprocal(out=stat[:, 10:11], in_=stat[:, 9:10]))
                            op("vector", lambda e, nr=nr, mb=mb: e.tensor_scalar_mul(
                                out=fcb[:, mb:mb + nr], in0=fcb[:, mb:mb + nr], scalar1=stat[:, 10:11]))
                            tnorm = None
                            for r in range(nr):
                                used = min(4, nfk - r * 4) * 128
                                tnorm = op("vector", lambda e, r=r, used=used, pb=pb, mb=mb: e.tensor_scalar_mul(
                                    out=pb[:, r * 512:r * 512 + used],
                                    in0=pb[:, r * 512:r * 512 + used],
                                    scalar1=fcb[:, mb + r:mb + r + 1]), "ve")
                            return (nfk, tnorm)

                        def txav_stage(t, tloc, par, info):
                            nfk, tnorm = info
                            pb = ab("pbuf%d" % par, 4096)
                            ptb = ab("ptb%d" % par, 4096)
                            wait("tensor", "ve", tnorm)
                            wait("tensor", "ptc", ptcg[par])
                            for bt in range((nfk + 7) // 8):
                                wait("tensor", "trc", semc["trc"])
                                cnt = min(8, nfk - bt * 8)
                                tp = tposes([pb[:, (bt * 8 + k) * 128:(bt * 8 + k + 1) * 128]
                                             for k in range(cnt)], list(range(cnt)))
                                wait("vector", "pe", tp)
                                op("vector", lambda e, bt=bt, cnt=cnt, ptb=ptb: e.tensor_copy(
                                    out=ptb[:, bt * 1024:bt * 1024 + cnt * 128],
                                    in_=trB[:, 0:cnt * 128]), "trc", serial=False)
                            pbg[par] = semc["pe"]
                            wait("tensor", "trc", semc["trc"])
                            fks = [("L", j) for j in range(t + 1)] + [("P", j) for j in range(t + 1)]
                            for i_fk, (side, j) in enumerate(fks):
                                vcol = (j if side == "L" else 16 + j) * 128
                                tav = mm(avP[:, (tloc % 4) * 128:(tloc % 4) * 128 + 128],
                                         ab("vhb", 4096)[:, vcol:vcol + 128],
                                         ptb[:, i_fk * 128:(i_fk + 1) * 128],
                                         i_fk == 0, i_fk == nfk - 1)
                            semc["ptc"] += 1
                            def fpt(eng, h2=semh["ptc"]):
                                eng.nop().then_inc(h2, 1)
                            q["tensor"].append(fpt)
                            ptcg[par] = semc["ptc"]
                            if tloc % 4 == 3:
                                wait("vector", "pe", tav)
                                wait("vector", "st_at")
                                tv = op("vector", lambda e: e.tensor_copy(out=ab("atst", 512),
                                                                          in_=avP[:]), "ve", serial=False)
                                wait("sync", "ve", tv)
                                Gt = t // 4
                                dst = at3[4 * Gt:4 * Gt + 4, :, qh, :].rearrange("j p q -> p j q")
                                src = ab("atst", 512).rearrange("p (j q) -> p j q", q=128)
                                dma("sync", dst, src, s="st_at")
                                wait("tensor", "ve", tv)

                        pend = None
                        for tloc in range(LB[3]):
                            t = half * 8 + tloc
                            par = tloc % 2
                            info = softmax_stage(tloc, t, par)
                            if pend is not None:
                                txav_stage(*pend)
                            pend = (t, tloc, par, info)
                        if pend is not None:
                            txav_stage(*pend)
                        qb_guard[qh % 2] = semc["pe"]
            barrier()

        PHASE_MARKS.append(("before_C1", {e: len(q[e]) for e in ENGS}))
        if "C" in phases:
            aguard = [0, 0]
            wo3 = ab("wosb", 32768).rearrange("p (a n) -> p a n", n=1024)
            for qq in range(4):
                for piece in range(2):
                    wait("sync", "gp", semc["gp"])
                    dma("sync", af("stC", 16384), wo_t[qq * 2 + piece].rearrange("p a n -> p (a n)"), s="ld_w")
                    wait("gpsimd", "ld_w")
                    wait("gpsimd", "pe", semc["pe"])
                    src = af("stC", 16384).rearrange("p (a n) -> p a n", n=512)
                    dst = wo3[:, :, piece * 512:(piece + 1) * 512]
                    tg = op("gpsimd", lambda e, src=src, dst=dst: e.tensor_copy(out=dst, in_=src), "gp")
                for tt in range(16):
                    bix = tt % 2
                    anames = ("atb", "atb1")
                    astr = ("ld_a", "ld_k")
                    if qq == 0 and tt == 0:
                        dma("sync", ab(anames[0], 4096), at3[0].rearrange("p c q -> p (c q)"), s=astr[0])
                    if tt < 15 or qq < 3:
                        ntt = (tt + 1) % 16
                        nbix = (tt + 1) % 2
                        wait("sync", "pe", aguard[nbix])
                        dma("sync", ab(anames[nbix], 4096),
                            at3[ntt].rearrange("p c q -> p (c q)"), s=astr[nbix])
                    wait("tensor", astr[bix])
                    wait("tensor", "gp", semc["gp"])
                    wait("tensor", "ve", semc["ve"])
                    for dq in range(NCH):
                        for obi in range(2):
                            tpk = mm(arena[:, obi * 512:(obi + 1) * 512],
                                     ab(anames[bix], 4096)[:, dq * 128:(dq + 1) * 128],
                                     wo3[:, dq, obi * 512:(obi + 1) * 512],
                                     dq == 0, dq == NCH - 1)
                    aguard[bix] = semc["pe"]
                    wait("vector", "pe", tpk)
                    wait("vector", "st_y")
                    op("vector", lambda e: e.tensor_copy(out=af("yev", 1024)[:, 0:512],
                                                         in_=arena[:, 0:512]), serial=False)
                    tv = op("vector", lambda e: e.tensor_copy(out=af("yev", 1024)[:, 512:1024],
                                                              in_=arena[:, 512:1024]), "ve", serial=False)
                    wait("sync", "ve", tv)
                    dma("sync", y_s[tt][:, qq * 1024:(qq + 1) * 1024],
                        af("yev", 1024), s="st_y")
            barrier()

        PHASE_MARKS.append(("before_C2", {e: len(q[e]) for e in ENGS}))
        if "C" in phases:
            centb = AB[:, 0:2048]
            for piece in range(2):
                dma("sync", af("yev", 1024).rearrange("p (a c) -> p a c", c=64),
                    centT.rearrange("(a p) c -> p a c", p=128)[:, piece * 16:(piece + 1) * 16, :],
                    s="ld_m")
                wait("gpsimd", "ld_m")
                tgc = op("gpsimd", lambda e, piece=piece: e.tensor_copy(
                    out=centb[:, piece * 1024:(piece + 1) * 1024], in_=af("yev", 1024)), "gp")
                wait("sync", "gp", tgc)
            cb3 = centb.rearrange("p (a c) -> p a c", c=64)
            x1_store = 0
            lct_store = 0
            for tt in range(16):
                wait("sync", "ve", semc["ve"])
                wait("sync", "ac", semc["ac"])
                dma("sync", af("yfull", 4096), y_s[tt], s="ld_y")
                dma("sync", af("x0t", 4096), x0[tt * 128:(tt + 1) * 128, :], s="ld_x")
                wait("vector", "ld_y")
                CUR_SQ[0] = af("sq2", 4096)
                ln_stats(af("yfull", 4096))
                ta = ln_apply(af("yfull", 4096), af("sq2", 4096))
                wait("vector", "ac", ta)
                wait("vector", "ld_x")
                wait("vector", "st_x1")
                wait("vector", "ac", semc["ac"])
                tv = op("vector", lambda e: e.tensor_add(out=af("x1t", 4096), in0=af("x0t", 4096),
                                                         in1=af("sq2", 4096)), "ve")
                wait("sync", "ve", tv)
                dma("sync", x1o[tt * 128:(tt + 1) * 128, :], af("x1t", 4096), s="st_x1")
                ln_stats(af("x1t", 4096))
                wait("scalar", "pe", semc["pe"])
                ta = ln_apply(af("x1t", 4096), ab("lcC", 4096))
                wait("tensor", "ac", ta)
                wait("tensor", "st_lct")
                for bch in range(4):
                    wait("tensor", "trc", semc["trc"])
                    tp = tposes([ab("lcC", 4096)[:, (bch * 8 + k) * 128:(bch * 8 + k + 1) * 128]
                                 for k in range(8)], list(range(8)))
                    wait("vector", "pe", tp)
                    dst = ab("lcTt", 4096).rearrange("p (c t) -> p c t", t=128)[:, bch * 8:(bch + 1) * 8, :]
                    op("vector", lambda e, dst=dst: e.tensor_copy(out=dst, in_=trB3[:, 0:8, :]), "trc")
                wait("sync", "trc", semc["trc"])
                dma("sync", lcT4[tt].rearrange("p c q -> p (c q)"), ab("lcTt", 4096), s="st_lct")
                wait("tensor", "trc", semc["trc"])
                for ch in range(NCH):
                    tpk = mm(arena[0:64, 1024:1152], cb3[:, ch, :],
                             ab("lcTt", 4096)[:, ch * 128:(ch + 1) * 128],
                             ch == 0, ch == NCH - 1)
                wait("vector", "pe", tpk)
                tv = op("vector", lambda e: e.tensor_copy(out=lgb[0:64, :], in_=arena[0:64, 1024:1152]), "ve")
                wait("tensor", "ve", tv)
                semc["pe"] += 1
                def ftr1(eng, h=semh["pe"]):
                    eng.transpose(out=arena[:, 512:640], in_=lgb[:], identity=idf[:]).then_inc(h, 1)
                q["tensor"].append(ftr1)
                wait("vector", "pe", semc["pe"])
                tv = op("vector", lambda e: e.reduce_max(out=dsm[:, 64:65], in_=arena[:, 512:576],
                                                         axis=AX.X, negate=True), "ve")
                tv = op("vector", lambda e: e.tensor_scalar_mul(out=dsm[:, 65:66], in0=dsm[:, 64:65],
                                                                scalar1=10.0), "ve")
                wait("scalar", "ve", tv)
                ts = op("scalar", lambda e: e.activation(out=dsm[:, 0:64], in_=arena[:, 512:576],
                                                         func=AF.Exp, scale=10.0, bias=dsm[:, 65:66],
                                                         accum_out=dsm[:, 66:67]), "ac")
                wait("vector", "ac", ts)
                op("vector", lambda e: e.reciprocal(out=dsm[:, 67:68], in_=dsm[:, 66:67]))
                op("vector", lambda e: e.tensor_scalar_mul(out=dsm[:, 0:64], in0=dsm[:, 0:64],
                                                       scalar1=dsm[:, 67:68]))
                op("vector", lambda e: e.reduce_max(out=dsm[:, 68:69], in_=dsm[:, 0:64], axis=AX.X))
                tv = op("vector", lambda e, tt=tt: e.tensor_single_scalar(
                    out=qsel_sb[:, tt:tt + 1], in_=dsm[:, 68:69], scalar=0.01, op=ALU.is_gt), "ve")
                wait("tensor", "ve", tv)
                semc["pe"] += 1
                def ftr2(eng, h=semh["pe"]):
                    eng.transpose(out=arena[0:64, 1536:1664], in_=dsm[:, 0:64],
                                  identity=idf[:]).then_inc(h, 1)
                q["tensor"].append(ftr2)
                wait("vector", "pe", semc["pe"])
                op("vector", lambda e: e.reduce_max(out=cmax_sb[:, 1:2],
                                                    in_=arena[0:64, 1536:1664], axis=AX.X))
                op("vector", lambda e: e.tensor_copy(out=cmax_sb[:, 2:3], in_=cmax_sb[:, 0:1]))
                op("vector", lambda e: e.tensor_scalar_max(out=cmax_sb[:, 0:1],
                                                           in0=cmax_sb[:, 1:2],
                                                           scalar1=cmax_sb[:, 2:3]))
            tv = semc["ve"]
            wait("sync", "ve", tv)
            dma("sync", qselo, qsel_sb[:], s="st_o")
            dma("sync", cmaxo, cmax_sb[:, 0:1], s="st_o")
            barrier()

        PHASE_MARKS.append(("before_D", {e: len(q[e]) for e in ENGS}))
        if "D" in phases:
            dma("sync", af("stD0", 4096), bias_i, s="ld_m")
            wait("gpsimd", "ld_m")
            tgb = op("gpsimd", lambda e: e.tensor_copy(out=ab("biasb2", 4096), in_=af("stD0", 4096)), "gp")
            wait("sync", "gp", tgb)
            dguard = [0, 0]
            fillgp = [0, 0]
            for half in range(2):
                wait("sync", "pe", semc["pe"])
                for tt8 in range(8):
                    dma("sync", ab("lch", 32768)[:, tt8 * 4096:(tt8 + 1) * 4096],
                        lcT4[half * 8 + tt8].rearrange("p c q -> p (c q)"), s="ld_l")
                for ob in range(8):
                    gob = half * 8 + ob
                    wslot = gob % 2
                    wsn = ("wTsb", "wTsb1")[wslot]
                    if gob == 0:
                        for piece in range(2):
                            stn = ("stD0", "stD1")[piece]
                            dma("sync", af(stn, 8192),
                                wT_t[piece].rearrange("p a n -> p (a n)"), s=("ld_a", "ld_k")[piece])
                            wait("gpsimd", ("ld_a", "ld_k")[piece])
                            src = af(stn, 8192).rearrange("p (a n) -> p a n", n=256)
                            dst = ab(wsn, 16384).rearrange("p (a n) -> p a n", n=512)[:, :, piece * 256:(piece + 1) * 256]
                            op("gpsimd", lambda e, src=src, dst=dst: e.tensor_copy(out=dst, in_=src), "gp")
                        fillgp[0] = semc["gp"]
                    ngob = gob + 1
                    if ngob < 16:
                        nob = ngob % 8
                        nslot = ngob % 2
                        nwsn = ("wTsb", "wTsb1")[nslot]
                        for piece in range(2):
                            stn = ("stD0", "stD1")[piece]
                            wait("sync", "gp", semc["gp"])
                            dma("sync", af(stn, 8192),
                                wT_t[nob * 2 + piece].rearrange("p a n -> p (a n)"), s=("ld_a", "ld_k")[piece])
                            wait("gpsimd", ("ld_a", "ld_k")[piece])
                            wait("gpsimd", "pe", dguard[nslot])
                            src = af(stn, 8192).rearrange("p (a n) -> p a n", n=256)
                            dst = ab(nwsn, 16384).rearrange("p (a n) -> p a n", n=512)[:, :, piece * 256:(piece + 1) * 256]
                            op("gpsimd", lambda e, src=src, dst=dst: e.tensor_copy(out=dst, in_=src), "gp")
                        fillgp[nslot] = semc["gp"]
                    wT3 = ab(wsn, 16384).rearrange("p (a n) -> p a n", n=512)
                    wait("tensor", "gp", fillgp[wslot])
                    wait("tensor", "ld_l")
                    wait("tensor", "ld_m")
                    for tt8 in range(8):
                        wait("tensor", "ve", semc["ve"])
                        pk = arena[:, (tt8 % 2) * 512:(tt8 % 2) * 512 + 512]
                        for ch in range(NCH):
                            tpk = mm(pk, ab("lch", 32768)[:, (tt8 * 32 + ch) * 128:(tt8 * 32 + ch + 1) * 128],
                                     wT3[:, ch, :], ch == 0, ch == NCH - 1)
                        wait("vector", "pe", tpk)
                        wait("vector", "st_d")
                        tv = op("vector", lambda e, pk=pk, ob=ob: e.tensor_add(
                            out=ab("dstg", 512), in0=pk,
                            in1=ab("biasb2", 4096)[:, ob * 512:(ob + 1) * 512]), "ve", serial=False)
                        wait("sync", "ve", tv)
                        r0 = (half * 8 + tt8) * 128
                        dma("sync", denseo[r0:r0 + 128, ob * 512:(ob + 1) * 512],
                            ab("dstg", 512), s="st_d")
                    dguard[wslot] = semc["pe"]
            barrier()

        wait_all_dma("sync")

        for e in ENGS:
            getattr(blk, e)(lambda eng, qq=q[e]: [f(eng) for f in qq])

    return nc


def build_launch2():
    nc = bass.Bass("TRN2", target_bir_lowering=False, debug=False)
    x1 = nc.dram_tensor("x1", [TL, H], F32, kind="ExternalInput").ap()
    dense = nc.dram_tensor("dense", [TL, H], BF16, kind="ExternalInput").ap()
    qsel = nc.dram_tensor("qsel", [128, 16], F32, kind="ExternalInput").ap()
    bits = nc.dram_tensor("bits", [64, 8], F32, kind="ExternalInput").ap()
    AT = nc.dram_tensor("AT", [64, H], BF16, kind="ExternalInput").ap()
    ones64 = nc.dram_tensor("ones64", [64, 128], BF16, kind="ExternalInput").ap()
    out = nc.dram_tensor("out", [TL, H], F32, kind="ExternalOutput").ap()

    q = {e: [] for e in ENGS}
    ctx = ExitStack()
    with ctx:
        semh, semc = {}, {}
        for n in ["ld_m", "ld_d0", "ld_d1", "ld_x0", "ld_x1", "st_o", "pe", "ve", "ac"]:
            semh[n] = ctx.enter_context(nc.semaphore(n + "2")); semc[n] = 0
        SBt = lambda n, sh, dt=F32: ctx.enter_context(nc.sbuf_tensor(n, sh, dt))
        bsb = SBt("bsb", [64, 8]); sel = SBt("sel", [64, 2])
        ATs = SBt("ATs", [64, H], BF16); ones_b = SBt("ones_b", [64, 128], BF16)
        rsel = SBt("rsel", [128, H])
        dsb_ = [SBt("dsb0", [128, H], BF16), SBt("dsb1", [128, H], BF16)]
        x1b_ = [SBt("x1b0", [128, H]), SBt("x1b1", [128, H])]
        vb = SBt("vb", [128, H])
        sq = SBt("sq", [128, H]); qsb = SBt("qsb", [128, 16]); st2 = SBt("st2", [128, 16])
        ob_out = SBt("ob_out", [128, H])
        epsb2 = SBt("epsb2", [128, 1])
        pr = ctx.enter_context(nc.psum_tensor("pr2", [128, 1024], F32))
        blk = ctx.enter_context(nc.Block())

        def dma(e, o, i, s="ld"):
            semc[s] += 16
            def f(eng, o=o, i=i, h=semh[s]):
                eng.dma_start(out=o, in_=i).then_inc(h, 16)
            q[e].append(f)
            return semc[s]

        def wait(e, s, v=None):
            if v is None:
                v = semc[s]
            if v <= 0:
                return
            def f(eng, h=semh[s], v=v):
                eng.wait_ge(h, v)
            q[e].append(f)

        OWN2 = {"vector": "ve", "scalar": "ac"}

        def op(e, fn, s=None):
            if e in OWN2:
                own = OWN2[e]
                wait(e, own, semc[own])
                if s is None:
                    s = own
            if s is None:
                def f0(eng, fn=fn):
                    fn(eng)
                q[e].append(f0)
                return None
            semc[s] += 1
            def f(eng, fn=fn, h=semh[s]):
                fn(eng).then_inc(h, 1)
            q[e].append(f)
            return semc[s]

        op("vector", lambda e: e.memset(epsb2[:], EPS))
        dma("sync", bsb[:], bits, s="ld_m")
        dma("sync", ATs[:], AT, s="ld_m")
        dma("sync", ones_b[:], ones64, s="ld_m")
        dma("sync", qsb[:], qsel, s="ld_m")
        wait("vector", "ld_m")
        op("vector", lambda e: e.reduce_max(out=sel[:, 0:1], in_=bsb[:], axis=AX.X))
        op("vector", lambda e: e.tensor_single_scalar(out=sel[:, 1:2], in_=sel[:, 0:1],
                                                      scalar=0.01, op=ALU.is_gt))
        tv = op("vector", lambda e: e.tensor_scalar_mul(out=ATs[:], in0=ATs[:], scalar1=sel[:, 1:2]), "ve")
        wait("tensor", "ve", tv)
        copy_t = [0, 0]
        for ob in range(8):
            wait("tensor", "ve", copy_t[ob % 2])
            tp = op("tensor", lambda e, ob=ob: e.matmul(
                out=pr[:, (ob % 2) * 512:(ob % 2) * 512 + 512], lhsT=ones_b[:],
                rhs=ATs[:, ob * 512:(ob + 1) * 512], start=True, stop=True), "pe")
            wait("vector", "pe", tp)
            copy_t[ob % 2] = op("vector", lambda e, ob=ob: e.tensor_copy(
                out=rsel[:, ob * 512:(ob + 1) * 512],
                in_=pr[:, (ob % 2) * 512:(ob % 2) * 512 + 512]), "ve")

        lguard = [(0, 0), (0, 0)]
        dma("sync", dsb_[0][:], dense[0:128, :], s="ld_d0")
        dma("sync", x1b_[0][:], x1[0:128, :], s="ld_x0")
        for tt in range(16):
            bix = tt % 2
            dsb = dsb_[bix]; x1b = x1b_[bix]
            if tt < 15:
                nbix = (tt + 1) % 2
                wait("sync", "ve", lguard[nbix][0])
                wait("sync", "ac", lguard[nbix][1])
                dma("sync", dsb_[nbix][:], dense[(tt + 1) * 128:(tt + 2) * 128, :], s="ld_d%d" % nbix)
                dma("sync", x1b_[nbix][:], x1[(tt + 1) * 128:(tt + 2) * 128, :], s="ld_x%d" % nbix)
            wait("vector", "ld_d%d" % bix)
            op("vector", lambda e, tt=tt, dsb=dsb: e.tensor_scalar_mul(out=vb[:], in0=dsb[:],
                                                          scalar1=qsb[:, tt:tt + 1]))
            op("vector", lambda e: e.tensor_mul(out=vb[:], in0=vb[:], in1=rsel[:]))
            op("vector", lambda e: e.reduce_sum(out=st2[:, 0:1], in_=vb[:], axis=AX.X))
            op("vector", lambda e: e.scalar_tensor_tensor(out=sq[:], in0=vb[:], scalar=1.0,
                                                          in1=vb[:], op0=ALU.mult, op1=ALU.mult,
                                                          accum_out=st2[:, 1:2]))
            op("vector", lambda e: e.tensor_scalar_mul(out=st2[:, 2:3], in0=st2[:, 0:1],
                                                       scalar1=1.0 / H))
            op("vector", lambda e: e.tensor_copy(out=st2[:, 9:10], in_=st2[:, 2:3]))
            op("vector", lambda e: e.tensor_scalar_mul(out=st2[:, 3:4], in0=st2[:, 2:3],
                                                       scalar1=st2[:, 9:10]))
            op("vector", lambda e: e.tensor_scalar_mul(out=st2[:, 8:9], in0=st2[:, 1:2],
                                                       scalar1=1.0 / H))
            tv = op("vector", lambda e: e.tensor_single_scalar(
                out=st2[:, 4:5], in_=st2[:, 8:9], scalar=st2[:, 3:4], op=ALU.subtract), "ve")
            wait("scalar", "ve", tv)
            ts = op("scalar", lambda e: e.activation(out=st2[:, 5:6], in_=st2[:, 4:5],
                                                     func=AF.Sqrt, bias=epsb2[:]), "ac")
            wait("vector", "ac", ts)
            op("vector", lambda e: e.reciprocal(out=st2[:, 6:7], in_=st2[:, 5:6]))
            op("vector", lambda e: e.tensor_scalar_mul(out=st2[:, 10:11], in0=st2[:, 2:3],
                                                       scalar1=-1.0))
            tv = op("vector", lambda e: e.tensor_scalar_mul(out=st2[:, 7:8], in0=st2[:, 10:11],
                                                            scalar1=st2[:, 6:7]), "ve")
            wait("scalar", "ve", tv)
            ts = op("scalar", lambda e: e.activation(out=sq[:], in_=vb[:], func=AF.Identity,
                                                     scale=st2[:, 6:7], bias=st2[:, 7:8]), "ac")
            wait("vector", "ac", ts)
            wait("vector", "ld_x%d" % bix)
            wait("vector", "st_o")
            tv = op("vector", lambda e, x1b=x1b: e.tensor_add(out=ob_out[:], in0=x1b[:], in1=sq[:]), "ve")
            lguard[bix] = (semc["ve"], semc["ac"])
            wait("sync", "ve", tv)
            dma("sync", out[tt * 128:(tt + 1) * 128, :], ob_out[:], s="st_o")
        wait("sync", "st_o")

        for e in ENGS:
            getattr(blk, e)(lambda eng, qq=q[e]: [f(eng) for f in qq])
    return nc


def prep_shared(inputs):
    f4 = np.float32
    wq = np.ascontiguousarray(inputs["L0_attn_w_q"], f4)
    wk = np.ascontiguousarray(inputs["L0_attn_w_k"], f4)
    wv = np.ascontiguousarray(inputs["L0_attn_w_v"], f4)
    wo = np.ascontiguousarray(inputs["L0_attn_w_o"], f4)
    W = np.ascontiguousarray(inputs["L0_ffn_weight"], f4)
    bias = np.asarray(inputs["L0_ffn_bias"], f4)
    cent = np.ascontiguousarray(inputs["L0_ffn_centroids"], f4)
    assign = np.asarray(inputs["L0_ffn_assignments"])

    d = {}
    d["wq_t"] = np.ascontiguousarray(wq.reshape(32, 128, 32, 128).transpose(2, 1, 0, 3))
    wk_t = wk.reshape(32, 128, 4, 256).transpose(2, 1, 0, 3)
    wv_t = wv.reshape(32, 128, 4, 256).transpose(2, 1, 0, 3)
    d["wkv_t"] = np.ascontiguousarray(np.concatenate([wk_t, wv_t], axis=0))
    d["wo_t"] = np.ascontiguousarray(wo.reshape(32, 128, 8, 512).transpose(2, 1, 0, 3))
    wT = np.ascontiguousarray(W.T)
    d["wT_t"] = np.ascontiguousarray(wT.reshape(32, 128, 16, 256).transpose(2, 1, 0, 3))
    d["bias_i"] = np.ascontiguousarray(np.broadcast_to(bias[None, :], (128, H)))
    d["centT"] = np.ascontiguousarray(cent.T)
    d["identb"] = np.eye(128, dtype=ml_dtypes.bfloat16)
    d["identf"] = np.eye(128, dtype=np.float32)
    jj = np.arange(128)[None, :]
    pp = np.arange(128)[:, None]
    d["tri1"] = np.where(jj <= pp, 0.0, -1e9).astype(ml_dtypes.bfloat16)
    tri2_p0 = np.where(jj <= pp - 1, 0.0, -1e9).astype(ml_dtypes.bfloat16)
    d["tri2_p0"] = tri2_p0
    d["tri2_p1"] = d["tri1"]
    AT = np.zeros((64, H), np.float32)
    AT[np.asarray(assign, np.int64), np.arange(H)] = 1.0
    d["AT"] = AT.astype(ml_dtypes.bfloat16)
    d["ones64"] = np.ones((64, 128), dtype=ml_dtypes.bfloat16)
    return d


def core_inputs1(inputs, shared, c):
    b, p = c // 2, c % 2
    x = np.asarray(inputs["L0_attn_hidden_states"], np.float32)
    x0p = np.ascontiguousarray(np.concatenate([x[b, p::2], x[b, 1 - p::2]], axis=0))
    return {
        "x0": x0p, "wq_t": shared["wq_t"], "wkv_t": shared["wkv_t"],
        "wo_t": shared["wo_t"], "wT_t": shared["wT_t"], "bias_i": shared["bias_i"],
        "centT": shared["centT"], "tri1": shared["tri1"],
        "tri2": shared["tri2_p0"] if p == 0 else shared["tri2_p1"],
        "identb": shared["identb"], "identf": shared["identf"],
    }


def kernel(**inputs):
    shared = prep_shared(inputs)
    nc1 = build_launch1()
    in_maps1 = [core_inputs1(inputs, shared, c) for c in range(8)]
    res1 = run_bass_kernel_spmd(nc1, in_maps1, core_ids=list(range(8)))
    bits = np.stack([np.asarray(res1.results[c]["cmax"][:, 0], np.float32)
                     for c in range(8)], axis=1)
    bits = np.ascontiguousarray(bits)
    nc2 = build_launch2()
    in_maps2 = []
    for c in range(8):
        r = res1.results[c]
        in_maps2.append({
            "x1": np.asarray(r["x1"], np.float32),
            "dense": np.asarray(r["dense"]).view(ml_dtypes.bfloat16)
                     if r["dense"].dtype != ml_dtypes.bfloat16 else r["dense"],
            "qsel": np.asarray(r["qsel"], np.float32),
            "bits": bits, "AT": shared["AT"], "ones64": shared["ones64"],
        })
    res2 = run_bass_kernel_spmd(nc2, in_maps2, core_ids=list(range(8)))
    out = np.empty((B_, S_, H), np.float32)
    for c in range(8):
        b, p = c // 2, c % 2
        x2 = np.asarray(res2.results[c]["out"], np.float32)
        out[b, p::2, :] = x2
    return out


# version 06
# speedup vs baseline: 1.2605x; 1.0499x over previous
import numpy as np
import ml_dtypes
from contextlib import ExitStack
import concourse.bass as bass
import concourse.mybir as mybir
from concourse.bass_utils import run_bass_kernel_spmd

F32 = mybir.dt.float32
BF16 = mybir.dt.bfloat16
AF = mybir.ActivationFunctionType
ALU = mybir.AluOpType
AX = mybir.AxisListType

B_, S_, H = 4, 4096, 4096
NKV, D = 8, 128
TL, TF = 2048, 4096
NCH = H // 128
SCALE = float(1.0 / np.sqrt(128.0))
EPS = 1e-5

ENGS = ("sync", "tensor", "vector", "scalar", "gpsimd")
B_LIMIT = None
PHASE_MARKS = []
B_STAGE = 6

AB_CAP = 70144
ABO = dict(lnT=0, wsbA=32768, lnb=40960, kst=45056, vtm=45568, vst=46080,
           wsbB=32768, wsbB1=66048, khb=36864, vhb=40960, qs0=45056, qs1=47104,
           pbuf0=49152, pbuf1=53248, ptb0=57344, ptb1=61440, atst=65536,
           wosb=0, atb=32768, atb1=36864,
           lcC=40960, lcTt=36864,
           lch=0, wTsb=32768, wTsb1=49664, dstg=49152, biasb2=66048)
AF_CAP = 17408
AFO = dict(stA=0, xbuf0=8192, xbuf1=12288,
           stB=0, stB1=4096,
           stC=0, yev=16384,
           yfull=0, x0t=4096, x1t=8192, sq2=12288,
           stD0=0, stD1=8192, biasb=8192)


def build_launch1(dbg=(), phases="ABCD"):
    nc = bass.Bass("TRN2", target_bir_lowering=False, debug=False)

    def inp(name, shape, dt=F32):
        return nc.dram_tensor(name, shape, dt, kind="ExternalInput").ap()

    def outp(name, shape, dt=F32):
        return nc.dram_tensor(name, shape, dt, kind="ExternalOutput").ap()

    def scr(name, shape, dt):
        kind = "ExternalOutput" if name in dbg else "Internal"
        return nc.dram_tensor(name, shape, dt, kind=kind).ap()

    x0 = inp("x0", [TF, H])
    wq_t = inp("wq_t", [32, 128, NCH, 128])
    wkv_t = inp("wkv_t", [8, 128, NCH, 256])
    wo_t = inp("wo_t", [8, 128, NCH, 512])
    wT_t = inp("wT_t", [16, 128, NCH, 256])
    bias_i = inp("bias_i", [128, H])
    centT = inp("centT", [H, 64])
    tri1 = inp("tri1", [128, 128], BF16); tri2 = inp("tri2", [128, 128], BF16)
    identb = inp("identb", [128, 128], BF16)
    identf = inp("identf", [128, 128], F32)

    kT = scr("kT", [NKV, 128, TF], BF16)
    vT = scr("vT", [NKV, 128, 32, 128], BF16)
    lnTs = scr("lnTs", [2, 128, NCH * 1024], BF16)
    at3 = scr("at3", [16, 128, NCH, 128], BF16)
    y_s = scr("y_s", [16, 128, H], F32)
    lcT4 = scr("lcT4", [16, 128, NCH, 128], BF16)

    x1o = outp("x1", [TL, H], F32)
    denseo = outp("dense", [TL, H], BF16)
    qselo = outp("qsel", [128, 16], F32)
    cmaxo = outp("cmax", [64, 1], F32)

    q = {e: [] for e in ENGS}
    pbuf_guard = [0]
    at_store = [0]
    yev_store = [0]
    ctx = ExitStack()
    with ctx:
        semh, semc = {}, {}
        for n in ["ld_x", "ld_w", "ld_w2", "ld_k", "ld_v", "ld_l", "ld_a", "ld_y", "ld_m",
                  "st_kv", "st_ln", "st_at", "st_y", "st_x1", "st_lct", "st_d", "st_o",
                  "pe", "ve", "ac", "gp", "bar", "arn0", "arn1", "arn2", "arn3", "trc", "ptc"]:
            semh[n] = ctx.enter_context(nc.semaphore(n)); semc[n] = 0

        AB = ctx.enter_context(nc.sbuf_tensor("AB", [128, AB_CAP], BF16))
        AFm = ctx.enter_context(nc.sbuf_tensor("AFm", [128, AF_CAP], F32))
        tri1b = ctx.enter_context(nc.sbuf_tensor("tri1b", [128, 128], BF16))
        tri2b = ctx.enter_context(nc.sbuf_tensor("tri2b", [128, 128], BF16))
        idb = ctx.enter_context(nc.sbuf_tensor("idb", [128, 128], BF16))
        idf = ctx.enter_context(nc.sbuf_tensor("idf", [128, 128], F32))
        stat = ctx.enter_context(nc.sbuf_tensor("stat", [128, 16], F32))
        mxb = ctx.enter_context(nc.sbuf_tensor("mxb", [128, 16], F32))
        sab = ctx.enter_context(nc.sbuf_tensor("sab", [128, 16], F32))
        fcb = ctx.enter_context(nc.sbuf_tensor("fcb", [128, 16], F32))
        qsel_sb = ctx.enter_context(nc.sbuf_tensor("qsel_sb", [128, 16], F32))
        cmax_sb = ctx.enter_context(nc.sbuf_tensor("cmax_sb", [64, 4], F32))
        dsm = ctx.enter_context(nc.sbuf_tensor("dsm", [128, 80], F32))
        epsb = ctx.enter_context(nc.sbuf_tensor("epsb", [128, 1], F32))
        lgb = ctx.enter_context(nc.sbuf_tensor("lgb", [128, 128], F32))

        arena = ctx.enter_context(nc.psum_tensor("arena", [128, 1536], F32))
        trB = ctx.enter_context(nc.psum_tensor("trBp", [128, 1024], BF16))
        trB2 = ctx.enter_context(nc.psum_tensor("trBp2", [128, 1024], BF16))
        avP = ctx.enter_context(nc.psum_tensor("avPp", [128, 512], F32))
        qpP = ctx.enter_context(nc.psum_tensor("qpPp", [128, 1024], F32))

        blk = ctx.enter_context(nc.Block())

        def ab(name, n):
            return AB[:, ABO[name]:ABO[name] + n]

        def af(name, n):
            return AFm[:, AFO[name]:AFO[name] + n]

        def dma(e, out, in_, s="ld", **kw):
            semc[s] += 16
            h, tgt = semh[s], semc[s]
            def f(eng, out=out, in_=in_, h=h, kw=kw):
                eng.dma_start(out=out, in_=in_, **kw).then_inc(h, 16)
            q[e].append(f)
            return tgt

        def wait(e, s, v=None):
            if v is None:
                v = semc[s]
            if v <= 0:
                return
            def f(eng, h=semh[s], v=v):
                eng.wait_ge(h, v)
            q[e].append(f)

        def wait_all_dma(e):
            for s2 in ("ld_x", "ld_w", "ld_w2", "ld_k", "ld_v", "ld_l", "ld_a", "ld_y", "ld_m",
                       "st_kv", "st_ln", "st_at", "st_y", "st_x1", "st_lct", "st_d", "st_o"):
                wait(e, s2)

        OWN = {"vector": "ve", "scalar": "ac", "gpsimd": "gp"}

        def op(e, fn, s=None, serial=True):
            if e in ("vector", "scalar"):
                own = OWN[e]
                if serial:
                    wait(e, own, semc[own])
                if s is None:
                    s = own
            if s is None:
                def f0(eng, fn=fn):
                    fn(eng)
                q[e].append(f0)
                return None
            semc[s] += 1
            h, tgt = semh[s], semc[s]
            def f(eng, fn=fn, h=h):
                fn(eng).then_inc(h, 1)
            q[e].append(f)
            return tgt

        def mm(out, lhsT, rhs, first, last, s="pe"):
            if last:
                semc[s] += 1
                def f(eng, out=out, lhsT=lhsT, rhs=rhs, first=first, h=semh[s]):
                    eng.matmul(out=out, lhsT=lhsT, rhs=rhs, start=first,
                               stop=True).then_inc(h, 1)
            else:
                def f(eng, out=out, lhsT=lhsT, rhs=rhs, first=first):
                    eng.matmul(out=out, lhsT=lhsT, rhs=rhs, start=first, stop=False)
            q["tensor"].append(f)
            return semc[s]

        def tposes(srcs, dsts_in_trB, last_inc=True, bank=None):
            n = len(srcs)
            bk = bank if bank is not None else trB
            for i, src in enumerate(srcs):
                last = (i == n - 1) and last_inc
                if last:
                    semc["pe"] += 1
                    def f(eng, src=src, i=dsts_in_trB[i], h=semh["pe"], bk=bk):
                        eng.transpose(out=bk[:, i * 128:(i + 1) * 128], in_=src,
                                      identity=idb[:]).then_inc(h, 1)
                else:
                    def f(eng, src=src, i=dsts_in_trB[i], bk=bk):
                        eng.transpose(out=bk[:, i * 128:(i + 1) * 128], in_=src,
                                      identity=idb[:])
                q["tensor"].append(f)
            return semc["pe"]

        def barrier():
            wait_all_dma("sync")
            for e in ENGS:
                semc["bar"] += 1
                def f(eng, h=semh["bar"]):
                    eng.nop().then_inc(h, 1)
                q[e].append(f)
            v = semc["bar"]
            for e in ENGS:
                wait(e, "bar", v)

        def ln_stats(src):
            sq = CUR_SQ[0]
            op("vector", lambda e, src=src: e.reduce_sum(out=stat[:, 0:1], in_=src, axis=AX.X))
            op("vector", lambda e, src=src, sq=sq: e.scalar_tensor_tensor(
                out=sq, in0=src, scalar=1.0, in1=src, op0=ALU.mult, op1=ALU.mult,
                accum_out=stat[:, 1:2]))
            op("vector", lambda e: e.tensor_scalar_mul(out=stat[:, 2:3], in0=stat[:, 0:1],
                                                       scalar1=1.0 / H))
            op("vector", lambda e: e.tensor_copy(out=stat[:, 12:13], in_=stat[:, 2:3]))
            op("vector", lambda e: e.tensor_scalar_mul(out=stat[:, 3:4], in0=stat[:, 2:3],
                                                       scalar1=stat[:, 12:13]))
            op("vector", lambda e: e.tensor_scalar_mul(out=stat[:, 13:14], in0=stat[:, 1:2],
                                                       scalar1=1.0 / H))
            tv = op("vector", lambda e: e.tensor_single_scalar(
                out=stat[:, 4:5], in_=stat[:, 13:14], scalar=stat[:, 3:4], op=ALU.subtract))
            wait("scalar", "ve", tv)
            ts = op("scalar", lambda e: e.activation(out=stat[:, 5:6], in_=stat[:, 4:5],
                                                     func=AF.Sqrt, bias=epsb[:]))
            wait("vector", "ac", ts)
            op("vector", lambda e: e.reciprocal(out=stat[:, 6:7], in_=stat[:, 5:6]))
            op("vector", lambda e: e.tensor_scalar_mul(out=stat[:, 14:15], in0=stat[:, 2:3],
                                                       scalar1=-1.0))
            tv = op("vector", lambda e: e.tensor_scalar_mul(out=stat[:, 7:8], in0=stat[:, 14:15],
                                                            scalar1=stat[:, 6:7]))
            wait("scalar", "ve", tv)

        def ln_apply(src, dst):
            return op("scalar", lambda e, src=src, dst=dst: e.activation(
                out=dst, in_=src, func=AF.Identity, scale=stat[:, 6:7],
                bias=stat[:, 7:8]))

        CUR_SQ = [ab("lnb", 4096)]

        dma("sync", tri1b[:], tri1, s="ld_m")
        dma("sync", tri2b[:], tri2, s="ld_m")
        dma("sync", idb[:], identb, s="ld_m")
        t0 = dma("sync", idf[:], identf, s="ld_m")
        wait("tensor", "ld_m")
        wait("vector", "ld_m")
        wait("scalar", "ld_m")
        op("vector", lambda e: e.memset(cmax_sb[:, 0:1], -1e30), "ve")
        op("vector", lambda e: e.memset(epsb[:], EPS), "ve")
        op("vector", lambda e: e.memset(qsel_sb[:], 0.0), "ve")

        lnT3 = ab("lnT", 32768).rearrange("p (c t) -> p c t", t=1024)
        trB3 = trB[:].rearrange("p (k q) -> p k q", q=128)

        PHASE_MARKS.append(("preamble_end", {e: len(q[e]) for e in ENGS}))
        if "A" in phases:
            CUR_SQ[0] = ab("lnb", 4096)
            xguard = [(0, 0), (0, 0)]
            xld = ["ld_x", "ld_y"]
            for sset in range(2):
                row0 = TL if sset == 0 else 0
                tok0 = TL if sset == 0 else 0
                for half in range(2):
                    wait("vector", "st_ln")
                    for tloc in range(8):
                        r0 = row0 + half * 1024 + tloc * 128
                        bix = tloc % 2
                        xb = af("xbuf%d" % bix, 4096)
                        wait("sync", "ve", xguard[bix][0])
                        wait("sync", "ac", xguard[bix][1])
                        dma("sync", xb, x0[r0:r0 + 128, :], s=xld[bix])
                        wait("vector", xld[bix])
                        wait("vector", "pe", semc["pe"])
                        ln_stats(xb)
                        wait("scalar", "pe", semc["pe"])
                        ta = ln_apply(xb, ab("lnb", 4096))
                        xguard[bix] = (semc["ve"], semc["ac"])
                        wait("tensor", "ac", ta)
                        for bch in range(4):
                            wait("tensor", "trc", semc["trc"])
                            tp = tposes([ab("lnb", 4096)[:, (bch * 8 + k) * 128:(bch * 8 + k + 1) * 128]
                                         for k in range(8)], list(range(8)))
                            wait("vector", "pe", tp)
                            dst = lnT3[:, bch * 8:(bch + 1) * 8, tloc * 128:(tloc + 1) * 128]
                            op("vector", lambda e, dst=dst: e.tensor_copy(out=dst, in_=trB3[:, 0:8, :]), "trc",
                               serial=False)
                    if sset == 1:
                        wait("sync", "trc", semc["trc"])
                        dma("sync", lnTs[half], ab("lnT", 32768), s="st_ln")
                    for sl in range(8):
                        wait("sync", "gp", semc["gp"])
                        dma("sync", af("stA", 8192), wkv_t[sl].rearrange("p a n -> p (a n)"), s="ld_w")
                        wait("gpsimd", "ld_w")
                        wait("gpsimd", "pe", semc["pe"])
                        tg = op("gpsimd", lambda e: e.tensor_copy(out=ab("wsbA", 8192),
                                                                  in_=af("stA", 8192)), "gp")
                        wsA = ab("wsbA", 8192).rearrange("p (a n) -> p a n", n=256)
                        wait("tensor", "gp", tg)
                        wait("tensor", "trc", semc["trc"])
                        for grp in range(2):
                            for dc in range(2):
                                pk = qpP[:, dc * 512:(dc + 1) * 512]
                                wait("tensor", "ve", semc["ve"])
                                for ch in range(NCH):
                                    tpk = mm(pk, wsA[:, ch, dc * 128:(dc + 1) * 128],
                                             lnT3[:, ch, grp * 512:(grp + 1) * 512],
                                             ch == 0, ch == NCH - 1)
                                wait("vector", "pe", tpk)
                                hh = (sl % 4) * 2 + dc
                                c0 = tok0 + half * 1024 + grp * 512
                                if sl < 4:
                                    wait("vector", "st_kv")
                                    tv = op("vector", lambda e, pk=pk: e.tensor_copy(
                                        out=ab("kst", 512), in_=pk), "ve", serial=False)
                                    wait("sync", "ve", tv)
                                    dma("sync", kT[hh][:, c0:c0 + 512],
                                        ab("kst", 512), s="st_kv")
                                else:
                                    wait("vector", "st_kv")
                                    tv = op("vector", lambda e, pk=pk: e.tensor_copy(
                                        out=ab("vtm", 512), in_=pk), "ve", serial=False)
                                    wait("tensor", "ve", tv)
                                    wait("tensor", "trc", semc["trc"])
                                    tp = tposes([ab("vtm", 512)[:, j * 128:(j + 1) * 128]
                                                 for j in range(4)], list(range(4)))
                                    wait("vector", "pe", tp)
                                    tv = op("vector", lambda e: e.tensor_copy(
                                        out=ab("vst", 512), in_=trB[:, 0:512]), "trc", serial=False)
                                    wait("sync", "trc", tv)
                                    kt0 = c0 // 128
                                    dst = vT[hh, :, kt0:kt0 + 4, :].rearrange("p j c -> p (j c)")
                                    dma("sync", dst, ab("vst", 512), s="st_kv")
            barrier()

        PHASE_MARKS.append(("before_B", {e: len(q[e]) for e in ENGS}))
        if "B" in phases:
            LB = B_LIMIT or (2, 8, 4, 8)
            arn_cnt = {0: 0, 1: 0, 2: 0}
            pbg = {0: 0, 1: 0}
            ptcg = {0: 0, 1: 0}
            rrctr = [0]
            wqgp = {0: 0, 1: 0}
            wqpe = {0: 0, 1: 0}
            trguard = {0: ("ve", 0), 1: ("ac", 0)}
            qb_guard = [0, 0]
            for half in (LB[0] if isinstance(LB[0], (list, tuple)) else range(LB[0])):
                wait("sync", "pe", semc["pe"])
                dma("sync", ab("lnT", 32768), lnTs[half], s="ld_l")
                for h in range(LB[1]):
                    wait("sync", "pe", semc["pe"])
                    dma("sync", ab("khb", 4096), kT[h], s="ld_k")
                    dma("sync", ab("vhb", 4096), vT[h].rearrange("p k c -> p (k c)"), s="ld_v")
                    for g in range(LB[2]):
                        qh = h * 4 + g
                        qpar = qh % 2
                        qb = ab("qs0", 1024) if qpar == 0 else ab("qs1", 1024)
                        stBn = ("stB", "stB1")[qpar]
                        wsn = ("wsbB", "wsbB1")[qpar]
                        wstr = ("ld_w", "ld_w2")[qpar]
                        if qh == h * 4 and half == 0 and h == 0:
                            dma("sync", af(stBn, 4096), wq_t[qh].rearrange("p a n -> p (a n)"), s=wstr)
                        nqh = qh + 1 if g < LB[2] - 1 else (h + 1) * 4
                        if nqh < 32 or half == 0:
                            nqh2 = nqh % 32
                            npar = nqh2 % 2
                            wait("sync", "gp", wqgp[npar])
                            dma("sync", af(("stB", "stB1")[npar], 4096),
                                wq_t[nqh2].rearrange("p a n -> p (a n)"), s=("ld_w", "ld_w2")[npar])
                        wait("gpsimd", wstr)
                        wait("gpsimd", "pe", wqpe[qpar])
                        tg = op("gpsimd", lambda e, wsn=wsn, stBn=stBn: e.tensor_copy(
                            out=ab(wsn, 4096), in_=af(stBn, 4096)), "gp")
                        wqgp[qpar] = semc["gp"]
                        wsB = ab(wsn, 4096).rearrange("p (a n) -> p a n", n=128)
                        wait("tensor", "gp", tg)
                        wait("tensor", "ld_l")
                        wait("tensor", "pe", qb_guard[qh % 2])
                        wait("tensor", "ve", semc["ve"])
                        for tgrp in range(2):
                            pk = qpP[:, tgrp * 512:(tgrp + 1) * 512]
                            for ch in range(NCH):
                                tpk = mm(pk, wsB[:, ch, :],
                                         lnT3[:, ch, tgrp * 512:(tgrp + 1) * 512],
                                         ch == 0, ch == NCH - 1)
                            wait("vector", "pe", tpk)
                            op("vector", lambda e, pk=pk, qb=qb, tgrp=tgrp: e.tensor_scalar_mul(
                                out=qb[:, tgrp * 512:(tgrp + 1) * 512], in0=pk, scalar1=SCALE), "ve",
                               serial=False)
                        tqev = semc["ve"]
                        wqpe[qpar] = semc["pe"]
                        wait("tensor", "ve", tqev)
                        wait("tensor", "ld_k")
                        wait("tensor", "ld_v")
                        def softmax_stage(tloc, t, par):
                            fks = [("L", j) for j in range(t + 1)] + [("P", j) for j in range(t + 1)]
                            nfk = len(fks)
                            nr = (nfk + 3) // 4
                            pb = ab("pbuf%d" % par, 4096)
                            mb = par * 8
                            for r in range(nr):
                                sub = fks[r * 4:(r + 1) * 4]
                                hb = rrctr[0] % 3
                                rrctr[0] += 1
                                abh = arena[:, hb * 512:hb * 512 + 512]
                                wait("tensor", "ac", arn_cnt[hb])
                                kk = 0
                                while kk < len(sub):
                                    side, j0 = sub[kk]
                                    run = 1
                                    while (kk + run < len(sub) and sub[kk + run][0] == side
                                           and sub[kk + run][1] == j0 + run):
                                        run += 1
                                    base = (0 if side == "L" else 2048) + j0 * 128
                                    tpk = mm(abh[:, kk * 128:(kk + run) * 128],
                                             qb[:, tloc * 128:(tloc + 1) * 128],
                                             ab("khb", 4096)[:, base:base + run * 128],
                                             True, True)
                                    kk += run
                                wait("vector", "pe", tpk)
                                for kk2, (side, j) in enumerate(sub):
                                    if j == t:
                                        trib = tri1b if side == "L" else tri2b
                                        op("vector", lambda e, kk2=kk2, abh=abh, trib=trib: e.tensor_add(
                                            out=abh[:, kk2 * 128:(kk2 + 1) * 128],
                                            in0=abh[:, kk2 * 128:(kk2 + 1) * 128], in1=trib[:]),
                                           serial=False)
                                used = len(sub) * 128
                                tv = op("vector", lambda e, abh=abh, used=used, r=r, mb=mb: e.reduce_max(
                                    out=mxb[:, mb + r:mb + r + 1], in_=abh[:, 0:used], axis=AX.X,
                                    negate=True))
                                wait("scalar", "ve", tv)
                                if r == 0:
                                    wait("scalar", "pe", pbg[par])
                                wait("scalar", "ac", semc["ac"])
                                semc["ac"] += 1
                                arn_cnt[hb] = semc["ac"]
                                def fe(eng, abh=abh, used=used, r=r, mb=mb, pb=pb, h2=semh["ac"]):
                                    eng.activation(out=pb[:, r * 512:r * 512 + used],
                                                   in_=abh[:, 0:used], func=AF.Exp,
                                                   bias=mxb[:, mb + r:mb + r + 1],
                                                   accum_out=sab[:, mb + r:mb + r + 1]).then_inc(h2, 1)
                                q["scalar"].append(fe)
                            wait("vector", "ac", semc["ac"])
                            op("vector", lambda e, nr=nr, mb=mb: e.tensor_scalar_mul(
                                out=fcb[:, mb:mb + nr], in0=mxb[:, mb:mb + nr], scalar1=-1.0))
                            tv = op("vector", lambda e, nr=nr, mb=mb: e.reduce_max(
                                out=stat[:, 8:9], in_=fcb[:, mb:mb + nr], axis=AX.X, negate=True))
                            wait("scalar", "ve", tv)
                            ts = op("scalar", lambda e, nr=nr, mb=mb: e.activation(
                                out=fcb[:, mb:mb + nr], in_=fcb[:, mb:mb + nr], func=AF.Exp,
                                bias=stat[:, 8:9]))
                            wait("vector", "ac", ts)
                            for r in range(nr):
                                op("vector", lambda e, r=r, mb=mb: e.tensor_scalar_mul(
                                    out=sab[:, mb + r:mb + r + 1], in0=sab[:, mb + r:mb + r + 1],
                                    scalar1=fcb[:, mb + r:mb + r + 1]))
                            op("vector", lambda e, nr=nr, mb=mb: e.reduce_sum(
                                out=stat[:, 9:10], in_=sab[:, mb:mb + nr], axis=AX.X))
                            op("vector", lambda e: e.reciprocal(out=stat[:, 10:11], in_=stat[:, 9:10]))
                            op("vector", lambda e, nr=nr, mb=mb: e.tensor_scalar_mul(
                                out=fcb[:, mb:mb + nr], in0=fcb[:, mb:mb + nr], scalar1=stat[:, 10:11]))
                            tnorm = None
                            for r in range(nr):
                                used = min(4, nfk - r * 4) * 128
                                tnorm = op("vector", lambda e, r=r, used=used, pb=pb, mb=mb: e.tensor_scalar_mul(
                                    out=pb[:, r * 512:r * 512 + used],
                                    in0=pb[:, r * 512:r * 512 + used],
                                    scalar1=fcb[:, mb + r:mb + r + 1]), "ve")
                            return (nfk, tnorm)

                        def txav_stage(t, tloc, par, info):
                            nfk, tnorm = info
                            pb = ab("pbuf%d" % par, 4096)
                            ptb = ab("ptb%d" % par, 4096)
                            wait("tensor", "ve", tnorm)
                            wait("tensor", "ptc", ptcg[par])
                            ctargets = []
                            for bt in range((nfk + 7) // 8):
                                bsel = bt % 2
                                bk = trB if bsel == 0 else trB2
                                gsem, gval = trguard[bsel]
                                wait("tensor", gsem, gval)
                                cnt = min(8, nfk - bt * 8)
                                tp = tposes([pb[:, (bt * 8 + k) * 128:(bt * 8 + k + 1) * 128]
                                             for k in range(cnt)], list(range(cnt)), bank=bk)
                                ceng = "vector" if bsel == 0 else "scalar"
                                wait(ceng, "pe", tp)
                                if ceng == "vector":
                                    tc = op("vector", lambda e, bt=bt, cnt=cnt, ptb=ptb, bk=bk: e.tensor_copy(
                                        out=ptb[:, bt * 1024:bt * 1024 + cnt * 128],
                                        in_=bk[:, 0:cnt * 128]), serial=False)
                                    trguard[bsel] = ("ve", tc)
                                    ctargets.append(("ve", tc))
                                else:
                                    tc = op("scalar", lambda e, bt=bt, cnt=cnt, ptb=ptb, bk=bk: e.activation(
                                        out=ptb[:, bt * 1024:bt * 1024 + cnt * 128],
                                        in_=bk[:, 0:cnt * 128], func=AF.Identity), serial=False)
                                    trguard[bsel] = ("ac", tc)
                                    ctargets.append(("ac", tc))
                            pbg[par] = semc["pe"]
                            for gs, gv in ctargets:
                                wait("tensor", gs, gv)
                            fks = [("L", j) for j in range(t + 1)] + [("P", j) for j in range(t + 1)]
                            for i_fk, (side, j) in enumerate(fks):
                                vcol = (j if side == "L" else 16 + j) * 128
                                tav = mm(avP[:, (tloc % 4) * 128:(tloc % 4) * 128 + 128],
                                         ab("vhb", 4096)[:, vcol:vcol + 128],
                                         ptb[:, i_fk * 128:(i_fk + 1) * 128],
                                         i_fk == 0, i_fk == nfk - 1)
                            semc["ptc"] += 1
                            def fpt(eng, h2=semh["ptc"]):
                                eng.nop().then_inc(h2, 1)
                            q["tensor"].append(fpt)
                            ptcg[par] = semc["ptc"]
                            if tloc % 4 == 3:
                                wait("vector", "pe", tav)
                                wait("vector", "st_at")
                                tv = op("vector", lambda e: e.tensor_copy(out=ab("atst", 512),
                                                                          in_=avP[:]), "ve", serial=False)
                                wait("sync", "ve", tv)
                                Gt = t // 4
                                dst = at3[4 * Gt:4 * Gt + 4, :, qh, :].rearrange("j p q -> p j q")
                                src = ab("atst", 512).rearrange("p (j q) -> p j q", q=128)
                                dma("sync", dst, src, s="st_at")
                                wait("tensor", "ve", tv)

                        pend = None
                        for tloc in range(LB[3]):
                            t = half * 8 + tloc
                            par = tloc % 2
                            info = softmax_stage(tloc, t, par)
                            if pend is not None:
                                txav_stage(*pend)
                            pend = (t, tloc, par, info)
                        if pend is not None:
                            txav_stage(*pend)
                        qb_guard[qh % 2] = semc["pe"]
            barrier()

        PHASE_MARKS.append(("before_C1", {e: len(q[e]) for e in ENGS}))
        if "C" in phases:
            aguard = [0, 0]
            wo3 = ab("wosb", 32768).rearrange("p (a n) -> p a n", n=1024)
            for qq in range(4):
                for piece in range(2):
                    wait("sync", "gp", semc["gp"])
                    dma("sync", af("stC", 16384), wo_t[qq * 2 + piece].rearrange("p a n -> p (a n)"), s="ld_w")
                    wait("gpsimd", "ld_w")
                    wait("gpsimd", "pe", semc["pe"])
                    src = af("stC", 16384).rearrange("p (a n) -> p a n", n=512)
                    dst = wo3[:, :, piece * 512:(piece + 1) * 512]
                    tg = op("gpsimd", lambda e, src=src, dst=dst: e.tensor_copy(out=dst, in_=src), "gp")
                for tt in range(16):
                    bix = tt % 2
                    anames = ("atb", "atb1")
                    astr = ("ld_a", "ld_k")
                    if qq == 0 and tt == 0:
                        dma("sync", ab(anames[0], 4096), at3[0].rearrange("p c q -> p (c q)"), s=astr[0])
                    if tt < 15 or qq < 3:
                        ntt = (tt + 1) % 16
                        nbix = (tt + 1) % 2
                        wait("sync", "pe", aguard[nbix])
                        dma("sync", ab(anames[nbix], 4096),
                            at3[ntt].rearrange("p c q -> p (c q)"), s=astr[nbix])
                    wait("tensor", astr[bix])
                    wait("tensor", "gp", semc["gp"])
                    wait("tensor", "ve", semc["ve"])
                    for dq in range(NCH):
                        for obi in range(2):
                            tpk = mm(arena[:, obi * 512:(obi + 1) * 512],
                                     ab(anames[bix], 4096)[:, dq * 128:(dq + 1) * 128],
                                     wo3[:, dq, obi * 512:(obi + 1) * 512],
                                     dq == 0, dq == NCH - 1)
                    aguard[bix] = semc["pe"]
                    wait("vector", "pe", tpk)
                    wait("vector", "st_y")
                    op("vector", lambda e: e.tensor_copy(out=af("yev", 1024)[:, 0:512],
                                                         in_=arena[:, 0:512]), serial=False)
                    tv = op("vector", lambda e: e.tensor_copy(out=af("yev", 1024)[:, 512:1024],
                                                              in_=arena[:, 512:1024]), "ve", serial=False)
                    wait("sync", "ve", tv)
                    dma("sync", y_s[tt][:, qq * 1024:(qq + 1) * 1024],
                        af("yev", 1024), s="st_y")
            barrier()

        PHASE_MARKS.append(("before_C2", {e: len(q[e]) for e in ENGS}))
        if "C" in phases:
            centb = AB[:, 0:2048]
            for piece in range(2):
                dma("sync", af("yev", 1024).rearrange("p (a c) -> p a c", c=64),
                    centT.rearrange("(a p) c -> p a c", p=128)[:, piece * 16:(piece + 1) * 16, :],
                    s="ld_m")
                wait("gpsimd", "ld_m")
                tgc = op("gpsimd", lambda e, piece=piece: e.tensor_copy(
                    out=centb[:, piece * 1024:(piece + 1) * 1024], in_=af("yev", 1024)), "gp")
                wait("sync", "gp", tgc)
            cb3 = centb.rearrange("p (a c) -> p a c", c=64)
            x1_store = 0
            lct_store = 0
            for tt in range(16):
                wait("sync", "ve", semc["ve"])
                wait("sync", "ac", semc["ac"])
                dma("sync", af("yfull", 4096), y_s[tt], s="ld_y")
                dma("sync", af("x0t", 4096), x0[tt * 128:(tt + 1) * 128, :], s="ld_x")
                wait("vector", "ld_y")
                CUR_SQ[0] = af("sq2", 4096)
                ln_stats(af("yfull", 4096))
                ta = ln_apply(af("yfull", 4096), af("sq2", 4096))
                wait("vector", "ac", ta)
                wait("vector", "ld_x")
                wait("vector", "st_x1")
                wait("vector", "ac", semc["ac"])
                tv = op("vector", lambda e: e.tensor_add(out=af("x1t", 4096), in0=af("x0t", 4096),
                                                         in1=af("sq2", 4096)), "ve")
                wait("sync", "ve", tv)
                dma("sync", x1o[tt * 128:(tt + 1) * 128, :], af("x1t", 4096), s="st_x1")
                ln_stats(af("x1t", 4096))
                wait("scalar", "pe", semc["pe"])
                ta = ln_apply(af("x1t", 4096), ab("lcC", 4096))
                wait("tensor", "ac", ta)
                wait("tensor", "st_lct")
                for bch in range(4):
                    wait("tensor", "trc", semc["trc"])
                    tp = tposes([ab("lcC", 4096)[:, (bch * 8 + k) * 128:(bch * 8 + k + 1) * 128]
                                 for k in range(8)], list(range(8)))
                    wait("vector", "pe", tp)
                    dst = ab("lcTt", 4096).rearrange("p (c t) -> p c t", t=128)[:, bch * 8:(bch + 1) * 8, :]
                    op("vector", lambda e, dst=dst: e.tensor_copy(out=dst, in_=trB3[:, 0:8, :]), "trc")
                wait("sync", "trc", semc["trc"])
                dma("sync", lcT4[tt].rearrange("p c q -> p (c q)"), ab("lcTt", 4096), s="st_lct")
                wait("tensor", "trc", semc["trc"])
                for ch in range(NCH):
                    tpk = mm(arena[0:64, 1024:1152], cb3[:, ch, :],
                             ab("lcTt", 4096)[:, ch * 128:(ch + 1) * 128],
                             ch == 0, ch == NCH - 1)
                wait("vector", "pe", tpk)
                tv = op("vector", lambda e: e.tensor_copy(out=lgb[0:64, :], in_=arena[0:64, 1024:1152]), "ve")
                wait("tensor", "ve", tv)
                semc["pe"] += 1
                def ftr1(eng, h=semh["pe"]):
                    eng.transpose(out=arena[:, 512:640], in_=lgb[:], identity=idf[:]).then_inc(h, 1)
                q["tensor"].append(ftr1)
                wait("vector", "pe", semc["pe"])
                tv = op("vector", lambda e: e.reduce_max(out=dsm[:, 64:65], in_=arena[:, 512:576],
                                                         axis=AX.X, negate=True), "ve")
                tv = op("vector", lambda e: e.tensor_scalar_mul(out=dsm[:, 65:66], in0=dsm[:, 64:65],
                                                                scalar1=10.0), "ve")
                wait("scalar", "ve", tv)
                ts = op("scalar", lambda e: e.activation(out=dsm[:, 0:64], in_=arena[:, 512:576],
                                                         func=AF.Exp, scale=10.0, bias=dsm[:, 65:66],
                                                         accum_out=dsm[:, 66:67]), "ac")
                wait("vector", "ac", ts)
                op("vector", lambda e: e.reciprocal(out=dsm[:, 67:68], in_=dsm[:, 66:67]))
                op("vector", lambda e: e.tensor_scalar_mul(out=dsm[:, 0:64], in0=dsm[:, 0:64],
                                                       scalar1=dsm[:, 67:68]))
                op("vector", lambda e: e.reduce_max(out=dsm[:, 68:69], in_=dsm[:, 0:64], axis=AX.X))
                tv = op("vector", lambda e, tt=tt: e.tensor_single_scalar(
                    out=qsel_sb[:, tt:tt + 1], in_=dsm[:, 68:69], scalar=0.01, op=ALU.is_gt), "ve")
                wait("tensor", "ve", tv)
                semc["pe"] += 1
                def ftr2(eng, h=semh["pe"]):
                    eng.transpose(out=avP[0:64, 0:128], in_=dsm[:, 0:64],
                                  identity=idf[:]).then_inc(h, 1)
                q["tensor"].append(ftr2)
                wait("vector", "pe", semc["pe"])
                op("vector", lambda e: e.reduce_max(out=cmax_sb[:, 1:2],
                                                    in_=avP[0:64, 0:128], axis=AX.X))
                op("vector", lambda e: e.tensor_copy(out=cmax_sb[:, 2:3], in_=cmax_sb[:, 0:1]))
                op("vector", lambda e: e.tensor_scalar_max(out=cmax_sb[:, 0:1],
                                                           in0=cmax_sb[:, 1:2],
                                                           scalar1=cmax_sb[:, 2:3]))
            tv = semc["ve"]
            wait("sync", "ve", tv)
            dma("sync", qselo, qsel_sb[:], s="st_o")
            dma("sync", cmaxo, cmax_sb[:, 0:1], s="st_o")
            barrier()

        PHASE_MARKS.append(("before_D", {e: len(q[e]) for e in ENGS}))
        if "D" in phases:
            dma("sync", af("stD0", 4096), bias_i, s="ld_m")
            wait("gpsimd", "ld_m")
            tgb = op("gpsimd", lambda e: e.tensor_copy(out=ab("biasb2", 4096), in_=af("stD0", 4096)), "gp")
            wait("sync", "gp", tgb)
            dguard = [0, 0]
            fillgp = [0, 0]
            for half in range(2):
                wait("sync", "pe", semc["pe"])
                for tt8 in range(8):
                    dma("sync", ab("lch", 32768)[:, tt8 * 4096:(tt8 + 1) * 4096],
                        lcT4[half * 8 + tt8].rearrange("p c q -> p (c q)"), s="ld_l")
                for ob in range(8):
                    gob = half * 8 + ob
                    wslot = gob % 2
                    wsn = ("wTsb", "wTsb1")[wslot]
                    if gob == 0:
                        for piece in range(2):
                            stn = ("stD0", "stD1")[piece]
                            dma("sync", af(stn, 8192),
                                wT_t[piece].rearrange("p a n -> p (a n)"), s=("ld_a", "ld_k")[piece])
                            wait("gpsimd", ("ld_a", "ld_k")[piece])
                            src = af(stn, 8192).rearrange("p (a n) -> p a n", n=256)
                            dst = ab(wsn, 16384).rearrange("p (a n) -> p a n", n=512)[:, :, piece * 256:(piece + 1) * 256]
                            op("gpsimd", lambda e, src=src, dst=dst: e.tensor_copy(out=dst, in_=src), "gp")
                        fillgp[0] = semc["gp"]
                    ngob = gob + 1
                    if ngob < 16:
                        nob = ngob % 8
                        nslot = ngob % 2
                        nwsn = ("wTsb", "wTsb1")[nslot]
                        for piece in range(2):
                            stn = ("stD0", "stD1")[piece]
                            wait("sync", "gp", semc["gp"])
                            dma("sync", af(stn, 8192),
                                wT_t[nob * 2 + piece].rearrange("p a n -> p (a n)"), s=("ld_a", "ld_k")[piece])
                            wait("gpsimd", ("ld_a", "ld_k")[piece])
                            wait("gpsimd", "pe", dguard[nslot])
                            src = af(stn, 8192).rearrange("p (a n) -> p a n", n=256)
                            dst = ab(nwsn, 16384).rearrange("p (a n) -> p a n", n=512)[:, :, piece * 256:(piece + 1) * 256]
                            op("gpsimd", lambda e, src=src, dst=dst: e.tensor_copy(out=dst, in_=src), "gp")
                        fillgp[nslot] = semc["gp"]
                    wT3 = ab(wsn, 16384).rearrange("p (a n) -> p a n", n=512)
                    wait("tensor", "gp", fillgp[wslot])
                    wait("tensor", "ld_l")
                    wait("tensor", "ld_m")
                    for tt8 in range(8):
                        wait("tensor", "ve", semc["ve"])
                        pk = arena[:, (tt8 % 2) * 512:(tt8 % 2) * 512 + 512]
                        for ch in range(NCH):
                            tpk = mm(pk, ab("lch", 32768)[:, (tt8 * 32 + ch) * 128:(tt8 * 32 + ch + 1) * 128],
                                     wT3[:, ch, :], ch == 0, ch == NCH - 1)
                        wait("vector", "pe", tpk)
                        wait("vector", "st_d")
                        tv = op("vector", lambda e, pk=pk, ob=ob: e.tensor_add(
                            out=ab("dstg", 512), in0=pk,
                            in1=ab("biasb2", 4096)[:, ob * 512:(ob + 1) * 512]), "ve", serial=False)
                        wait("sync", "ve", tv)
                        r0 = (half * 8 + tt8) * 128
                        dma("sync", denseo[r0:r0 + 128, ob * 512:(ob + 1) * 512],
                            ab("dstg", 512), s="st_d")
                    dguard[wslot] = semc["pe"]
            barrier()

        wait_all_dma("sync")

        for e in ENGS:
            getattr(blk, e)(lambda eng, qq=q[e]: [f(eng) for f in qq])

    return nc


def build_launch2():
    nc = bass.Bass("TRN2", target_bir_lowering=False, debug=False)
    x1 = nc.dram_tensor("x1", [TL, H], F32, kind="ExternalInput").ap()
    dense = nc.dram_tensor("dense", [TL, H], BF16, kind="ExternalInput").ap()
    qsel = nc.dram_tensor("qsel", [128, 16], F32, kind="ExternalInput").ap()
    bits = nc.dram_tensor("bits", [64, 8], F32, kind="ExternalInput").ap()
    AT = nc.dram_tensor("AT", [64, H], BF16, kind="ExternalInput").ap()
    ones64 = nc.dram_tensor("ones64", [64, 128], BF16, kind="ExternalInput").ap()
    out = nc.dram_tensor("out", [TL, H], F32, kind="ExternalOutput").ap()

    q = {e: [] for e in ENGS}
    ctx = ExitStack()
    with ctx:
        semh, semc = {}, {}
        for n in ["ld_m", "ld_d0", "ld_d1", "ld_x0", "ld_x1", "st_o", "pe", "ve", "ac"]:
            semh[n] = ctx.enter_context(nc.semaphore(n + "2")); semc[n] = 0
        SBt = lambda n, sh, dt=F32: ctx.enter_context(nc.sbuf_tensor(n, sh, dt))
        bsb = SBt("bsb", [64, 8]); sel = SBt("sel", [64, 2])
        ATs = SBt("ATs", [64, H], BF16); ones_b = SBt("ones_b", [64, 128], BF16)
        rsel = SBt("rsel", [128, H])
        dsb_ = [SBt("dsb0", [128, H], BF16), SBt("dsb1", [128, H], BF16)]
        x1b_ = [SBt("x1b0", [128, H]), SBt("x1b1", [128, H])]
        vb = SBt("vb", [128, H])
        sq = SBt("sq", [128, H]); qsb = SBt("qsb", [128, 16]); st2 = SBt("st2", [128, 16])
        ob_out = SBt("ob_out", [128, H])
        epsb2 = SBt("epsb2", [128, 1])
        pr = ctx.enter_context(nc.psum_tensor("pr2", [128, 1024], F32))
        blk = ctx.enter_context(nc.Block())

        def dma(e, o, i, s="ld"):
            semc[s] += 16
            def f(eng, o=o, i=i, h=semh[s]):
                eng.dma_start(out=o, in_=i).then_inc(h, 16)
            q[e].append(f)
            return semc[s]

        def wait(e, s, v=None):
            if v is None:
                v = semc[s]
            if v <= 0:
                return
            def f(eng, h=semh[s], v=v):
                eng.wait_ge(h, v)
            q[e].append(f)

        OWN2 = {"vector": "ve", "scalar": "ac"}

        def op(e, fn, s=None):
            if e in OWN2:
                own = OWN2[e]
                wait(e, own, semc[own])
                if s is None:
                    s = own
            if s is None:
                def f0(eng, fn=fn):
                    fn(eng)
                q[e].append(f0)
                return None
            semc[s] += 1
            def f(eng, fn=fn, h=semh[s]):
                fn(eng).then_inc(h, 1)
            q[e].append(f)
            return semc[s]

        op("vector", lambda e: e.memset(epsb2[:], EPS))
        dma("sync", bsb[:], bits, s="ld_m")
        dma("sync", ATs[:], AT, s="ld_m")
        dma("sync", ones_b[:], ones64, s="ld_m")
        dma("sync", qsb[:], qsel, s="ld_m")
        wait("vector", "ld_m")
        op("vector", lambda e: e.reduce_max(out=sel[:, 0:1], in_=bsb[:], axis=AX.X))
        op("vector", lambda e: e.tensor_single_scalar(out=sel[:, 1:2], in_=sel[:, 0:1],
                                                      scalar=0.01, op=ALU.is_gt))
        tv = op("vector", lambda e: e.tensor_scalar_mul(out=ATs[:], in0=ATs[:], scalar1=sel[:, 1:2]), "ve")
        wait("tensor", "ve", tv)
        copy_t = [0, 0]
        for ob in range(8):
            wait("tensor", "ve", copy_t[ob % 2])
            tp = op("tensor", lambda e, ob=ob: e.matmul(
                out=pr[:, (ob % 2) * 512:(ob % 2) * 512 + 512], lhsT=ones_b[:],
                rhs=ATs[:, ob * 512:(ob + 1) * 512], start=True, stop=True), "pe")
            wait("vector", "pe", tp)
            copy_t[ob % 2] = op("vector", lambda e, ob=ob: e.tensor_copy(
                out=rsel[:, ob * 512:(ob + 1) * 512],
                in_=pr[:, (ob % 2) * 512:(ob % 2) * 512 + 512]), "ve")

        lguard = [(0, 0), (0, 0)]
        dma("sync", dsb_[0][:], dense[0:128, :], s="ld_d0")
        dma("sync", x1b_[0][:], x1[0:128, :], s="ld_x0")
        for tt in range(16):
            bix = tt % 2
            dsb = dsb_[bix]; x1b = x1b_[bix]
            if tt < 15:
                nbix = (tt + 1) % 2
                wait("sync", "ve", lguard[nbix][0])
                wait("sync", "ac", lguard[nbix][1])
                dma("sync", dsb_[nbix][:], dense[(tt + 1) * 128:(tt + 2) * 128, :], s="ld_d%d" % nbix)
                dma("sync", x1b_[nbix][:], x1[(tt + 1) * 128:(tt + 2) * 128, :], s="ld_x%d" % nbix)
            wait("vector", "ld_d%d" % bix)
            op("vector", lambda e, tt=tt, dsb=dsb: e.tensor_scalar_mul(out=vb[:], in0=dsb[:],
                                                          scalar1=qsb[:, tt:tt + 1]))
            op("vector", lambda e: e.tensor_mul(out=vb[:], in0=vb[:], in1=rsel[:]))
            op("vector", lambda e: e.reduce_sum(out=st2[:, 0:1], in_=vb[:], axis=AX.X))
            op("vector", lambda e: e.scalar_tensor_tensor(out=sq[:], in0=vb[:], scalar=1.0,
                                                          in1=vb[:], op0=ALU.mult, op1=ALU.mult,
                                                          accum_out=st2[:, 1:2]))
            op("vector", lambda e: e.tensor_scalar_mul(out=st2[:, 2:3], in0=st2[:, 0:1],
                                                       scalar1=1.0 / H))
            op("vector", lambda e: e.tensor_copy(out=st2[:, 9:10], in_=st2[:, 2:3]))
            op("vector", lambda e: e.tensor_scalar_mul(out=st2[:, 3:4], in0=st2[:, 2:3],
                                                       scalar1=st2[:, 9:10]))
            op("vector", lambda e: e.tensor_scalar_mul(out=st2[:, 8:9], in0=st2[:, 1:2],
                                                       scalar1=1.0 / H))
            tv = op("vector", lambda e: e.tensor_single_scalar(
                out=st2[:, 4:5], in_=st2[:, 8:9], scalar=st2[:, 3:4], op=ALU.subtract), "ve")
            wait("scalar", "ve", tv)
            ts = op("scalar", lambda e: e.activation(out=st2[:, 5:6], in_=st2[:, 4:5],
                                                     func=AF.Sqrt, bias=epsb2[:]), "ac")
            wait("vector", "ac", ts)
            op("vector", lambda e: e.reciprocal(out=st2[:, 6:7], in_=st2[:, 5:6]))
            op("vector", lambda e: e.tensor_scalar_mul(out=st2[:, 10:11], in0=st2[:, 2:3],
                                                       scalar1=-1.0))
            tv = op("vector", lambda e: e.tensor_scalar_mul(out=st2[:, 7:8], in0=st2[:, 10:11],
                                                            scalar1=st2[:, 6:7]), "ve")
            wait("scalar", "ve", tv)
            ts = op("scalar", lambda e: e.activation(out=sq[:], in_=vb[:], func=AF.Identity,
                                                     scale=st2[:, 6:7], bias=st2[:, 7:8]), "ac")
            wait("vector", "ac", ts)
            wait("vector", "ld_x%d" % bix)
            wait("vector", "st_o")
            tv = op("vector", lambda e, x1b=x1b: e.tensor_add(out=ob_out[:], in0=x1b[:], in1=sq[:]), "ve")
            lguard[bix] = (semc["ve"], semc["ac"])
            wait("sync", "ve", tv)
            dma("sync", out[tt * 128:(tt + 1) * 128, :], ob_out[:], s="st_o")
        wait("sync", "st_o")

        for e in ENGS:
            getattr(blk, e)(lambda eng, qq=q[e]: [f(eng) for f in qq])
    return nc


def prep_shared(inputs):
    f4 = np.float32
    wq = np.ascontiguousarray(inputs["L0_attn_w_q"], f4)
    wk = np.ascontiguousarray(inputs["L0_attn_w_k"], f4)
    wv = np.ascontiguousarray(inputs["L0_attn_w_v"], f4)
    wo = np.ascontiguousarray(inputs["L0_attn_w_o"], f4)
    W = np.ascontiguousarray(inputs["L0_ffn_weight"], f4)
    bias = np.asarray(inputs["L0_ffn_bias"], f4)
    cent = np.ascontiguousarray(inputs["L0_ffn_centroids"], f4)
    assign = np.asarray(inputs["L0_ffn_assignments"])

    d = {}
    d["wq_t"] = np.ascontiguousarray(wq.reshape(32, 128, 32, 128).transpose(2, 1, 0, 3))
    wk_t = wk.reshape(32, 128, 4, 256).transpose(2, 1, 0, 3)
    wv_t = wv.reshape(32, 128, 4, 256).transpose(2, 1, 0, 3)
    d["wkv_t"] = np.ascontiguousarray(np.concatenate([wk_t, wv_t], axis=0))
    d["wo_t"] = np.ascontiguousarray(wo.reshape(32, 128, 8, 512).transpose(2, 1, 0, 3))
    wT = np.ascontiguousarray(W.T)
    d["wT_t"] = np.ascontiguousarray(wT.reshape(32, 128, 16, 256).transpose(2, 1, 0, 3))
    d["bias_i"] = np.ascontiguousarray(np.broadcast_to(bias[None, :], (128, H)))
    d["centT"] = np.ascontiguousarray(cent.T)
    d["identb"] = np.eye(128, dtype=ml_dtypes.bfloat16)
    d["identf"] = np.eye(128, dtype=np.float32)
    jj = np.arange(128)[None, :]
    pp = np.arange(128)[:, None]
    d["tri1"] = np.where(jj <= pp, 0.0, -1e9).astype(ml_dtypes.bfloat16)
    tri2_p0 = np.where(jj <= pp - 1, 0.0, -1e9).astype(ml_dtypes.bfloat16)
    d["tri2_p0"] = tri2_p0
    d["tri2_p1"] = d["tri1"]
    AT = np.zeros((64, H), np.float32)
    AT[np.asarray(assign, np.int64), np.arange(H)] = 1.0
    d["AT"] = AT.astype(ml_dtypes.bfloat16)
    d["ones64"] = np.ones((64, 128), dtype=ml_dtypes.bfloat16)
    return d


def core_inputs1(inputs, shared, c):
    b, p = c // 2, c % 2
    x = np.asarray(inputs["L0_attn_hidden_states"], np.float32)
    x0p = np.ascontiguousarray(np.concatenate([x[b, p::2], x[b, 1 - p::2]], axis=0))
    return {
        "x0": x0p, "wq_t": shared["wq_t"], "wkv_t": shared["wkv_t"],
        "wo_t": shared["wo_t"], "wT_t": shared["wT_t"], "bias_i": shared["bias_i"],
        "centT": shared["centT"], "tri1": shared["tri1"],
        "tri2": shared["tri2_p0"] if p == 0 else shared["tri2_p1"],
        "identb": shared["identb"], "identf": shared["identf"],
    }


def kernel(**inputs):
    shared = prep_shared(inputs)
    nc1 = build_launch1()
    in_maps1 = [core_inputs1(inputs, shared, c) for c in range(8)]
    res1 = run_bass_kernel_spmd(nc1, in_maps1, core_ids=list(range(8)))
    bits = np.stack([np.asarray(res1.results[c]["cmax"][:, 0], np.float32)
                     for c in range(8)], axis=1)
    bits = np.ascontiguousarray(bits)
    nc2 = build_launch2()
    in_maps2 = []
    for c in range(8):
        r = res1.results[c]
        in_maps2.append({
            "x1": np.asarray(r["x1"], np.float32),
            "dense": np.asarray(r["dense"]).view(ml_dtypes.bfloat16)
                     if r["dense"].dtype != ml_dtypes.bfloat16 else r["dense"],
            "qsel": np.asarray(r["qsel"], np.float32),
            "bits": bits, "AT": shared["AT"], "ones64": shared["ones64"],
        })
    res2 = run_bass_kernel_spmd(nc2, in_maps2, core_ids=list(range(8)))
    out = np.empty((B_, S_, H), np.float32)
    for c in range(8):
        b, p = c // 2, c % 2
        x2 = np.asarray(res2.results[c]["out"], np.float32)
        out[b, p::2, :] = x2
    return out


# version 07 (final)
# speedup vs baseline: 1.3395x; 1.0627x over previous
import numpy as np
import ml_dtypes
from contextlib import ExitStack
import concourse.bass as bass
import concourse.mybir as mybir
from concourse.bass_utils import run_bass_kernel_spmd

F32 = mybir.dt.float32
BF16 = mybir.dt.bfloat16
AF = mybir.ActivationFunctionType
ALU = mybir.AluOpType
AX = mybir.AxisListType

B_, S_, H = 4, 4096, 4096
NKV, D = 8, 128
TL, TF = 2048, 4096
NCH = H // 128
SCALE = float(1.0 / np.sqrt(128.0))
EPS = 1e-5

ENGS = ("sync", "tensor", "vector", "scalar", "gpsimd")
B_LIMIT = None
PHASE_MARKS = []
B_STAGE = 6

AB_CAP = 70144
ABO = dict(lnT=0, wsbA=32768, lnb=40960, kst=45056, vtm=45568, vst=46080,
           wsbB=32768, wsbB1=66048, khb=36864, vhb=40960, qs0=45056, qs1=47104,
           pbuf0=49152, pbuf1=53248, ptb0=57344, ptb1=61440, atst=65536,
           wosb=0, atb=32768, atb1=36864,
           lcC=40960, lcTt=36864,
           lch=0, wTsb=32768, wTsb1=49664, dstg=49152, biasb2=66048)
AF_CAP = 17408
AFO = dict(stA=0, xbuf0=8192, xbuf1=12288,
           stB=0, stB1=4096,
           stC=0, yev=16384,
           yfull=0, x0t=4096, x1t=8192, sq2=12288,
           stD0=0, stD1=8192, biasb=8192)


def build_launch1(dbg=(), phases="ABCD"):
    nc = bass.Bass("TRN2", target_bir_lowering=False, debug=False)

    def inp(name, shape, dt=F32):
        return nc.dram_tensor(name, shape, dt, kind="ExternalInput").ap()

    def outp(name, shape, dt=F32):
        return nc.dram_tensor(name, shape, dt, kind="ExternalOutput").ap()

    def scr(name, shape, dt):
        kind = "ExternalOutput" if name in dbg else "Internal"
        return nc.dram_tensor(name, shape, dt, kind=kind).ap()

    x0 = inp("x0", [TF, H])
    wq_t = inp("wq_t", [32, 128, NCH, 128])
    wkv_t = inp("wkv_t", [8, 128, NCH, 256])
    wo_t = inp("wo_t", [8, 128, NCH, 512])
    wT_t = inp("wT_t", [16, 128, NCH, 256])
    bias_i = inp("bias_i", [128, H])
    centT = inp("centT", [H, 64])
    tri1 = inp("tri1", [128, 128], BF16); tri2 = inp("tri2", [128, 128], BF16)
    identb = inp("identb", [128, 128], BF16)
    identf = inp("identf", [128, 128], F32)

    kT = scr("kT", [NKV, 128, TF], BF16)
    vT = scr("vT", [NKV, 128, 32, 128], BF16)
    lnTs = scr("lnTs", [2, 128, NCH * 1024], BF16)
    at3 = scr("at3", [16, 128, NCH, 128], BF16)
    y_s = scr("y_s", [16, 128, H], F32)
    lcT4 = scr("lcT4", [16, 128, NCH, 128], BF16)

    x1o = outp("x1", [TL, H], F32)
    denseo = outp("dense", [TL, H], BF16)
    qselo = outp("qsel", [128, 16], F32)
    cmaxo = outp("cmax", [64, 1], F32)

    q = {e: [] for e in ENGS}
    pbuf_guard = [0]
    at_store = [0]
    yev_store = [0]
    ctx = ExitStack()
    with ctx:
        semh, semc = {}, {}
        for n in ["ld_x", "ld_w", "ld_w2", "ld_k", "ld_v", "ld_l", "ld_a", "ld_y", "ld_m",
                  "st_kv", "st_ln", "st_at", "st_y", "st_x1", "st_lct", "st_d", "st_o",
                  "pe", "ve", "ac", "gp", "bar", "arn0", "arn1", "arn2", "arn3", "trc", "ptc"]:
            semh[n] = ctx.enter_context(nc.semaphore(n)); semc[n] = 0

        AB = ctx.enter_context(nc.sbuf_tensor("AB", [128, AB_CAP], BF16))
        AFm = ctx.enter_context(nc.sbuf_tensor("AFm", [128, AF_CAP], F32))
        tri1b = ctx.enter_context(nc.sbuf_tensor("tri1b", [128, 128], BF16))
        tri2b = ctx.enter_context(nc.sbuf_tensor("tri2b", [128, 128], BF16))
        idb = ctx.enter_context(nc.sbuf_tensor("idb", [128, 128], BF16))
        idf = ctx.enter_context(nc.sbuf_tensor("idf", [128, 128], F32))
        stat = ctx.enter_context(nc.sbuf_tensor("stat", [128, 16], F32))
        mxb = ctx.enter_context(nc.sbuf_tensor("mxb", [128, 16], F32))
        sab = ctx.enter_context(nc.sbuf_tensor("sab", [128, 16], F32))
        fcb = ctx.enter_context(nc.sbuf_tensor("fcb", [128, 16], F32))
        qsel_sb = ctx.enter_context(nc.sbuf_tensor("qsel_sb", [128, 16], F32))
        cmax_sb = ctx.enter_context(nc.sbuf_tensor("cmax_sb", [64, 4], F32))
        dsm = ctx.enter_context(nc.sbuf_tensor("dsm", [128, 80], F32))
        epsb = ctx.enter_context(nc.sbuf_tensor("epsb", [128, 1], F32))
        lgb = ctx.enter_context(nc.sbuf_tensor("lgb", [128, 128], F32))

        arena = ctx.enter_context(nc.psum_tensor("arena", [128, 1536], F32))
        trB = ctx.enter_context(nc.psum_tensor("trBp", [128, 1024], BF16))
        trB2 = ctx.enter_context(nc.psum_tensor("trBp2", [128, 1024], BF16))
        avP = ctx.enter_context(nc.psum_tensor("avPp", [128, 512], F32))
        qpP = ctx.enter_context(nc.psum_tensor("qpPp", [128, 1024], F32))

        blk = ctx.enter_context(nc.Block())

        def ab(name, n):
            return AB[:, ABO[name]:ABO[name] + n]

        def af(name, n):
            return AFm[:, AFO[name]:AFO[name] + n]

        def dma(e, out, in_, s="ld", **kw):
            semc[s] += 16
            h, tgt = semh[s], semc[s]
            def f(eng, out=out, in_=in_, h=h, kw=kw):
                eng.dma_start(out=out, in_=in_, **kw).then_inc(h, 16)
            q[e].append(f)
            return tgt

        def wait(e, s, v=None):
            if v is None:
                v = semc[s]
            if v <= 0:
                return
            def f(eng, h=semh[s], v=v):
                eng.wait_ge(h, v)
            q[e].append(f)

        def wait_all_dma(e):
            for s2 in ("ld_x", "ld_w", "ld_w2", "ld_k", "ld_v", "ld_l", "ld_a", "ld_y", "ld_m",
                       "st_kv", "st_ln", "st_at", "st_y", "st_x1", "st_lct", "st_d", "st_o"):
                wait(e, s2)

        OWN = {"vector": "ve", "scalar": "ac", "gpsimd": "gp"}

        def op(e, fn, s=None, serial=True):
            if e in ("vector", "scalar"):
                own = OWN[e]
                if serial:
                    wait(e, own, semc[own])
                if s is None:
                    s = own
            if s is None:
                def f0(eng, fn=fn):
                    fn(eng)
                q[e].append(f0)
                return None
            semc[s] += 1
            h, tgt = semh[s], semc[s]
            def f(eng, fn=fn, h=h):
                fn(eng).then_inc(h, 1)
            q[e].append(f)
            return tgt

        def mm(out, lhsT, rhs, first, last, s="pe"):
            if last:
                semc[s] += 1
                def f(eng, out=out, lhsT=lhsT, rhs=rhs, first=first, h=semh[s]):
                    eng.matmul(out=out, lhsT=lhsT, rhs=rhs, start=first,
                               stop=True).then_inc(h, 1)
            else:
                def f(eng, out=out, lhsT=lhsT, rhs=rhs, first=first):
                    eng.matmul(out=out, lhsT=lhsT, rhs=rhs, start=first, stop=False)
            q["tensor"].append(f)
            return semc[s]

        def tposes(srcs, dsts_in_trB, last_inc=True, bank=None):
            n = len(srcs)
            bk = bank if bank is not None else trB
            for i, src in enumerate(srcs):
                last = (i == n - 1) and last_inc
                if last:
                    semc["pe"] += 1
                    def f(eng, src=src, i=dsts_in_trB[i], h=semh["pe"], bk=bk):
                        eng.transpose(out=bk[:, i * 128:(i + 1) * 128], in_=src,
                                      identity=idb[:]).then_inc(h, 1)
                else:
                    def f(eng, src=src, i=dsts_in_trB[i], bk=bk):
                        eng.transpose(out=bk[:, i * 128:(i + 1) * 128], in_=src,
                                      identity=idb[:])
                q["tensor"].append(f)
            return semc["pe"]

        def barrier():
            wait_all_dma("sync")
            for e in ENGS:
                semc["bar"] += 1
                def f(eng, h=semh["bar"]):
                    eng.nop().then_inc(h, 1)
                q[e].append(f)
            v = semc["bar"]
            for e in ENGS:
                wait(e, "bar", v)

        def ln_stats(src):
            sq = CUR_SQ[0]
            op("vector", lambda e, src=src: e.reduce_sum(out=stat[:, 0:1], in_=src, axis=AX.X))
            op("vector", lambda e, src=src, sq=sq: e.scalar_tensor_tensor(
                out=sq, in0=src, scalar=1.0, in1=src, op0=ALU.mult, op1=ALU.mult,
                accum_out=stat[:, 1:2]))
            op("vector", lambda e: e.tensor_scalar_mul(out=stat[:, 2:3], in0=stat[:, 0:1],
                                                       scalar1=1.0 / H))
            op("vector", lambda e: e.tensor_copy(out=stat[:, 12:13], in_=stat[:, 2:3]))
            op("vector", lambda e: e.tensor_scalar_mul(out=stat[:, 3:4], in0=stat[:, 2:3],
                                                       scalar1=stat[:, 12:13]))
            op("vector", lambda e: e.tensor_scalar_mul(out=stat[:, 13:14], in0=stat[:, 1:2],
                                                       scalar1=1.0 / H))
            tv = op("vector", lambda e: e.tensor_single_scalar(
                out=stat[:, 4:5], in_=stat[:, 13:14], scalar=stat[:, 3:4], op=ALU.subtract))
            wait("scalar", "ve", tv)
            ts = op("scalar", lambda e: e.activation(out=stat[:, 5:6], in_=stat[:, 4:5],
                                                     func=AF.Sqrt, bias=epsb[:]))
            wait("vector", "ac", ts)
            op("vector", lambda e: e.reciprocal(out=stat[:, 6:7], in_=stat[:, 5:6]))
            op("vector", lambda e: e.tensor_scalar_mul(out=stat[:, 14:15], in0=stat[:, 2:3],
                                                       scalar1=-1.0))
            tv = op("vector", lambda e: e.tensor_scalar_mul(out=stat[:, 7:8], in0=stat[:, 14:15],
                                                            scalar1=stat[:, 6:7]))
            wait("scalar", "ve", tv)

        def ln_apply(src, dst):
            return op("scalar", lambda e, src=src, dst=dst: e.activation(
                out=dst, in_=src, func=AF.Identity, scale=stat[:, 6:7],
                bias=stat[:, 7:8]))

        CUR_SQ = [ab("lnb", 4096)]

        dma("sync", tri1b[:], tri1, s="ld_m")
        dma("sync", tri2b[:], tri2, s="ld_m")
        dma("sync", idb[:], identb, s="ld_m")
        t0 = dma("sync", idf[:], identf, s="ld_m")
        wait("tensor", "ld_m")
        wait("vector", "ld_m")
        wait("scalar", "ld_m")
        op("vector", lambda e: e.memset(cmax_sb[:, 0:1], -1e30), "ve")
        op("vector", lambda e: e.memset(epsb[:], EPS), "ve")
        op("vector", lambda e: e.memset(qsel_sb[:], 0.0), "ve")

        lnT3 = ab("lnT", 32768).rearrange("p (c t) -> p c t", t=1024)
        trB3 = trB[:].rearrange("p (k q) -> p k q", q=128)

        PHASE_MARKS.append(("preamble_end", {e: len(q[e]) for e in ENGS}))
        if "A" in phases:
            CUR_SQ[0] = ab("lnb", 4096)
            xguard = [(0, 0), (0, 0)]
            xld = ["ld_x", "ld_y"]
            kvgp = [0, 0]
            kvpe = [0, 0]
            for sset in range(2):
                row0 = TL if sset == 0 else 0
                tok0 = TL if sset == 0 else 0
                for half in range(2):
                    wait("vector", "st_ln")
                    for tloc in range(8):
                        r0 = row0 + half * 1024 + tloc * 128
                        bix = tloc % 2
                        xb = af("xbuf%d" % bix, 4096)
                        wait("sync", "ve", xguard[bix][0])
                        wait("sync", "ac", xguard[bix][1])
                        dma("sync", xb, x0[r0:r0 + 128, :], s=xld[bix])
                        wait("vector", xld[bix])
                        wait("vector", "pe", semc["pe"])
                        ln_stats(xb)
                        wait("scalar", "pe", semc["pe"])
                        ta = ln_apply(xb, ab("lnb", 4096))
                        xguard[bix] = (semc["ve"], semc["ac"])
                        wait("tensor", "ac", ta)
                        for bch in range(4):
                            wait("tensor", "trc", semc["trc"])
                            tp = tposes([ab("lnb", 4096)[:, (bch * 8 + k) * 128:(bch * 8 + k + 1) * 128]
                                         for k in range(8)], list(range(8)))
                            wait("vector", "pe", tp)
                            dst = lnT3[:, bch * 8:(bch + 1) * 8, tloc * 128:(tloc + 1) * 128]
                            op("vector", lambda e, dst=dst: e.tensor_copy(out=dst, in_=trB3[:, 0:8, :]), "trc",
                               serial=False)
                    if sset == 1:
                        wait("sync", "trc", semc["trc"])
                        dma("sync", lnTs[half], ab("lnT", 32768), s="st_ln")
                    wait("sync", "ve", semc["ve"])
                    wait("sync", "ac", semc["ac"])
                    for sl in range(8):
                        slot = sl % 2
                        wof = (32768, 49152)[slot]
                        stn = ("stA", "xbuf0")[slot]
                        wstr = ("ld_w", "ld_w2")[slot]
                        if sl == 0:
                            dma("sync", af(stn, 8192), wkv_t[sl].rearrange("p a n -> p (a n)"), s=wstr)
                        if sl < 7:
                            nslot = (sl + 1) % 2
                            nstn = ("stA", "xbuf0")[nslot]
                            wait("sync", "gp", kvgp[nslot])
                            dma("sync", af(nstn, 8192),
                                wkv_t[sl + 1].rearrange("p a n -> p (a n)"), s=("ld_w", "ld_w2")[nslot])
                        wait("gpsimd", wstr)
                        wait("gpsimd", "pe", kvpe[slot])
                        tg = op("gpsimd", lambda e, wof=wof, stn=stn: e.tensor_copy(
                            out=AB[:, wof:wof + 8192], in_=af(stn, 8192)), "gp")
                        kvgp[slot] = semc["gp"]
                        wsA = AB[:, wof:wof + 8192].rearrange("p (a n) -> p a n", n=256)
                        wait("tensor", "gp", tg)
                        wait("tensor", "trc", semc["trc"])
                        for grp in range(2):
                            for dc in range(2):
                                pk = qpP[:, dc * 512:(dc + 1) * 512]
                                wait("tensor", "ve", semc["ve"])
                                for ch in range(NCH):
                                    tpk = mm(pk, wsA[:, ch, dc * 128:(dc + 1) * 128],
                                             lnT3[:, ch, grp * 512:(grp + 1) * 512],
                                             ch == 0, ch == NCH - 1)
                                wait("vector", "pe", tpk)
                                hh = (sl % 4) * 2 + dc
                                c0 = tok0 + half * 1024 + grp * 512
                                if sl < 4:
                                    wait("vector", "st_kv")
                                    tv = op("vector", lambda e, pk=pk: e.tensor_copy(
                                        out=ab("kst", 512), in_=pk), "ve", serial=False)
                                    wait("sync", "ve", tv)
                                    dma("sync", kT[hh][:, c0:c0 + 512],
                                        ab("kst", 512), s="st_kv")
                                else:
                                    wait("vector", "st_kv")
                                    tv = op("vector", lambda e, pk=pk: e.tensor_copy(
                                        out=ab("vtm", 512), in_=pk), "ve", serial=False)
                                    wait("tensor", "ve", tv)
                                    wait("tensor", "trc", semc["trc"])
                                    tp = tposes([ab("vtm", 512)[:, j * 128:(j + 1) * 128]
                                                 for j in range(4)], list(range(4)))
                                    wait("vector", "pe", tp)
                                    tv = op("vector", lambda e: e.tensor_copy(
                                        out=ab("vst", 512), in_=trB[:, 0:512]), "trc", serial=False)
                                    wait("sync", "trc", tv)
                                    kt0 = c0 // 128
                                    dst = vT[hh, :, kt0:kt0 + 4, :].rearrange("p j c -> p (j c)")
                                    dma("sync", dst, ab("vst", 512), s="st_kv")
                        kvpe[slot] = semc["pe"]
            barrier()

        PHASE_MARKS.append(("before_B", {e: len(q[e]) for e in ENGS}))
        if "B" in phases:
            LB = B_LIMIT or (2, 8, 4, 8)
            arn_cnt = {0: 0, 1: 0, 2: 0}
            pbg = {0: 0, 1: 0}
            ptcg = {0: 0, 1: 0}
            rrctr = [0]
            wqgp = {0: 0, 1: 0}
            wqpe = {0: 0, 1: 0}
            trguard = {0: ("ve", 0), 1: ("ac", 0)}
            qb_guard = [0, 0]
            for half in (LB[0] if isinstance(LB[0], (list, tuple)) else range(LB[0])):
                wait("sync", "pe", semc["pe"])
                dma("sync", ab("lnT", 32768), lnTs[half], s="ld_l")
                for h in range(LB[1]):
                    wait("sync", "pe", semc["pe"])
                    dma("sync", ab("khb", 4096), kT[h], s="ld_k")
                    dma("sync", ab("vhb", 4096), vT[h].rearrange("p k c -> p (k c)"), s="ld_v")
                    for g in range(LB[2]):
                        qh = h * 4 + g
                        qpar = qh % 2
                        qb = ab("qs0", 1024) if qpar == 0 else ab("qs1", 1024)
                        stBn = ("stB", "stB1")[qpar]
                        wsn = ("wsbB", "wsbB1")[qpar]
                        wstr = ("ld_w", "ld_w2")[qpar]
                        if qh == h * 4 and half == 0 and h == 0:
                            dma("sync", af(stBn, 4096), wq_t[qh].rearrange("p a n -> p (a n)"), s=wstr)
                        nqh = qh + 1 if g < LB[2] - 1 else (h + 1) * 4
                        if nqh < 32 or half == 0:
                            nqh2 = nqh % 32
                            npar = nqh2 % 2
                            wait("sync", "gp", wqgp[npar])
                            dma("sync", af(("stB", "stB1")[npar], 4096),
                                wq_t[nqh2].rearrange("p a n -> p (a n)"), s=("ld_w", "ld_w2")[npar])
                        wait("gpsimd", wstr)
                        wait("gpsimd", "pe", wqpe[qpar])
                        tg = op("gpsimd", lambda e, wsn=wsn, stBn=stBn: e.tensor_copy(
                            out=ab(wsn, 4096), in_=af(stBn, 4096)), "gp")
                        wqgp[qpar] = semc["gp"]
                        wsB = ab(wsn, 4096).rearrange("p (a n) -> p a n", n=128)
                        wait("tensor", "gp", tg)
                        wait("tensor", "ld_l")
                        wait("tensor", "pe", qb_guard[qh % 2])
                        wait("tensor", "ve", semc["ve"])
                        for tgrp in range(2):
                            pk = qpP[:, tgrp * 512:(tgrp + 1) * 512]
                            for ch in range(NCH):
                                tpk = mm(pk, wsB[:, ch, :],
                                         lnT3[:, ch, tgrp * 512:(tgrp + 1) * 512],
                                         ch == 0, ch == NCH - 1)
                            wait("vector", "pe", tpk)
                            op("vector", lambda e, pk=pk, qb=qb, tgrp=tgrp: e.tensor_scalar_mul(
                                out=qb[:, tgrp * 512:(tgrp + 1) * 512], in0=pk, scalar1=SCALE), "ve",
                               serial=False)
                        tqev = semc["ve"]
                        wqpe[qpar] = semc["pe"]
                        wait("tensor", "ve", tqev)
                        wait("tensor", "ld_k")
                        wait("tensor", "ld_v")
                        def softmax_stage(tloc, t, par):
                            fks = [("L", j) for j in range(t + 1)] + [("P", j) for j in range(t + 1)]
                            nfk = len(fks)
                            nr = (nfk + 3) // 4
                            pb = ab("pbuf%d" % par, 4096)
                            mb = par * 8
                            for r in range(nr):
                                sub = fks[r * 4:(r + 1) * 4]
                                hb = rrctr[0] % 3
                                rrctr[0] += 1
                                abh = arena[:, hb * 512:hb * 512 + 512]
                                wait("tensor", "ac", arn_cnt[hb])
                                kk = 0
                                while kk < len(sub):
                                    side, j0 = sub[kk]
                                    run = 1
                                    while (kk + run < len(sub) and sub[kk + run][0] == side
                                           and sub[kk + run][1] == j0 + run):
                                        run += 1
                                    base = (0 if side == "L" else 2048) + j0 * 128
                                    tpk = mm(abh[:, kk * 128:(kk + run) * 128],
                                             qb[:, tloc * 128:(tloc + 1) * 128],
                                             ab("khb", 4096)[:, base:base + run * 128],
                                             True, True)
                                    kk += run
                                wait("vector", "pe", tpk)
                                for kk2, (side, j) in enumerate(sub):
                                    if j == t:
                                        trib = tri1b if side == "L" else tri2b
                                        op("vector", lambda e, kk2=kk2, abh=abh, trib=trib: e.tensor_add(
                                            out=abh[:, kk2 * 128:(kk2 + 1) * 128],
                                            in0=abh[:, kk2 * 128:(kk2 + 1) * 128], in1=trib[:]),
                                           serial=False)
                                used = len(sub) * 128
                                tv = op("vector", lambda e, abh=abh, used=used, r=r, mb=mb: e.reduce_max(
                                    out=mxb[:, mb + r:mb + r + 1], in_=abh[:, 0:used], axis=AX.X,
                                    negate=True))
                                wait("scalar", "ve", tv)
                                if r == 0:
                                    wait("scalar", "pe", pbg[par])
                                wait("scalar", "ac", semc["ac"])
                                semc["ac"] += 1
                                arn_cnt[hb] = semc["ac"]
                                def fe(eng, abh=abh, used=used, r=r, mb=mb, pb=pb, h2=semh["ac"]):
                                    eng.activation(out=pb[:, r * 512:r * 512 + used],
                                                   in_=abh[:, 0:used], func=AF.Exp,
                                                   bias=mxb[:, mb + r:mb + r + 1],
                                                   accum_out=sab[:, mb + r:mb + r + 1]).then_inc(h2, 1)
                                q["scalar"].append(fe)
                            wait("vector", "ac", semc["ac"])
                            op("vector", lambda e, nr=nr, mb=mb: e.tensor_scalar_mul(
                                out=fcb[:, mb:mb + nr], in0=mxb[:, mb:mb + nr], scalar1=-1.0))
                            tv = op("vector", lambda e, nr=nr, mb=mb: e.reduce_max(
                                out=stat[:, 8:9], in_=fcb[:, mb:mb + nr], axis=AX.X, negate=True))
                            wait("scalar", "ve", tv)
                            ts = op("scalar", lambda e, nr=nr, mb=mb: e.activation(
                                out=fcb[:, mb:mb + nr], in_=fcb[:, mb:mb + nr], func=AF.Exp,
                                bias=stat[:, 8:9]))
                            wait("vector", "ac", ts)
                            for r in range(nr):
                                op("vector", lambda e, r=r, mb=mb: e.tensor_scalar_mul(
                                    out=sab[:, mb + r:mb + r + 1], in0=sab[:, mb + r:mb + r + 1],
                                    scalar1=fcb[:, mb + r:mb + r + 1]))
                            op("vector", lambda e, nr=nr, mb=mb: e.reduce_sum(
                                out=stat[:, 9:10], in_=sab[:, mb:mb + nr], axis=AX.X))
                            op("vector", lambda e: e.reciprocal(out=stat[:, 10:11], in_=stat[:, 9:10]))
                            op("vector", lambda e, nr=nr, mb=mb: e.tensor_scalar_mul(
                                out=fcb[:, mb:mb + nr], in0=fcb[:, mb:mb + nr], scalar1=stat[:, 10:11]))
                            tnorm = None
                            for r in range(nr):
                                used = min(4, nfk - r * 4) * 128
                                tnorm = op("vector", lambda e, r=r, used=used, pb=pb, mb=mb: e.tensor_scalar_mul(
                                    out=pb[:, r * 512:r * 512 + used],
                                    in0=pb[:, r * 512:r * 512 + used],
                                    scalar1=fcb[:, mb + r:mb + r + 1]), "ve")
                            return (nfk, tnorm)

                        def txav_stage(t, tloc, par, info):
                            nfk, tnorm = info
                            pb = ab("pbuf%d" % par, 4096)
                            ptb = ab("ptb%d" % par, 4096)
                            wait("tensor", "ve", tnorm)
                            wait("tensor", "ptc", ptcg[par])
                            ctargets = []
                            for bt in range((nfk + 7) // 8):
                                bsel = bt % 2
                                bk = trB if bsel == 0 else trB2
                                gsem, gval = trguard[bsel]
                                wait("tensor", gsem, gval)
                                cnt = min(8, nfk - bt * 8)
                                tp = tposes([pb[:, (bt * 8 + k) * 128:(bt * 8 + k + 1) * 128]
                                             for k in range(cnt)], list(range(cnt)), bank=bk)
                                ceng = "vector" if bsel == 0 else "scalar"
                                wait(ceng, "pe", tp)
                                if ceng == "vector":
                                    tc = op("vector", lambda e, bt=bt, cnt=cnt, ptb=ptb, bk=bk: e.tensor_copy(
                                        out=ptb[:, bt * 1024:bt * 1024 + cnt * 128],
                                        in_=bk[:, 0:cnt * 128]), serial=False)
                                    trguard[bsel] = ("ve", tc)
                                    ctargets.append(("ve", tc))
                                else:
                                    tc = op("scalar", lambda e, bt=bt, cnt=cnt, ptb=ptb, bk=bk: e.activation(
                                        out=ptb[:, bt * 1024:bt * 1024 + cnt * 128],
                                        in_=bk[:, 0:cnt * 128], func=AF.Identity), serial=False)
                                    trguard[bsel] = ("ac", tc)
                                    ctargets.append(("ac", tc))
                            pbg[par] = semc["pe"]
                            for gs, gv in ctargets:
                                wait("tensor", gs, gv)
                            fks = [("L", j) for j in range(t + 1)] + [("P", j) for j in range(t + 1)]
                            for i_fk, (side, j) in enumerate(fks):
                                vcol = (j if side == "L" else 16 + j) * 128
                                tav = mm(avP[:, (tloc % 4) * 128:(tloc % 4) * 128 + 128],
                                         ab("vhb", 4096)[:, vcol:vcol + 128],
                                         ptb[:, i_fk * 128:(i_fk + 1) * 128],
                                         i_fk == 0, i_fk == nfk - 1)
                            semc["ptc"] += 1
                            def fpt(eng, h2=semh["ptc"]):
                                eng.nop().then_inc(h2, 1)
                            q["tensor"].append(fpt)
                            ptcg[par] = semc["ptc"]
                            if tloc % 4 == 3:
                                wait("vector", "pe", tav)
                                wait("vector", "st_at")
                                tv = op("vector", lambda e: e.tensor_copy(out=ab("atst", 512),
                                                                          in_=avP[:]), "ve", serial=False)
                                wait("sync", "ve", tv)
                                Gt = t // 4
                                dst = at3[4 * Gt:4 * Gt + 4, :, qh, :].rearrange("j p q -> p j q")
                                src = ab("atst", 512).rearrange("p (j q) -> p j q", q=128)
                                dma("sync", dst, src, s="st_at")
                                wait("tensor", "ve", tv)

                        pend = None
                        for tloc in range(LB[3]):
                            t = half * 8 + tloc
                            par = tloc % 2
                            info = softmax_stage(tloc, t, par)
                            if pend is not None:
                                txav_stage(*pend)
                            pend = (t, tloc, par, info)
                        if pend is not None:
                            txav_stage(*pend)
                        qb_guard[qh % 2] = semc["pe"]
            barrier()

        PHASE_MARKS.append(("before_C1", {e: len(q[e]) for e in ENGS}))
        if "C" in phases:
            aguard = [0, 0]
            wo3 = ab("wosb", 32768).rearrange("p (a n) -> p a n", n=1024)
            for qq in range(4):
                for piece in range(2):
                    wait("sync", "gp", semc["gp"])
                    dma("sync", af("stC", 16384), wo_t[qq * 2 + piece].rearrange("p a n -> p (a n)"), s="ld_w")
                    wait("gpsimd", "ld_w")
                    wait("gpsimd", "pe", semc["pe"])
                    src = af("stC", 16384).rearrange("p (a n) -> p a n", n=512)
                    dst = wo3[:, :, piece * 512:(piece + 1) * 512]
                    tg = op("gpsimd", lambda e, src=src, dst=dst: e.tensor_copy(out=dst, in_=src), "gp")
                for tt in range(16):
                    bix = tt % 2
                    anames = ("atb", "atb1")
                    astr = ("ld_a", "ld_k")
                    if qq == 0 and tt == 0:
                        dma("sync", ab(anames[0], 4096), at3[0].rearrange("p c q -> p (c q)"), s=astr[0])
                    if tt < 15 or qq < 3:
                        ntt = (tt + 1) % 16
                        nbix = (tt + 1) % 2
                        wait("sync", "pe", aguard[nbix])
                        dma("sync", ab(anames[nbix], 4096),
                            at3[ntt].rearrange("p c q -> p (c q)"), s=astr[nbix])
                    wait("tensor", astr[bix])
                    wait("tensor", "gp", semc["gp"])
                    wait("tensor", "ve", semc["ve"])
                    for dq in range(NCH):
                        for obi in range(2):
                            tpk = mm(arena[:, obi * 512:(obi + 1) * 512],
                                     ab(anames[bix], 4096)[:, dq * 128:(dq + 1) * 128],
                                     wo3[:, dq, obi * 512:(obi + 1) * 512],
                                     dq == 0, dq == NCH - 1)
                    aguard[bix] = semc["pe"]
                    wait("vector", "pe", tpk)
                    wait("vector", "st_y")
                    op("vector", lambda e: e.tensor_copy(out=af("yev", 1024)[:, 0:512],
                                                         in_=arena[:, 0:512]), serial=False)
                    tv = op("vector", lambda e: e.tensor_copy(out=af("yev", 1024)[:, 512:1024],
                                                              in_=arena[:, 512:1024]), "ve", serial=False)
                    wait("sync", "ve", tv)
                    dma("sync", y_s[tt][:, qq * 1024:(qq + 1) * 1024],
                        af("yev", 1024), s="st_y")
            barrier()

        PHASE_MARKS.append(("before_C2", {e: len(q[e]) for e in ENGS}))
        if "C" in phases:
            centb = AB[:, 0:2048]
            for piece in range(2):
                dma("sync", af("yev", 1024).rearrange("p (a c) -> p a c", c=64),
                    centT.rearrange("(a p) c -> p a c", p=128)[:, piece * 16:(piece + 1) * 16, :],
                    s="ld_m")
                wait("gpsimd", "ld_m")
                tgc = op("gpsimd", lambda e, piece=piece: e.tensor_copy(
                    out=centb[:, piece * 1024:(piece + 1) * 1024], in_=af("yev", 1024)), "gp")
                wait("sync", "gp", tgc)
            cb3 = centb.rearrange("p (a c) -> p a c", c=64)
            x1_store = 0
            lct_store = 0
            for tt in range(16):
                wait("sync", "ve", semc["ve"])
                wait("sync", "ac", semc["ac"])
                dma("sync", af("yfull", 4096), y_s[tt], s="ld_y")
                dma("sync", af("x0t", 4096), x0[tt * 128:(tt + 1) * 128, :], s="ld_x")
                wait("vector", "ld_y")
                CUR_SQ[0] = af("sq2", 4096)
                ln_stats(af("yfull", 4096))
                ta = ln_apply(af("yfull", 4096), af("sq2", 4096))
                wait("vector", "ac", ta)
                wait("vector", "ld_x")
                wait("vector", "st_x1")
                wait("vector", "ac", semc["ac"])
                tv = op("vector", lambda e: e.tensor_add(out=af("x1t", 4096), in0=af("x0t", 4096),
                                                         in1=af("sq2", 4096)), "ve")
                wait("sync", "ve", tv)
                dma("sync", x1o[tt * 128:(tt + 1) * 128, :], af("x1t", 4096), s="st_x1")
                ln_stats(af("x1t", 4096))
                wait("scalar", "pe", semc["pe"])
                ta = ln_apply(af("x1t", 4096), ab("lcC", 4096))
                wait("tensor", "ac", ta)
                wait("tensor", "st_lct")
                for bch in range(4):
                    wait("tensor", "trc", semc["trc"])
                    tp = tposes([ab("lcC", 4096)[:, (bch * 8 + k) * 128:(bch * 8 + k + 1) * 128]
                                 for k in range(8)], list(range(8)))
                    wait("vector", "pe", tp)
                    dst = ab("lcTt", 4096).rearrange("p (c t) -> p c t", t=128)[:, bch * 8:(bch + 1) * 8, :]
                    op("vector", lambda e, dst=dst: e.tensor_copy(out=dst, in_=trB3[:, 0:8, :]), "trc")
                wait("sync", "trc", semc["trc"])
                dma("sync", lcT4[tt].rearrange("p c q -> p (c q)"), ab("lcTt", 4096), s="st_lct")
                wait("tensor", "trc", semc["trc"])
                for ch in range(NCH):
                    tpk = mm(arena[0:64, 1024:1152], cb3[:, ch, :],
                             ab("lcTt", 4096)[:, ch * 128:(ch + 1) * 128],
                             ch == 0, ch == NCH - 1)
                wait("vector", "pe", tpk)
                tv = op("vector", lambda e: e.tensor_copy(out=lgb[0:64, :], in_=arena[0:64, 1024:1152]), "ve")
                wait("tensor", "ve", tv)
                semc["pe"] += 1
                def ftr1(eng, h=semh["pe"]):
                    eng.transpose(out=arena[:, 512:640], in_=lgb[:], identity=idf[:]).then_inc(h, 1)
                q["tensor"].append(ftr1)
                wait("vector", "pe", semc["pe"])
                tv = op("vector", lambda e: e.reduce_max(out=dsm[:, 64:65], in_=arena[:, 512:576],
                                                         axis=AX.X, negate=True), "ve")
                tv = op("vector", lambda e: e.tensor_scalar_mul(out=dsm[:, 65:66], in0=dsm[:, 64:65],
                                                                scalar1=10.0), "ve")
                wait("scalar", "ve", tv)
                ts = op("scalar", lambda e: e.activation(out=dsm[:, 0:64], in_=arena[:, 512:576],
                                                         func=AF.Exp, scale=10.0, bias=dsm[:, 65:66],
                                                         accum_out=dsm[:, 66:67]), "ac")
                wait("vector", "ac", ts)
                op("vector", lambda e: e.reciprocal(out=dsm[:, 67:68], in_=dsm[:, 66:67]))
                op("vector", lambda e: e.tensor_scalar_mul(out=dsm[:, 0:64], in0=dsm[:, 0:64],
                                                       scalar1=dsm[:, 67:68]))
                op("vector", lambda e: e.reduce_max(out=dsm[:, 68:69], in_=dsm[:, 0:64], axis=AX.X))
                tv = op("vector", lambda e, tt=tt: e.tensor_single_scalar(
                    out=qsel_sb[:, tt:tt + 1], in_=dsm[:, 68:69], scalar=0.01, op=ALU.is_gt), "ve")
                wait("tensor", "ve", tv)
                semc["pe"] += 1
                def ftr2(eng, h=semh["pe"]):
                    eng.transpose(out=avP[0:64, 0:128], in_=dsm[:, 0:64],
                                  identity=idf[:]).then_inc(h, 1)
                q["tensor"].append(ftr2)
                wait("vector", "pe", semc["pe"])
                op("vector", lambda e: e.reduce_max(out=cmax_sb[:, 1:2],
                                                    in_=avP[0:64, 0:128], axis=AX.X))
                op("vector", lambda e: e.tensor_copy(out=cmax_sb[:, 2:3], in_=cmax_sb[:, 0:1]))
                op("vector", lambda e: e.tensor_scalar_max(out=cmax_sb[:, 0:1],
                                                           in0=cmax_sb[:, 1:2],
                                                           scalar1=cmax_sb[:, 2:3]))
            tv = semc["ve"]
            wait("sync", "ve", tv)
            dma("sync", qselo, qsel_sb[:], s="st_o")
            dma("sync", cmaxo, cmax_sb[:, 0:1], s="st_o")
            barrier()

        PHASE_MARKS.append(("before_D", {e: len(q[e]) for e in ENGS}))
        if "D" in phases:
            dma("sync", af("stD0", 4096), bias_i, s="ld_m")
            wait("gpsimd", "ld_m")
            tgb = op("gpsimd", lambda e: e.tensor_copy(out=ab("biasb2", 4096), in_=af("stD0", 4096)), "gp")
            wait("sync", "gp", tgb)
            dguard = [0, 0]
            fillgp = [0, 0]
            for half in range(2):
                wait("sync", "pe", semc["pe"])
                for tt8 in range(8):
                    dma("sync", ab("lch", 32768)[:, tt8 * 4096:(tt8 + 1) * 4096],
                        lcT4[half * 8 + tt8].rearrange("p c q -> p (c q)"), s="ld_l")
                for ob in range(8):
                    gob = half * 8 + ob
                    wslot = gob % 2
                    wsn = ("wTsb", "wTsb1")[wslot]
                    if gob == 0:
                        for piece in range(2):
                            stn = ("stD0", "stD1")[piece]
                            dma("sync", af(stn, 8192),
                                wT_t[piece].rearrange("p a n -> p (a n)"), s=("ld_a", "ld_k")[piece])
                            wait("gpsimd", ("ld_a", "ld_k")[piece])
                            src = af(stn, 8192).rearrange("p (a n) -> p a n", n=256)
                            dst = ab(wsn, 16384).rearrange("p (a n) -> p a n", n=512)[:, :, piece * 256:(piece + 1) * 256]
                            op("gpsimd", lambda e, src=src, dst=dst: e.tensor_copy(out=dst, in_=src), "gp")
                        fillgp[0] = semc["gp"]
                    ngob = gob + 1
                    if ngob < 16:
                        nob = ngob % 8
                        nslot = ngob % 2
                        nwsn = ("wTsb", "wTsb1")[nslot]
                        for piece in range(2):
                            stn = ("stD0", "stD1")[piece]
                            wait("sync", "gp", semc["gp"])
                            dma("sync", af(stn, 8192),
                                wT_t[nob * 2 + piece].rearrange("p a n -> p (a n)"), s=("ld_a", "ld_k")[piece])
                            wait("gpsimd", ("ld_a", "ld_k")[piece])
                            wait("gpsimd", "pe", dguard[nslot])
                            src = af(stn, 8192).rearrange("p (a n) -> p a n", n=256)
                            dst = ab(nwsn, 16384).rearrange("p (a n) -> p a n", n=512)[:, :, piece * 256:(piece + 1) * 256]
                            op("gpsimd", lambda e, src=src, dst=dst: e.tensor_copy(out=dst, in_=src), "gp")
                        fillgp[nslot] = semc["gp"]
                    wT3 = ab(wsn, 16384).rearrange("p (a n) -> p a n", n=512)
                    wait("tensor", "gp", fillgp[wslot])
                    wait("tensor", "ld_l")
                    wait("tensor", "ld_m")
                    for tt8 in range(8):
                        wait("tensor", "ve", semc["ve"])
                        pk = arena[:, (tt8 % 2) * 512:(tt8 % 2) * 512 + 512]
                        for ch in range(NCH):
                            tpk = mm(pk, ab("lch", 32768)[:, (tt8 * 32 + ch) * 128:(tt8 * 32 + ch + 1) * 128],
                                     wT3[:, ch, :], ch == 0, ch == NCH - 1)
                        wait("vector", "pe", tpk)
                        wait("vector", "st_d")
                        tv = op("vector", lambda e, pk=pk, ob=ob: e.tensor_add(
                            out=ab("dstg", 512), in0=pk,
                            in1=ab("biasb2", 4096)[:, ob * 512:(ob + 1) * 512]), "ve", serial=False)
                        wait("sync", "ve", tv)
                        r0 = (half * 8 + tt8) * 128
                        dma("sync", denseo[r0:r0 + 128, ob * 512:(ob + 1) * 512],
                            ab("dstg", 512), s="st_d")
                    dguard[wslot] = semc["pe"]
            barrier()

        wait_all_dma("sync")

        for e in ENGS:
            getattr(blk, e)(lambda eng, qq=q[e]: [f(eng) for f in qq])

    return nc


def build_launch2():
    nc = bass.Bass("TRN2", target_bir_lowering=False, debug=False)
    x1 = nc.dram_tensor("x1", [TL, H], F32, kind="ExternalInput").ap()
    dense = nc.dram_tensor("dense", [TL, H], BF16, kind="ExternalInput").ap()
    qsel = nc.dram_tensor("qsel", [128, 16], F32, kind="ExternalInput").ap()
    bits = nc.dram_tensor("bits", [64, 8], F32, kind="ExternalInput").ap()
    AT = nc.dram_tensor("AT", [64, H], BF16, kind="ExternalInput").ap()
    ones64 = nc.dram_tensor("ones64", [64, 128], BF16, kind="ExternalInput").ap()
    out = nc.dram_tensor("out", [TL, H], F32, kind="ExternalOutput").ap()

    q = {e: [] for e in ENGS}
    ctx = ExitStack()
    with ctx:
        semh, semc = {}, {}
        for n in ["ld_m", "ld_d0", "ld_d1", "ld_x0", "ld_x1", "st_o", "pe", "ve", "ac"]:
            semh[n] = ctx.enter_context(nc.semaphore(n + "2")); semc[n] = 0
        SBt = lambda n, sh, dt=F32: ctx.enter_context(nc.sbuf_tensor(n, sh, dt))
        bsb = SBt("bsb", [64, 8]); sel = SBt("sel", [64, 2])
        ATs = SBt("ATs", [64, H], BF16); ones_b = SBt("ones_b", [64, 128], BF16)
        rsel = SBt("rsel", [128, H])
        dsb_ = [SBt("dsb0", [128, H], BF16), SBt("dsb1", [128, H], BF16)]
        x1b_ = [SBt("x1b0", [128, H]), SBt("x1b1", [128, H])]
        vb = SBt("vb", [128, H])
        sq = SBt("sq", [128, H]); qsb = SBt("qsb", [128, 16]); st2 = SBt("st2", [128, 16])
        ob_out = SBt("ob_out", [128, H])
        epsb2 = SBt("epsb2", [128, 1])
        pr = ctx.enter_context(nc.psum_tensor("pr2", [128, 1024], F32))
        blk = ctx.enter_context(nc.Block())

        def dma(e, o, i, s="ld"):
            semc[s] += 16
            def f(eng, o=o, i=i, h=semh[s]):
                eng.dma_start(out=o, in_=i).then_inc(h, 16)
            q[e].append(f)
            return semc[s]

        def wait(e, s, v=None):
            if v is None:
                v = semc[s]
            if v <= 0:
                return
            def f(eng, h=semh[s], v=v):
                eng.wait_ge(h, v)
            q[e].append(f)

        OWN2 = {"vector": "ve", "scalar": "ac"}

        def op(e, fn, s=None):
            if e in OWN2:
                own = OWN2[e]
                wait(e, own, semc[own])
                if s is None:
                    s = own
            if s is None:
                def f0(eng, fn=fn):
                    fn(eng)
                q[e].append(f0)
                return None
            semc[s] += 1
            def f(eng, fn=fn, h=semh[s]):
                fn(eng).then_inc(h, 1)
            q[e].append(f)
            return semc[s]

        op("vector", lambda e: e.memset(epsb2[:], EPS))
        dma("sync", bsb[:], bits, s="ld_m")
        dma("sync", ATs[:], AT, s="ld_m")
        dma("sync", ones_b[:], ones64, s="ld_m")
        dma("sync", qsb[:], qsel, s="ld_m")
        wait("vector", "ld_m")
        op("vector", lambda e: e.reduce_max(out=sel[:, 0:1], in_=bsb[:], axis=AX.X))
        op("vector", lambda e: e.tensor_single_scalar(out=sel[:, 1:2], in_=sel[:, 0:1],
                                                      scalar=0.01, op=ALU.is_gt))
        tv = op("vector", lambda e: e.tensor_scalar_mul(out=ATs[:], in0=ATs[:], scalar1=sel[:, 1:2]), "ve")
        wait("tensor", "ve", tv)
        copy_t = [0, 0]
        for ob in range(8):
            wait("tensor", "ve", copy_t[ob % 2])
            tp = op("tensor", lambda e, ob=ob: e.matmul(
                out=pr[:, (ob % 2) * 512:(ob % 2) * 512 + 512], lhsT=ones_b[:],
                rhs=ATs[:, ob * 512:(ob + 1) * 512], start=True, stop=True), "pe")
            wait("vector", "pe", tp)
            copy_t[ob % 2] = op("vector", lambda e, ob=ob: e.tensor_copy(
                out=rsel[:, ob * 512:(ob + 1) * 512],
                in_=pr[:, (ob % 2) * 512:(ob % 2) * 512 + 512]), "ve")

        lguard = [(0, 0), (0, 0)]
        dma("sync", dsb_[0][:], dense[0:128, :], s="ld_d0")
        dma("sync", x1b_[0][:], x1[0:128, :], s="ld_x0")
        for tt in range(16):
            bix = tt % 2
            dsb = dsb_[bix]; x1b = x1b_[bix]
            if tt < 15:
                nbix = (tt + 1) % 2
                wait("sync", "ve", lguard[nbix][0])
                wait("sync", "ac", lguard[nbix][1])
                dma("sync", dsb_[nbix][:], dense[(tt + 1) * 128:(tt + 2) * 128, :], s="ld_d%d" % nbix)
                dma("sync", x1b_[nbix][:], x1[(tt + 1) * 128:(tt + 2) * 128, :], s="ld_x%d" % nbix)
            wait("vector", "ld_d%d" % bix)
            op("vector", lambda e, tt=tt, dsb=dsb: e.tensor_scalar_mul(out=vb[:], in0=dsb[:],
                                                          scalar1=qsb[:, tt:tt + 1]))
            op("vector", lambda e: e.tensor_mul(out=vb[:], in0=vb[:], in1=rsel[:]))
            op("vector", lambda e: e.reduce_sum(out=st2[:, 0:1], in_=vb[:], axis=AX.X))
            op("vector", lambda e: e.scalar_tensor_tensor(out=sq[:], in0=vb[:], scalar=1.0,
                                                          in1=vb[:], op0=ALU.mult, op1=ALU.mult,
                                                          accum_out=st2[:, 1:2]))
            op("vector", lambda e: e.tensor_scalar_mul(out=st2[:, 2:3], in0=st2[:, 0:1],
                                                       scalar1=1.0 / H))
            op("vector", lambda e: e.tensor_copy(out=st2[:, 9:10], in_=st2[:, 2:3]))
            op("vector", lambda e: e.tensor_scalar_mul(out=st2[:, 3:4], in0=st2[:, 2:3],
                                                       scalar1=st2[:, 9:10]))
            op("vector", lambda e: e.tensor_scalar_mul(out=st2[:, 8:9], in0=st2[:, 1:2],
                                                       scalar1=1.0 / H))
            tv = op("vector", lambda e: e.tensor_single_scalar(
                out=st2[:, 4:5], in_=st2[:, 8:9], scalar=st2[:, 3:4], op=ALU.subtract), "ve")
            wait("scalar", "ve", tv)
            ts = op("scalar", lambda e: e.activation(out=st2[:, 5:6], in_=st2[:, 4:5],
                                                     func=AF.Sqrt, bias=epsb2[:]), "ac")
            wait("vector", "ac", ts)
            op("vector", lambda e: e.reciprocal(out=st2[:, 6:7], in_=st2[:, 5:6]))
            op("vector", lambda e: e.tensor_scalar_mul(out=st2[:, 10:11], in0=st2[:, 2:3],
                                                       scalar1=-1.0))
            tv = op("vector", lambda e: e.tensor_scalar_mul(out=st2[:, 7:8], in0=st2[:, 10:11],
                                                            scalar1=st2[:, 6:7]), "ve")
            wait("scalar", "ve", tv)
            ts = op("scalar", lambda e: e.activation(out=sq[:], in_=vb[:], func=AF.Identity,
                                                     scale=st2[:, 6:7], bias=st2[:, 7:8]), "ac")
            wait("vector", "ac", ts)
            wait("vector", "ld_x%d" % bix)
            wait("vector", "st_o")
            tv = op("vector", lambda e, x1b=x1b: e.tensor_add(out=ob_out[:], in0=x1b[:], in1=sq[:]), "ve")
            lguard[bix] = (semc["ve"], semc["ac"])
            wait("sync", "ve", tv)
            dma("sync", out[tt * 128:(tt + 1) * 128, :], ob_out[:], s="st_o")
        wait("sync", "st_o")

        for e in ENGS:
            getattr(blk, e)(lambda eng, qq=q[e]: [f(eng) for f in qq])
    return nc


def prep_shared(inputs):
    f4 = np.float32
    wq = np.ascontiguousarray(inputs["L0_attn_w_q"], f4)
    wk = np.ascontiguousarray(inputs["L0_attn_w_k"], f4)
    wv = np.ascontiguousarray(inputs["L0_attn_w_v"], f4)
    wo = np.ascontiguousarray(inputs["L0_attn_w_o"], f4)
    W = np.ascontiguousarray(inputs["L0_ffn_weight"], f4)
    bias = np.asarray(inputs["L0_ffn_bias"], f4)
    cent = np.ascontiguousarray(inputs["L0_ffn_centroids"], f4)
    assign = np.asarray(inputs["L0_ffn_assignments"])

    d = {}
    d["wq_t"] = np.ascontiguousarray(wq.reshape(32, 128, 32, 128).transpose(2, 1, 0, 3))
    wk_t = wk.reshape(32, 128, 4, 256).transpose(2, 1, 0, 3)
    wv_t = wv.reshape(32, 128, 4, 256).transpose(2, 1, 0, 3)
    d["wkv_t"] = np.ascontiguousarray(np.concatenate([wk_t, wv_t], axis=0))
    d["wo_t"] = np.ascontiguousarray(wo.reshape(32, 128, 8, 512).transpose(2, 1, 0, 3))
    wT = np.ascontiguousarray(W.T)
    d["wT_t"] = np.ascontiguousarray(wT.reshape(32, 128, 16, 256).transpose(2, 1, 0, 3))
    d["bias_i"] = np.ascontiguousarray(np.broadcast_to(bias[None, :], (128, H)))
    d["centT"] = np.ascontiguousarray(cent.T)
    d["identb"] = np.eye(128, dtype=ml_dtypes.bfloat16)
    d["identf"] = np.eye(128, dtype=np.float32)
    jj = np.arange(128)[None, :]
    pp = np.arange(128)[:, None]
    d["tri1"] = np.where(jj <= pp, 0.0, -1e9).astype(ml_dtypes.bfloat16)
    tri2_p0 = np.where(jj <= pp - 1, 0.0, -1e9).astype(ml_dtypes.bfloat16)
    d["tri2_p0"] = tri2_p0
    d["tri2_p1"] = d["tri1"]
    AT = np.zeros((64, H), np.float32)
    AT[np.asarray(assign, np.int64), np.arange(H)] = 1.0
    d["AT"] = AT.astype(ml_dtypes.bfloat16)
    d["ones64"] = np.ones((64, 128), dtype=ml_dtypes.bfloat16)
    return d


def core_inputs1(inputs, shared, c):
    b, p = c // 2, c % 2
    x = np.asarray(inputs["L0_attn_hidden_states"], np.float32)
    x0p = np.ascontiguousarray(np.concatenate([x[b, p::2], x[b, 1 - p::2]], axis=0))
    return {
        "x0": x0p, "wq_t": shared["wq_t"], "wkv_t": shared["wkv_t"],
        "wo_t": shared["wo_t"], "wT_t": shared["wT_t"], "bias_i": shared["bias_i"],
        "centT": shared["centT"], "tri1": shared["tri1"],
        "tri2": shared["tri2_p0"] if p == 0 else shared["tri2_p1"],
        "identb": shared["identb"], "identf": shared["identf"],
    }


def kernel(**inputs):
    shared = prep_shared(inputs)
    nc1 = build_launch1()
    in_maps1 = [core_inputs1(inputs, shared, c) for c in range(8)]
    res1 = run_bass_kernel_spmd(nc1, in_maps1, core_ids=list(range(8)))
    bits = np.stack([np.asarray(res1.results[c]["cmax"][:, 0], np.float32)
                     for c in range(8)], axis=1)
    bits = np.ascontiguousarray(bits)
    nc2 = build_launch2()
    in_maps2 = []
    for c in range(8):
        r = res1.results[c]
        in_maps2.append({
            "x1": np.asarray(r["x1"], np.float32),
            "dense": np.asarray(r["dense"]).view(ml_dtypes.bfloat16)
                     if r["dense"].dtype != ml_dtypes.bfloat16 else r["dense"],
            "qsel": np.asarray(r["qsel"], np.float32),
            "bits": bits, "AT": shared["AT"], "ones64": shared["ones64"],
        })
    res2 = run_bass_kernel_spmd(nc2, in_maps2, core_ids=list(range(8)))
    out = np.empty((B_, S_, H), np.float32)
    for c in range(8):
        b, p = c // 2, c % 2
        x2 = np.asarray(res2.results[c]["out"], np.float32)
        out[b, p::2, :] = x2
    return out
